# version 09
# baseline (speedup 1.0000x reference)
import contextlib
import numpy as np
import ml_dtypes

import concourse.bacc as bacc
import concourse.bass as bass
import concourse.mybir as mybir
from concourse.bass_utils import run_bass_kernel_spmd
from concourse.bass_isa import ReduceOp

F32 = mybir.dt.float32
BF16 = mybir.dt.bfloat16
U8 = mybir.dt.uint8
AF = mybir.ActivationFunctionType
ALU = mybir.AluOpType
AX = mybir.AxisListType

B, S, E, H = 16, 2048, 2048, 16
NOPE, ROPE_D, VD, QH, LORA = 128, 64, 128, 192, 512
DFF = 8192
NSEQ = 2
QCW = 512
NQC = S // QCW
NCH = E // 128
NKB = S // 128
NOC = DFF // 128
SCALE = float(QH) ** -0.5
EXPB = -8.0
EPS_LN = 1e-5
EPS_ATT = 1e-6
EPS_FFN = 1e-6
CLIP_HI = 128.0 - 1e-6

ENGS = ("sync", "gpsimd", "scalar", "tensor", "vector")


class Plan:
    def __init__(self, nc):
        self.nc = nc
        self.prog = {e: [] for e in ENGS}
        self.sems = {}

    def sem(self, name):
        h = self.nc.alloc_semaphore(name=name)
        self.sems[name] = [h, 0]

    def emit(self, eng, fn, inc=None, amt=1):
        if inc is not None:
            rec = self.sems[inc]
            rec[1] += amt
            self.prog[eng].append((fn, rec[0], amt))
        else:
            self.prog[eng].append((fn, None, 0))

    def wait(self, eng, sem, val=None):
        rec = self.sems[sem]
        h = rec[0]
        v = rec[1] if val is None else val
        if v <= 0:
            return
        self.prog[eng].append((lambda e, h=h, v=v: e.wait_ge(h, v), None, 0))

    def val(self, sem):
        return self.sems[sem][1]

    def barrier(self):
        bar = self.sems["bar"]
        h = bar[0]
        for e in ENGS:
            self.prog[e].append((lambda eng, h=h: eng.sem_inc(h, 1), None, 0))
        bar[1] += len(ENGS)
        v = bar[1]
        for e in ENGS:
            self.prog[e].append((lambda eng, h=h, v=v: eng.wait_ge(h, v), None, 0))

    def replay(self, block):
        def runner(items):
            def run(e):
                for fn, semh, amt in items:
                    ins = fn(e)
                    if semh is not None:
                        ins.then_inc(semh, amt)
            return run

        block.sync(runner(self.prog["sync"]))
        block.gpsimd(runner(self.prog["gpsimd"]))
        block.scalar(runner(self.prog["scalar"]))
        block.tensor(runner(self.prog["tensor"]))
        block.vector(runner(self.prog["vector"]))


def mcol(v, c, s_):
    return v * 32 + c * 2 + s_


def build_nc(phases=("M", "A", "B"), dump=(), nogamma=False):
    nc = bacc.Bacc("TRN2", target_bir_lowering=False)
    dump = set(dump)

    def din(name, shape, dt=F32):
        return nc.dram_tensor(name, list(shape), dt, kind="ExternalInput").ap()

    x_in = din("x", (NSEQ, S, E))
    c_in = din("c", (NSEQ, E))
    wq_in = din("wq", (E, H * QH))
    wkva_in = din("wkva", (E, LORA + ROPE_D))
    wkvb_in = din("wkvb", (LORA, H * (NOPE + VD)))
    wo_in = din("wo", (H * VD, E))
    kvs_in = din("kvs", (1, LORA))
    wfc_in = din("wfc", (DFF, E))
    wpj_in = din("wpj", (E, DFF))
    wmod_in = din("wmod", (E, 6 * E))
    cos_in = din("cosT", (128, S))
    sin_in = din("sinT", (128, S))

    out_hbm = nc.dram_tensor("out", [NSEQ, S, E], F32, kind="ExternalOutput").ap()

    def scr(name, shape, dt):
        return nc.dram_tensor(name, list(shape), dt, kind="Internal").ap()

    x1_scr = scr("x1s", (NSEQ, E, S), F32)
    wqbf_scr = scr("wqbf", (24 * NCH, 128 * 128), BF16)
    wobf_scr = scr("wobf", (NCH * NCH, 128 * 128), BF16)
    wkvbf_scr = scr("wkvbf", (LORA, H * (NOPE + VD)), BF16)
    wkvabf_scr = scr("wkvabf", (E, LORA + 256), BF16)
    tfc_scr = scr("tfc", (NCH * NOC, 128 * 128), BF16)
    tpj_scr = scr("tpj", (NOC * NCH, 128 * 128), BF16)

    dumps = {}

    def mkdump(name, shape, dt):
        dumps[name] = nc.dram_tensor(name, list(shape), dt, kind="ExternalOutput").ap()

    if "mod" in dump:
        mkdump("d_mod", (128, 224), F32)
    if "gam" in dump:
        mkdump("d_gam", (128, 8), F32)
    if "h" in dump:
        mkdump("d_h", (128, NCH * QCW), BF16)
    if "ckv" in dump:
        mkdump("d_ckv", (128, 4 * QCW), BF16)
    if "kpe" in dump:
        mkdump("d_kpe", (128, 2 * S), BF16)
    if "kn" in dump:
        mkdump("d_kn", (128, H * S), BF16)
    if "vt" in dump:
        mkdump("d_vt", (128, NKB * 2048), BF16)
    if "qpe" in dump:
        mkdump("d_qpe", (128, 8 * QCW), BF16)
    if "ob" in dump:
        mkdump("d_ob", (128, 4 * QCW), BF16)
    if "xh" in dump:
        mkdump("d_xh", (128, NCH * QCW), BF16)
    if "g" in dump:
        mkdump("d_g", (128, NOC * QCW), BF16)
    if "wscr" in dump:
        mkdump("d_wscr", (LORA, H * (NOPE + VD)), BF16)
    if "wqscr" in dump:
        mkdump("d_wqscr", (24 * NCH, 128 * 128), BF16)
    if "x1scr" in dump:
        mkdump("d_x1scr", (E, S), F32)

    ctx = contextlib.ExitStack()

    def sbt(name, shape, dt=F32):
        return ctx.enter_context(nc.sbuf_tensor(name, list(shape), dt))

    modc = sbt("modc", (128, 224))
    smal = sbt("smal", (128, 8))
    kvsc = sbt("kvsc", (128, 4))
    gam = sbt("gam", (128, 8))
    ones32 = sbt("ones32", (128, 32), BF16)
    id128 = sbt("id128", (128, 128))
    id128b = sbt("id128b", (128, 128), BF16)
    POOL = [sbt(f"pool{i}", (128, 512)) for i in range(5)]
    SCB = sbt("scb", (128, 2048), BF16)
    STRM = sbt("strm", (128, 2048), BF16)
    STRF = [sbt(f"strf{i}", (128, 2048)) for i in range(2)]

    ar1 = sbt("ar1", (128, 65536), U8)
    ar2 = sbt("ar2", (128, 4096), U8)
    ar3 = sbt("ar3", (128, 8192), U8)
    a1 = nc.lookup_mloc(ar1).addr
    a2 = nc.lookup_mloc(ar2).addr
    a3 = nc.lookup_mloc(ar3).addr

    def at(name, shape, dt, base, off):
        return nc.alloc_sbuf_tensor_at(name, list(shape), dt, offset=base + off)

    Kn = at("Kn", (128, H * S), BF16, a1, 0)
    x1t = at("x1t", (128, NCH * 512), F32, a1, 0)
    h2t = at("h2t", (128, NCH * 512), F32, a1, 32768)
    cosq = at("cosq", (128, 2 * QCW), F32, a2, 0)
    T4 = [at("t4a", (128, 2048), BF16, a2, 0)]
    qpe = at("qpe", (128, 8 * QCW), BF16, a3, 0)
    otm = at("otm", (128, 2048), F32, a3, 0)

    Vt = sbt("Vt", (128, NKB * 2048), BF16)
    kpe1 = sbt("kpe1", (128, S), BF16)
    kpe2 = sbt("kpe2", (128, S), BF16)
    T4.append(at("t4b", (128, 2048), BF16, nc.lookup_mloc(kpe1).addr, 0))
    T4.append(at("t4c", (128, 2048), BF16, nc.lookup_mloc(kpe2).addr, 0))
    TERNB = at("ternb", (128, 6 * 2048), BF16, nc.lookup_mloc(Vt).addr, 0)
    TERNF = [at(f"ternf{i}", (128, 2048), F32, a1, i * 8192) for i in range(4)]
    hq = sbt("hq", (128, NCH * QCW), BF16)
    qn = sbt("qn", (128, QCW), BF16)
    Pb = sbt("Pb", (128, 2 * QCW), BF16)
    Ob = sbt("Ob", (128, 4 * QCW), BF16)
    gbuf = Vt
    xh = hq
    ckvn = SCB

    ctxM = contextlib.ExitStack()
    PSM = [ctxM.enter_context(nc.psum_tensor(f"psm{i}", [128, 512], F32))
           for i in range(3)]
    PSB = [ctxM.enter_context(nc.psum_tensor(f"psb{i}", [128, 512], BF16))
           for i in range(2)]

    p = Plan(nc)
    p.t4_tok = [0, 0, 0]
    p.gstat_tok = 0
    for s_ in ("bar", "dS", "dG", "dC", "pe", "ve", "se", "ge"):
        p.sem(s_)

    def MM(ps_ap, lhsT, rhs, first, last=True, inc=None, tp=None):
        p.emit("tensor",
               lambda e, a=ps_ap, l=lhsT, r=rhs, f=first, q=last, t=tp:
               e.matmul(a, l, r, start=f, stop=q, tile_position=t), inc=inc)

    def TR(ps_ap, in_ap, inc=None):
        p.emit("tensor",
               lambda e, a=ps_ap, i=in_ap: e.transpose(a, i, id128[:, :]), inc=inc)

    def DMA(eng, dst, src, sem="dS", slow=False):
        p.emit(eng, lambda e, d=dst, s=src, sl=slow: e.dma_start(
            out=d, in_=s, allow_slow_non_contiguous=sl), inc=sem, amt=16)

    def DMAT(eng, dst, src, sem="dS"):
        p.emit(eng, lambda e, d=dst, s=src: e.dma_start_transpose(out=d, in_=s),
               inc=sem, amt=16)

    def DMAACC(dst, src):
        p.emit("gpsimd", lambda e, d=dst, s=src:
               e.dma_start(out=d, in_=s, accum_op=ALU.add), inc="dG", amt=16)

    def COPY(eng, dst, src, inc=None):
        p.emit(eng, lambda e, d=dst, s=src: e.tensor_copy(d, s), inc=inc)

    def TT(eng, dst, a, b, op, inc=None):
        p.emit(eng, lambda e, d=dst, x=a, y=b, o=op: e.tensor_tensor(d, x, y, o),
               inc=inc)

    def TS(eng, dst, a, s1, s2, op0, op1=None, inc=None):
        if op1 is None:
            p.emit(eng, lambda e, d=dst, x=a, u=s1, o0=op0:
                   e.tensor_scalar(d, x, u, None, o0), inc=inc)
        else:
            p.emit(eng, lambda e, d=dst, x=a, u=s1, v=s2, o0=op0, o1=op1:
                   e.tensor_scalar(d, x, u, v, o0, o1), inc=inc)

    def STT(eng, dst, in0, scalar, in1, op0, op1, inc=None):
        p.emit(eng, lambda e, d=dst, a=in0, s=scalar, b=in1, o0=op0, o1=op1:
               e.scalar_tensor_tensor(d, a, s, b, o0, o1), inc=inc)

    def ACT(dst, src, func, bias=None, scale=None, accum=None, inc=None):
        kw = {}
        if bias is not None:
            kw["bias"] = bias
        if scale is not None:
            kw["scale"] = scale
        if accum is not None:
            kw["accum_out"] = accum
        p.emit("scalar", lambda e, d=dst, s=src, f=func, k=kw:
               e.activation(d, s, f, **k), inc=inc)

    def SCOPY2(dst, src, inc=None):
        p.emit("scalar", lambda e, d=dst, s=src: e.copy(d, s), inc=inc)

    def SIGN(dst, src, bias, inc=None):
        p.emit("scalar", lambda e, d=dst, s=src, b=bias: e.sign(d, s, bias=b), inc=inc)

    p.emit("gpsimd", lambda e: e.memset(ones32[:, :], 1.0))
    p.emit("gpsimd", lambda e: e.memset(id128[:, :], 0.0))
    p.emit("gpsimd", lambda e: e.affine_select(
        out=id128[:, :], in_=id128[:, :], pattern=[[-1, 128]],
        compare_op=ALU.not_equal, fill=1.0, base=0, channel_multiplier=1))
    p.emit("gpsimd", lambda e: e.memset(id128b[:, :], 0.0))
    p.emit("gpsimd", lambda e: e.affine_select(
        out=id128b[:, :], in_=id128b[:, :], pattern=[[-1, 128]],
        compare_op=ALU.not_equal, fill=1.0, base=0, channel_multiplier=1))
    p.emit("gpsimd", lambda e: e.memset(smal[:, 0:1], EXPB))
    p.emit("gpsimd", lambda e: e.memset(smal[:, 1:2], EPS_LN))
    p.emit("gpsimd", lambda e: e.memset(smal[:, 2:3], EPS_ATT))
    p.emit("gpsimd", lambda e: e.memset(smal[:, 3:4], EPS_FFN), inc="ge")

    DMA("sync", kvsc[:, 0:4], bass.AP(kvs_in.tensor, 0, [[1, 128], [128, 4]]),
        slow=True)
    for s_ in range(NSEQ):
        DMA("sync", bass.AP(POOL[4], s_, [[512, 128], [2, 16]]),
            bass.AP(c_in.tensor, s_ * E, [[1, 128], [128, 16]]), slow=True)

    def phase_M():
        p.wait("scalar", "dS")
        ACT(POOL[4][:, 32:64], POOL[4][:, 0:32], AF.Silu, inc="se")
        scv = POOL[4]

        strf_tok = [0, 0]
        first = True
        p.wait("tensor", "se")
        for ec in range(NCH):
            for cb in range(8):
                slot = (ec * 8 + cb) % 2
                p.wait("sync", "pe", strf_tok[slot])
                DMA("sync", STRF[slot][:, 0:1536],
                    wmod_in[ec * 128:(ec + 1) * 128, cb * 1536:(cb + 1) * 1536])
                p.wait("tensor", "dS")
                for t in range(12):
                    cc = cb * 12 + t
                    MM(PSM[0][:, 2 * cc:2 * cc + 2],
                       STRF[slot][:, t * 128:(t + 1) * 128],
                       scv[:, 32 + 2 * ec:34 + 2 * ec],
                       first=first, last=(ec == NCH - 1),
                       inc=("pe" if t == 11 else None))
                    first = False
                strf_tok[slot] = p.val("pe")
        p.wait("vector", "pe")
        COPY("vector", modc[:, 0:192], PSM[0][:, 0:192], inc="ve")

        for v in (2, 5):
            cols = modc[:, v * 32:(v + 1) * 32]
            TS("vector", POOL[0][:, 0:32], cols, 0.25, None, ALU.max)
            TS("vector", POOL[1][:, 0:32], cols, -0.25, None, ALU.min)
            TS("vector", POOL[2][:, 0:32], cols, 0.0, None, ALU.is_ge)
            TT("vector", POOL[3][:, 0:32], POOL[0][:, 0:32], POOL[1][:, 0:32],
               ALU.subtract)
            TT("vector", POOL[3][:, 0:32], POOL[2][:, 0:32], POOL[3][:, 0:32], ALU.mult)
            TT("vector", cols, POOL[1][:, 0:32], POOL[3][:, 0:32], ALU.add)
        for v in (1, 4):
            cols = modc[:, v * 32:(v + 1) * 32]
            TS("vector", cols, cols, 1.0, None, ALU.add, inc="ve")

        def gamma_pass(w_in, nslabs, slab_cols, ps_bank, gcols, eps):
            tok = [0, 0]
            firstg = True
            mm_tok = 0
            p.wait("scalar", "pe")
            p.wait("scalar", "ve")
            for i in range(nslabs):
                slot = i % 2
                p.wait("sync", "se", tok[slot])
                if i < 2:
                    p.wait("sync", "pe")
                r0 = (i * 2048) // slab_cols
                c0 = (i * 2048) % slab_cols
                DMA("sync", STRF[slot][:, 0:2048],
                    w_in[r0 * 128:(r0 + 1) * 128, c0:c0 + 2048])
                p.wait("scalar", "dS")
                p.wait("scalar", "pe", mm_tok)
                ACT(SCB[:, 0:2048], STRF[slot][:, 0:2048], AF.Abs, inc="se")
                tok[slot] = p.val("se")
                p.wait("tensor", "se")
                for t in range(4):
                    MM(ps_bank[0:32, :], ones32[:, :], SCB[:, t * 512:(t + 1) * 512],
                       first=firstg, last=(i == nslabs - 1 and t == 3),
                       inc=("pe" if t == 3 else None))
                    firstg = False
                mm_tok = p.val("pe")
            p.wait("vector", "pe")
            p.emit("vector", lambda e, pb=ps_bank: e.reduce_sum(
                POOL[0][0:1, 0:1], pb[0:1, :], AX.X), inc="ve")
            n_el = nslabs * 2048 * 128
            p.wait("gpsimd", "ve")
            TS("gpsimd", POOL[0][0:1, 1:2], POOL[0][0:1, 0:1], 1.0 / n_el, None, ALU.mult)
            TS("gpsimd", POOL[0][0:1, 2:3], POOL[0][0:1, 1:2], eps, 0.5, ALU.add, ALU.mult)
            TS("gpsimd", POOL[0][0:1, 3:4], POOL[0][0:1, 2:3], -1.0, None, ALU.mult)
            TS("gpsimd", POOL[0][0:1, 4:5], POOL[0][0:1, 1:2], 0.5, None, ALU.mult)
            p.emit("gpsimd", lambda e, g=gcols[0]: e.partition_broadcast(
                gam[:, g:g + 1], POOL[0][0:1, 3:4]))
            p.emit("gpsimd", lambda e, g=gcols[1]: e.partition_broadcast(
                gam[:, g:g + 1], POOL[0][0:1, 2:3]))
            p.emit("gpsimd", lambda e, g=gcols[2]: e.partition_broadcast(
                gam[:, g:g + 1], POOL[0][0:1, 4:5]), inc="ge")
            p.wait("vector", "ge")

        if not nogamma:
            gamma_pass(wfc_in, 64, E, PSM[1], (0, 1, 2), EPS_FFN)
            gamma_pass(wpj_in, 64, DFF, PSM[2], (3, 4, 5), EPS_FFN)

        p.wait("vector", "ge")
        p.emit("vector", lambda e: e.tensor_scalar(
            modc[:, 192:224], modc[:, 160:192], gam[:, 5:6], None, ALU.mult), inc="ve")

        p.wait("vector", "pe")
        p.wait("vector", "ge")
        p.wait("sync", "ge")
        STG = [STRM, SCB]
        strf_tok = [0, 0]
        strf_dma = [0, 0]
        stg_out = [0, 0]

        def cast_unit(i_u, in_src, n_in, copies_fn, outs_fn):
            slot = i_u % 2
            p.wait("sync", "ve", strf_tok[slot])
            DMA("sync", STRF[slot][:, 0:n_in], in_src)
            strf_dma[slot] = p.val("dS")
            p.wait("vector", "dS", max(strf_dma[slot], stg_out[slot]))
            copies_fn(STRF[slot], STG[slot])
            strf_tok[slot] = p.val("ve")
            p.wait("sync", "ve", strf_tok[slot])
            outs_fn(STG[slot])
            stg_out[slot] = p.val("dS")

        i_u = 0
        for c in range(NCH):
            for half in range(2):
                def cps(sf, sg):
                    COPY("vector", sg[:, 0:1024].rearrange("p (h d) -> p h d", d=128),
                         bass.AP(sf, 0, [[2048, 128], [192, 8], [1, 128]]))
                    COPY("vector", sg[:, 1024:1280].rearrange("p (h f) -> p h f", f=32),
                         bass.AP(sf, 128, [[2048, 128], [192, 8], [1, 32]]))
                    COPY("vector", sg[:, 1280:1536].rearrange("p (h f) -> p h f", f=32),
                         bass.AP(sf, 160, [[2048, 128], [192, 8], [1, 32]]), inc="ve")

                def outs(sg, c=c, half=half):
                    h0 = half * 8
                    DMA("sync",
                        bass.AP(wqbf_scr.tensor, (h0 * NCH + c) * 16384,
                                [[128, 128], [NCH * 16384, 8], [1, 128]]),
                        sg[:, 0:1024].rearrange("p (h d) -> p h d", d=128))
                    DMA("sync",
                        bass.AP(wqbf_scr.tensor, ((16 + 2 * half) * NCH + c) * 16384,
                                [[128, 128], [NCH * 16384, 2], [1, 128]]),
                        sg[:, 1024:1280].rearrange("p (j d) -> p j d", d=128))
                    DMA("sync",
                        bass.AP(wqbf_scr.tensor, ((20 + 2 * half) * NCH + c) * 16384,
                                [[128, 128], [NCH * 16384, 2], [1, 128]]),
                        sg[:, 1280:1536].rearrange("p (j d) -> p j d", d=128))
                cast_unit(i_u, wq_in[c * 128:(c + 1) * 128,
                                    half * 1536:(half + 1) * 1536], 1536, cps, outs)
                i_u += 1

        for hv in range(NCH):
            def cps(sf, sg):
                COPY("vector", sg[:, 0:2048], sf[:, 0:2048], inc="ve")

            def outs(sg, hv=hv):
                DMA("sync",
                    bass.AP(wobf_scr.tensor, hv * 16384,
                            [[128, 128], [NCH * 16384, 16], [1, 128]]),
                    sg[:, 0:2048].rearrange("p (o d) -> p o d", d=128))
            cast_unit(i_u, wo_in[hv * 128:(hv + 1) * 128, :], 2048, cps, outs)
            i_u += 1

        for lc in range(4):
            for piece in range(2):
                def cps(sf, sg):
                    COPY("vector", sg[:, 0:1024].rearrange("p (h d) -> p h d", d=128),
                         bass.AP(sf, 0, [[2048, 128], [256, 8], [1, 128]]))
                    COPY("vector", sg[:, 1024:2048].rearrange("p (h d) -> p h d", d=128),
                         bass.AP(sf, 128, [[2048, 128], [256, 8], [1, 128]]), inc="ve")

                def outs(sg, lc=lc, piece=piece):
                    DMA("sync", wkvbf_scr[lc * 128:(lc + 1) * 128,
                                          piece * 1024:piece * 1024 + 1024],
                        sg[:, 0:1024])
                    DMA("sync", wkvbf_scr[lc * 128:(lc + 1) * 128,
                                          2048 + piece * 1024:2048 + piece * 1024 + 1024],
                        sg[:, 1024:2048])
                cast_unit(i_u, wkvb_in[lc * 128:(lc + 1) * 128,
                                       piece * 2048:(piece + 1) * 2048], 2048, cps, outs)
                i_u += 1

        for ec in range(NCH):
            def cps(sf, sg):
                COPY("vector", sg[:, 0:512], sf[:, 0:512])
                for r in range(4):
                    COPY("vector", sg[:, 512 + r * 32:544 + r * 32], sf[:, 512:544])
                    COPY("vector", sg[:, 640 + r * 32:672 + r * 32], sf[:, 544:576],
                         inc=("ve" if r == 3 else None))

            def outs(sg, ec=ec):
                DMA("sync", wkvabf_scr[ec * 128:(ec + 1) * 128, :], sg[:, 0:768])
            cast_unit(i_u, wkva_in[ec * 128:(ec + 1) * 128, :], 576, cps, outs)
            i_u += 1


        def tern_pass(w_in, rows_chunks, col_pieces, gneg_col, gpos_col, unit_fn):
            slab_tok = [0, 0, 0, 0]
            slab_dma = [0, 0, 0, 0]
            t2_tok = [0, 0]
            pb_ve_tok = [0, 0]
            pb_dma_tok = [0, 0]
            units = [(rc, cp) for rc in range(rows_chunks) for cp in range(col_pieces)]
            n_u = len(units)
            for i_s in range(n_u):
                rc, cp = units[i_s]
                slot = i_s % 4
                p.wait("sync", "ve", slab_tok[slot])
                DMA("sync", TERNF[slot][:, 0:2048],
                    w_in[rc * 128:(rc + 1) * 128, cp * 2048:(cp + 1) * 2048])
                slab_dma[slot] = p.val("dS")
                t2s = i_s % 2
                p.wait("vector", "dS", slab_dma[slot])
                p.wait("vector", "pe", t2_tok[t2s])
                TS("vector", TERNB[:, 4096 + t2s * 2048:4096 + (t2s + 1) * 2048],
                   TERNF[slot][:, 0:2048], gam[:, gpos_col:gpos_col + 1], 2.0,
                   ALU.is_ge, ALU.mult)
                TS("vector", TERNB[:, 8192 + t2s * 2048:8192 + (t2s + 1) * 2048],
                   TERNF[slot][:, 0:2048], gam[:, gneg_col:gneg_col + 1], 2.0,
                   ALU.is_ge, ALU.mult, inc="ve")
                slab_tok[slot] = p.val("ve")
                STT("vector", TERNB[:, t2s * 2048:(t2s + 1) * 2048],
                    TERNB[:, 4096 + t2s * 2048:4096 + (t2s + 1) * 2048], -2.0,
                    TERNB[:, 8192 + t2s * 2048:8192 + (t2s + 1) * 2048],
                    ALU.add, ALU.add, inc="ve")
                p.wait("tensor", "ve")
                for g in range(4):
                    half = g % 2
                    p.wait("tensor", "se", pb_ve_tok[half])
                    for j in range(4):
                        p.emit("tensor", lambda e, h_=half, j_=j, g_=g, t_=t2s:
                               e.transpose(PSB[h_][:, j_ * 128:(j_ + 1) * 128],
                                           TERNB[:, t_ * 2048 + (g_ * 4 + j_) * 128:
                                                 t_ * 2048 + (g_ * 4 + j_ + 1) * 128],
                                           id128b[:, :]),
                               inc=("pe" if j == 3 else None))
                    p.wait("scalar", "pe")
                    p.wait("scalar", "dS", pb_dma_tok[half])
                    SCOPY2(Pb[:, half * 512:(half + 1) * 512], PSB[half][:, :],
                           inc="se")
                    pb_ve_tok[half] = p.val("se")
                    p.wait("sync", "se")
                    DMA("sync", unit_fn(rc, cp, g),
                        Pb[:, half * 512:(half + 1) * 512].rearrange(
                            "p (j q) -> p j q", q=128))
                    pb_dma_tok[half] = p.val("dS")
                t2_tok[t2s] = p.val("pe")

        def fc_unit(rc, cp, g):
            return bass.AP(tfc_scr.tensor,
                           ((rc // 4) * 4 + g) * 262144 + (rc % 4) * 16384,
                           [[128, 128], [65536, 4], [1, 128]])

        def pj_unit(rc, cp, g):
            return bass.AP(tpj_scr.tensor,
                           ((rc // 4) * 16 + cp * 4 + g) * 262144 + (rc % 4) * 16384,
                           [[128, 128], [65536, 4], [1, 128]])

        tern_pass(wfc_in, NOC, 1, 0, 1, fc_unit)
        tern_pass(wpj_in, NCH, 4, 3, 4, pj_unit)

        if "d_mod" in dumps:
            p.wait("sync", "ve")
            DMA("sync", dumps["d_mod"], modc[:, 0:224])
        if "d_gam" in dumps:
            p.wait("sync", "ge")
            DMA("sync", dumps["d_gam"], gam[:, 0:8])
        if "d_wscr" in dumps:
            p.wait("sync", "dS")
            DMA("sync", dumps["d_wscr"], wkvbf_scr[:, :])
        if "d_wqscr" in dumps:
            p.wait("sync", "dS")
            DMA("sync", dumps["d_wqscr"], wqbf_scr[:, :])

    def attn_qc(s, qc, do_dump):
        q0 = qc * QCW
        p.wait("sync", "ve")
        DMA("sync", cosq[:, 0:512], cos_in[:, q0:q0 + QCW])
        DMA("sync", cosq[:, 512:1024], sin_in[:, q0:q0 + QCW])

        stat_tok = [0, 0]
        stat_ve_tok = 0
        p.wait("scalar", "pe")
        for rt in range(4):
            slot = rt % 2
            if rt < 2:
                p.wait("sync", "se")
                p.wait("sync", "ve")
                p.wait("sync", "pe")
            else:
                p.wait("sync", "se", stat_tok[slot])
                p.wait("sync", "ve", stat_ve_tok)
            DMA("sync", STRF[slot][:, 0:2048],
                x_in[s, q0 + rt * 128:q0 + (rt + 1) * 128, :])
            p.wait("scalar", "dS")
            ACT(SCB[:, 0:2048], STRF[slot][:, 0:2048], AF.Square,
                accum=POOL[4][:, rt:rt + 1], inc="se")
            p.wait("vector", "dS")
            p.emit("vector", lambda e, sl=slot, r=rt: e.reduce_sum(
                POOL[4][:, 4 + r:5 + r], STRF[sl][:, 0:2048], AX.X), inc="ve")
            stat_tok[slot] = p.val("se")
            stat_ve_tok = p.val("ve")
        p.wait("tensor", "se")
        p.wait("tensor", "ve")
        for rt in range(4):
            TR(PS[4][0:1, rt * 128:(rt + 1) * 128], POOL[4][:, 4 + rt:5 + rt])
            TR(PS[5][0:1, rt * 128:(rt + 1) * 128], POOL[4][:, rt:rt + 1],
               inc=("pe" if rt == 3 else None))
        p.wait("vector", "pe")
        TS("vector", POOL[0][0:1, :], PS[4][0:1, :], 1.0 / E, None, ALU.mult)
        TT("vector", POOL[1][0:1, :], POOL[0][0:1, :], POOL[0][0:1, :], ALU.mult)
        STT("vector", POOL[1][0:1, :], PS[5][0:1, :], 1.0 / E, POOL[1][0:1, :],
            ALU.mult, ALU.subtract, inc="ve")
        p.wait("scalar", "ve")
        ACT(POOL[1][0:1, :], POOL[1][0:1, :], AF.Ln, bias=smal[0:1, 1:2])
        ACT(POOL[1][0:1, :], POOL[1][0:1, :], AF.Exp, scale=-0.5, inc="se")
        p.wait("gpsimd", "ve")
        p.wait("gpsimd", "se")
        p.emit("gpsimd", lambda e: e.partition_broadcast(POOL[2][:, :], POOL[0][0:1, :]))
        p.emit("gpsimd", lambda e: e.partition_broadcast(POOL[3][:, :], POOL[1][0:1, :]),
               inc="ge")

        ap_tok = [0, 0]
        xts_tok = [0, 0, 0]
        xts_dma_tok = [0, 0, 0]
        XTS3 = True
        p.wait("tensor", "ge")
        p.wait("vector", "ge")
        for rt in range(4):
            slot = rt % 2
            p.wait("sync", "pe", ap_tok[slot])
            DMA("sync", STRF[slot][:, 0:2048],
                x_in[s, q0 + rt * 128:q0 + (rt + 1) * 128, :])
            p.wait("tensor", "dS")
            for c in range(NCH):
                bank = PS[6 + (c % 2)]
                xsl = (rt * NCH + c) % 2
                p.wait("tensor", "ve", xts_tok[xsl])
                TR(bank[:, 0:128], STRF[slot][:, c * 128:(c + 1) * 128], inc="pe")
                p.wait("vector", "pe")
                p.wait("vector", "dS", xts_dma_tok[xsl])
                xts = POOL[4][:, xsl * 128:xsl * 128 + 128]
                COPY("vector", xts, bank[:, 0:128], inc="ve")
                p.wait("sync", "ve")
                DMA("sync",
                    bass.AP(x1_scr.tensor, s * E * S + (c * 128) * S + q0 + rt * 128,
                            [[S, 128], [1, 128]]),
                    xts)
                xts_dma_tok[xsl] = p.val("dS")
                TT("vector", POOL[4][:, 384:512], xts,
                   POOL[2][:, rt * 128:(rt + 1) * 128], ALU.subtract)
                TT("vector", POOL[4][:, 384:512], POOL[4][:, 384:512],
                   POOL[3][:, rt * 128:(rt + 1) * 128], ALU.mult)
                p.emit("vector", lambda e, c_=c, r_=rt: e.tensor_scalar(
                    hq[:, c_ * 512 + r_ * 128:c_ * 512 + (r_ + 1) * 128],
                    POOL[4][:, 384:512],
                    modc[:, mcol(1, c_, s):mcol(1, c_, s) + 1],
                    modc[:, mcol(0, c_, s):mcol(0, c_, s) + 1],
                    ALU.mult, ALU.add), inc="ve")
                xts_tok[xsl] = p.val("ve")
            ap_tok[slot] = p.val("pe")
        hq_ready = p.val("ve")
        xw_tok = p.val("dS")

        if do_dump and qc == 0 and "d_h" in dumps:
            p.wait("sync", "ve")
            DMA("sync", dumps["d_h"], hq[:, :])

        kva_tok = 0
        for ec in range(NCH):
            p.wait("sync", "pe", kva_tok)
            DMA("sync", STRM[:, 0:768], wkvabf_scr[ec * 128:(ec + 1) * 128, :])
            p.wait("tensor", "dS")
            if ec == 0:
                p.wait("tensor", "ve", hq_ready)
            rhs = hq[:, ec * 512:(ec + 1) * 512]
            for t in range(4):
                MM(PS[t][:, :], STRM[:, t * 128:(t + 1) * 128], rhs,
                   first=(ec == 0), last=(ec == NCH - 1))
            MM(PS[4][:, :], STRM[:, 512:640], rhs,
               first=(ec == 0), last=(ec == NCH - 1))
            MM(PS[5][:, :], STRM[:, 640:768], rhs,
               first=(ec == 0), last=(ec == NCH - 1), inc="pe")
            kva_tok = p.val("pe")

        p.wait("vector", "pe")
        TT("vector", POOL[0][:, :], PS[4][:, :], cosq[:, 0:512], ALU.mult)
        TT("vector", POOL[1][:, :], PS[5][:, :], cosq[:, 512:1024], ALU.mult)
        TT("vector", kpe1[:, q0:q0 + 512], POOL[0][:, :], POOL[1][:, :], ALU.subtract)
        TT("vector", POOL[0][:, :], PS[5][:, :], cosq[:, 0:512], ALU.mult)
        TT("vector", POOL[1][:, :], PS[4][:, :], cosq[:, 512:1024], ALU.mult)
        TT("vector", kpe2[:, q0:q0 + 512], POOL[0][:, :], POOL[1][:, :], ALU.add,
           inc="ve")

        p.wait("scalar", "ve")
        for i in range(4):
            ACT(SCB[:, i * 512:(i + 1) * 512], PS[i][:, :], AF.Square,
                inc=("se" if i == 3 else None))
        p.wait("tensor", "se")
        for i in range(4):
            MM(PS[6][0:32, :], ones32[:, :], SCB[:, i * 512:(i + 1) * 512],
               first=(i == 0), last=(i == 3), inc=("pe" if i == 3 else None))
        p.wait("vector", "pe")
        TS("vector", POOL[4][0:1, :], PS[6][0:1, :], 1.0 / LORA, None, ALU.mult,
           inc="ve")
        p.wait("scalar", "ve")
        ACT(POOL[4][0:1, :], POOL[4][0:1, :], AF.Ln, bias=smal[0:1, 2:3])
        ACT(POOL[4][0:1, :], POOL[4][0:1, :], AF.Exp, scale=-0.5, inc="se")
        p.wait("gpsimd", "se")
        p.emit("gpsimd", lambda e: e.partition_broadcast(POOL[2][:, :], POOL[4][0:1, :]),
               inc="ge")
        p.wait("vector", "ge")
        for i in range(4):
            TT("vector", POOL[3][:, :], PS[i][:, :], POOL[2][:, :], ALU.mult)
            p.emit("vector", lambda e, i_=i: e.tensor_scalar(
                ckvn[:, i_ * 512:(i_ + 1) * 512], POOL[3][:, :],
                kvsc[:, i_:i_ + 1], None, ALU.mult), inc="ve")
        ckv_ready = p.val("ve")

        if do_dump and qc == 0 and "d_ckv" in dumps:
            p.wait("sync", "ve")
            DMA("sync", dumps["d_ckv"], SCB[:, 0:2048])

        kb_tok = 0
        for hg in range(4):
            p.wait("sync", "pe", kb_tok)
            p.wait("sync", "ve")
            for lc in range(4):
                DMA("sync", STRM[:, lc * 512:(lc + 1) * 512],
                    wkvbf_scr[lc * 128:(lc + 1) * 128, hg * 512:(hg + 1) * 512])
            p.wait("tensor", "dS")
            if hg == 0:
                p.wait("tensor", "ve", ckv_ready)
            for hh in range(4):
                for lc in range(4):
                    MM(PS[4 + hh][:, :],
                       STRM[:, lc * 512 + hh * 128:lc * 512 + (hh + 1) * 128],
                       ckvn[:, lc * 512:(lc + 1) * 512],
                       first=(lc == 0), last=(lc == 3),
                       inc=("pe" if (hh == 3 and lc == 3) else None))
            kb_tok = p.val("pe")
            p.wait("vector", "pe")
            for hh in range(4):
                h = hg * 4 + hh
                COPY("vector", Kn[:, h * S + q0:h * S + q0 + 512], PS[4 + hh][:, :],
                     inc=("ve" if hh == 3 else None))

        v_tok = kb_tok
        for vg in range(4):
            p.wait("sync", "pe", v_tok)
            p.wait("sync", "ve")
            for lc in range(4):
                DMA("sync", STRM[:, lc * 512:(lc + 1) * 512],
                    wkvbf_scr[lc * 128:(lc + 1) * 128,
                              2048 + vg * 512:2048 + (vg + 1) * 512])
            p.wait("tensor", "dS")
            p.wait("tensor", "ve")
            for j in range(4):
                for lc in range(4):
                    MM(PS[4 + j][:, :],
                       ckvn[:, lc * 512 + j * 128:lc * 512 + (j + 1) * 128],
                       STRM[:, lc * 512:(lc + 1) * 512],
                       first=(lc == 0), last=(lc == 3),
                       inc=("pe" if (j == 3 and lc == 3) else None))
            v_tok = p.val("pe")
            p.wait("vector", "pe")
            for j in range(4):
                kb = qc * 4 + j
                COPY("vector", Vt[:, kb * 2048 + vg * 512:kb * 2048 + (vg + 1) * 512],
                     PS[4 + j][:, :], inc=("ve" if j == 3 else None))

        for j in range(4):
            p.wait("sync", "pe")
            p.wait("sync", "ve")
            DMA("sync", STRM[:, 0:2048],
                bass.AP(wqbf_scr.tensor, (16 + j) * NCH * 16384,
                        [[128, 128], [16384, 16], [1, 128]]))
            p.wait("tensor", "dS")
            p.wait("tensor", "ve")
            for c in range(NCH):
                MM(PS[0][:, :], STRM[:, c * 128:(c + 1) * 128],
                   hq[:, c * 512:(c + 1) * 512],
                   first=(c == 0), last=(c == NCH - 1),
                   inc=("pe" if c == NCH - 1 else None))
            p.wait("sync", "pe")
            DMA("sync", STRM[:, 0:2048],
                bass.AP(wqbf_scr.tensor, (20 + j) * NCH * 16384,
                        [[128, 128], [16384, 16], [1, 128]]))
            p.wait("tensor", "dS")
            for c in range(NCH):
                MM(PS[1][:, :], STRM[:, c * 128:(c + 1) * 128],
                   hq[:, c * 512:(c + 1) * 512],
                   first=(c == 0), last=(c == NCH - 1),
                   inc=("pe" if c == NCH - 1 else None))
            p.wait("vector", "pe")
            TT("vector", POOL[0][:, :], PS[0][:, :], cosq[:, 0:512], ALU.mult)
            TT("vector", POOL[1][:, :], PS[1][:, :], cosq[:, 512:1024], ALU.mult)
            TT("vector", qpe[:, j * 512:(j + 1) * 512], POOL[0][:, :], POOL[1][:, :],
               ALU.subtract)
            TT("vector", POOL[0][:, :], PS[1][:, :], cosq[:, 0:512], ALU.mult)
            TT("vector", POOL[1][:, :], PS[0][:, :], cosq[:, 512:1024], ALU.mult)
            TT("vector", qpe[:, (4 + j) * 512:(5 + j) * 512], POOL[0][:, :],
               POOL[1][:, :], ALU.add, inc="ve")

        if do_dump and qc == 0 and "d_qpe" in dumps:
            p.wait("sync", "ve")
            DMA("sync", dumps["d_qpe"], qpe[:, :])

        nkb = 4 * (qc + 1)
        qn_strm_tok = 0
        ps2_tok = 0
        qnbuf_tok = 0
        zo_tok = 0
        ob_free_tok = 0
        wo_strm_tok = [0, 0]
        acc_pool_tok = [0, 0]
        prev_acc_round = 0
        for h in range(16):
            j4 = h % 4
            jg = h // 4
            p.wait("sync", "pe", qn_strm_tok)
            DMA("sync", STRM[:, 0:2048],
                bass.AP(wqbf_scr.tensor, h * NCH * 16384,
                        [[128, 128], [16384, 16], [1, 128]]))
            p.wait("tensor", "dS")
            p.wait("tensor", "ve", ps2_tok)
            for c in range(NCH):
                MM(PS[2][:, :], STRM[:, c * 128:(c + 1) * 128],
                   hq[:, c * 512:(c + 1) * 512],
                   first=(c == 0), last=(c == NCH - 1),
                   inc=("pe" if c == NCH - 1 else None))
            qn_strm_tok = p.val("pe")
            p.wait("vector", "pe")
            p.wait("vector", "pe", qnbuf_tok)
            COPY("vector", qn[:, 0:512], PS[2][:, :], inc="ve")
            ps2_tok = p.val("ve")
            qn_ready = p.val("ve")

            sexp_tok = [0, 0]
            zpv_tok = [0, 0]
            for step in range(nkb + 1):
                if step < nkb:
                    kb = step
                    sbank = kb % 2
                    pslot = kb % 2
                    p.wait("tensor", "se", sexp_tok[sbank])
                    if kb == 0:
                        p.wait("tensor", "ve", qn_ready)
                        if h > 0 and j4 == 0:
                            p.wait("tensor", "pe", ob_free_tok)
                    tp96 = (96, 0) if j4 == 3 else None
                    MM(PS[sbank][:, :], Kn[:, h * S + kb * 128:h * S + (kb + 1) * 128],
                       qn[:, 0:512], first=True, last=False)
                    MM(PS[sbank][:, :],
                       kpe1[j4 * 32:(j4 + 1) * 32, kb * 128:(kb + 1) * 128],
                       qpe[j4 * 32:(j4 + 1) * 32, jg * 512:(jg + 1) * 512],
                       first=False, last=False, tp=tp96)
                    MM(PS[sbank][:, :],
                       kpe2[j4 * 32:(j4 + 1) * 32, kb * 128:(kb + 1) * 128],
                       qpe[j4 * 32:(j4 + 1) * 32, (4 + jg) * 512:(5 + jg) * 512],
                       first=False, last=True, inc="pe", tp=tp96)
                    score_tok = p.val("pe")
                    p.wait("scalar", "pe", score_tok)
                    if kb >= 2:
                        p.wait("scalar", "pe", zpv_tok[pslot])
                    ACT(Pb[:, pslot * 512:(pslot + 1) * 512], PS[sbank][:, :], AF.Exp,
                        bias=smal[:, 0:1], scale=SCALE, inc="se")
                    sexp_tok[sbank] = p.val("se")
                    if kb >= qc * 4:
                        p.wait("gpsimd", "se", sexp_tok[sbank])
                        p.emit("gpsimd", lambda e, ps_=pslot, b_=q0 - kb * 128:
                               e.affine_select(
                                   out=Pb[:, ps_ * 512:(ps_ + 1) * 512],
                                   in_=Pb[:, ps_ * 512:(ps_ + 1) * 512],
                                   pattern=[[1, 512]], compare_op=ALU.is_ge,
                                   fill=0.0, base=b_, channel_multiplier=-1),
                               inc="ge")
                if step >= 1:
                    kb = step - 1
                    pslot = kb % 2
                    if kb >= qc * 4:
                        p.wait("tensor", "ge")
                    else:
                        p.wait("tensor", "se", sexp_tok[kb % 2])
                    if kb == 0:
                        p.wait("tensor", "ve", zo_tok)
                    MM(PS[6][0:32, :], ones32[:, :],
                       Pb[:, pslot * 512:(pslot + 1) * 512],
                       first=(kb == 0), last=(kb == nkb - 1))
                    MM(PS[7][:, :],
                       Vt[:, kb * 2048 + h * 128:kb * 2048 + (h + 1) * 128],
                       Pb[:, pslot * 512:(pslot + 1) * 512],
                       first=(kb == 0), last=(kb == nkb - 1), inc="pe")
                    zpv_tok[pslot] = p.val("pe")
            qnbuf_tok = p.val("pe")
            p.wait("vector", "pe")
            p.emit("vector", lambda e: e.reciprocal(POOL[0][0:1, :], PS[6][0:1, :]),
                   inc="ve")
            p.wait("gpsimd", "ve")
            p.emit("gpsimd", lambda e: e.partition_broadcast(POOL[1][:, :],
                                                             POOL[0][0:1, :]), inc="ge")
            p.wait("vector", "ge")
            if h >= 4:
                p.wait("vector", "pe", ob_free_tok)
            TT("vector", Ob[:, j4 * 512:(j4 + 1) * 512], PS[7][:, :],
               POOL[1][:, :], ALU.mult, inc="ve")
            zo_tok = p.val("ve")

            if j4 == 3:
                quarter = h // 4
                if do_dump and qc == 0 and quarter == 0 and "d_ob" in dumps:
                    p.wait("sync", "ve")
                    DMA("sync", dumps["d_ob"], Ob[:, :])
                if quarter > 0:
                    p.wait("gpsimd", "dG", prev_acc_round)
                p.wait("gpsimd", "dS", xw_tok)
                ob_done = p.val("ve")
                wo_ts_tok = [0, 0]
                for oc in range(NCH):
                    slot2 = oc % 2
                    p.wait("sync", "pe", wo_strm_tok[slot2])
                    DMA("sync", STRM[:, slot2 * 1024:slot2 * 1024 + 512],
                        bass.AP(wobf_scr.tensor, (oc * NCH + quarter * 4) * 16384,
                                [[128, 128], [16384, 4], [1, 128]]))
                    p.wait("tensor", "dS")
                    if oc == 0:
                        p.wait("tensor", "ve", ob_done)
                    p.wait("tensor", "ve", wo_ts_tok[slot2])
                    for t in range(4):
                        MM(PS[2 + slot2][:, :],
                           STRM[:, slot2 * 1024 + t * 128:slot2 * 1024 + (t + 1) * 128],
                           Ob[:, t * 512:(t + 1) * 512],
                           first=(t == 0), last=(t == 3),
                           inc=("pe" if t == 3 else None))
                    wo_strm_tok[slot2] = p.val("pe")
                    p.wait("vector", "pe")
                    p.wait("vector", "dG", acc_pool_tok[slot2])
                    p.emit("vector", lambda e, sl=slot2, oc_=oc: e.tensor_scalar(
                        POOL[2 + sl][:, :], PS[2 + sl][:, :],
                        modc[:, mcol(2, oc_, s):mcol(2, oc_, s) + 1], None, ALU.mult),
                           inc="ve")
                    wo_ts_tok[slot2] = p.val("ve")
                    p.wait("gpsimd", "ve")
                    DMAACC(
                        bass.AP(x1_scr.tensor, s * E * S + (oc * 128) * S + q0,
                                [[S, 128], [1, 512]]),
                        POOL[2 + slot2][:, :])
                    acc_pool_tok[slot2] = p.val("dG")
                prev_acc_round = p.val("dG")
                ob_free_tok = p.val("pe")
                ps2_tok = p.val("ve")

        if do_dump and qc == NQC - 1:
            if "d_kn" in dumps:
                p.wait("sync", "ve")
                DMA("sync", dumps["d_kn"], Kn[:, :])
            if "d_vt" in dumps:
                DMA("sync", dumps["d_vt"], Vt[:, :])
            if "d_kpe" in dumps:
                DMA("sync", dumps["d_kpe"][:, 0:S], kpe1[:, :])
                DMA("sync", dumps["d_kpe"][:, S:2 * S], kpe2[:, :])

    def ffn_tt(s, tt, do_dump):
        t0 = tt * 512

        p.wait("sync", "dG")
        p.wait("sync", "dS")
        p.wait("sync", "ve")
        p.wait("sync", "pe")
        for c in range(NCH):
            DMA("sync", x1t[:, c * 512:(c + 1) * 512],
                bass.AP(x1_scr.tensor, s * E * S + (c * 128) * S + t0,
                        [[S, 128], [1, 512]]))

        p.wait("vector", "dS")
        p.wait("scalar", "dS")
        first_tok = 0
        for c in range(NCH):
            csl = c % 2
            if c >= 2:
                p.wait("vector", "pe", first_tok)
                p.wait("scalar", "pe", first_tok)
            COPY("vector", SCB[:, csl * 512:(csl + 1) * 512],
                 x1t[:, c * 512:(c + 1) * 512], inc="ve")
            ACT(SCB[:, 1024 + csl * 512:1024 + (csl + 1) * 512],
                x1t[:, c * 512:(c + 1) * 512], AF.Square, inc="se")
            p.wait("tensor", "ve")
            p.wait("tensor", "se")
            MM(PS[0][0:32, :], ones32[:, :], SCB[:, csl * 512:(csl + 1) * 512],
               first=(c == 0), last=(c == NCH - 1))
            MM(PS[1][0:32, :], ones32[:, :],
               SCB[:, 1024 + csl * 512:1024 + (csl + 1) * 512],
               first=(c == 0), last=(c == NCH - 1), inc="pe")
            first_tok = p.val("pe")

        def rowmath_bcast(ps_sum, ps_sq, n, eps_col, mu_dst, r_dst):
            p.wait("vector", "pe")
            TS("vector", POOL[0][0:1, :], ps_sum[0:1, :], 1.0 / n, None, ALU.mult)
            TT("vector", POOL[1][0:1, :], POOL[0][0:1, :], POOL[0][0:1, :], ALU.mult)
            STT("vector", POOL[1][0:1, :], ps_sq[0:1, :], 1.0 / n, POOL[1][0:1, :],
                ALU.mult, ALU.subtract, inc="ve")
            p.wait("scalar", "ve")
            ACT(POOL[1][0:1, :], POOL[1][0:1, :], AF.Ln, bias=eps_col[0:1, :])
            ACT(POOL[1][0:1, :], POOL[1][0:1, :], AF.Exp, scale=-0.5, inc="se")
            p.wait("gpsimd", "ve")
            p.wait("gpsimd", "se")
            p.emit("gpsimd", lambda e, d=mu_dst: e.partition_broadcast(d, POOL[0][0:1, :]))
            p.emit("gpsimd", lambda e, d=r_dst: e.partition_broadcast(d, POOL[1][0:1, :]),
                   inc="ge")
            p.wait("vector", "ge")

        rowmath_bcast(PS[0], PS[1], E, smal[:, 1:2], POOL[2][:, :], POOL[3][:, :])

        first_tok = 0
        for c in range(NCH):
            csl = c % 2
            if c >= 2:
                p.wait("vector", "pe", first_tok)
                p.wait("scalar", "pe", first_tok)
            TT("vector", POOL[4][:, :], x1t[:, c * 512:(c + 1) * 512], POOL[2][:, :],
               ALU.subtract)
            TT("vector", POOL[4][:, :], POOL[4][:, :], POOL[3][:, :], ALU.mult)
            p.emit("vector", lambda e, c_=c: e.tensor_scalar(
                h2t[:, c_ * 512:(c_ + 1) * 512], POOL[4][:, :],
                modc[:, mcol(4, c_, s):mcol(4, c_, s) + 1],
                modc[:, mcol(3, c_, s):mcol(3, c_, s) + 1],
                ALU.mult, ALU.add))
            COPY("vector", SCB[:, csl * 512:(csl + 1) * 512],
                 h2t[:, c * 512:(c + 1) * 512], inc="ve")
            if c == 0:
                COPY("vector", Pb[:, 0:512], SCB[:, csl * 512:(csl + 1) * 512])
                COPY("vector", Pb[:, 512:1024], SCB[:, csl * 512:(csl + 1) * 512])
            else:
                TT("vector", Pb[:, 0:512], Pb[:, 0:512],
                   SCB[:, csl * 512:(csl + 1) * 512], ALU.max)
                TT("vector", Pb[:, 512:1024], Pb[:, 512:1024],
                   SCB[:, csl * 512:(csl + 1) * 512], ALU.min)
            p.wait("scalar", "ve")
            ACT(SCB[:, 1024 + csl * 512:1024 + (csl + 1) * 512],
                h2t[:, c * 512:(c + 1) * 512], AF.Square, inc="se")
            p.wait("tensor", "ve")
            p.wait("tensor", "se")
            MM(PS[2][0:32, :], ones32[:, :], SCB[:, csl * 512:(csl + 1) * 512],
               first=(c == 0), last=(c == NCH - 1))
            MM(PS[3][0:32, :], ones32[:, :],
               SCB[:, 1024 + csl * 512:1024 + (csl + 1) * 512],
               first=(c == 0), last=(c == NCH - 1), inc="pe")
            first_tok = p.val("pe")
        rowmath_bcast(PS[2], PS[3], E, smal[:, 3:4], POOL[2][:, :], POOL[3][:, :])

        TT("vector", POOL[4][:, :], Pb[:, 0:512], POOL[2][:, :], ALU.subtract)
        TT("vector", POOL[0][:, :], POOL[2][:, :], Pb[:, 512:1024], ALU.subtract)
        TT("vector", POOL[4][:, :], POOL[4][:, :], POOL[0][:, :], ALU.max, inc="ve")
        p.wait("gpsimd", "ve")
        p.emit("gpsimd", lambda e: e.partition_all_reduce(
            POOL[1][:, :], POOL[4][:, :], 128, ReduceOp.max), inc="ge")
        p.wait("vector", "ge")
        p.emit("vector", lambda e: e.reciprocal(POOL[0][:, :], POOL[1][:, :]))
        TS("vector", POOL[0][:, :], POOL[0][:, :], 128.0, None, ALU.mult)
        TT("vector", POOL[3][:, :], POOL[3][:, :], POOL[1][:, :], ALU.mult)
        TS("vector", POOL[3][:, :], POOL[3][:, :], 1.0 / 128.0, None, ALU.mult, inc="ve")

        xh_tok = [0] * NCH
        for c in range(NCH):
            TT("vector", POOL[4][:, :], h2t[:, c * 512:(c + 1) * 512], POOL[2][:, :],
               ALU.subtract)
            TT("vector", POOL[4][:, :], POOL[4][:, :], POOL[0][:, :], ALU.mult)
            TS("vector", POOL[4][:, :], POOL[4][:, :], CLIP_HI, -CLIP_HI, ALU.min,
               ALU.max)
            TT("vector", xh[:, c * 512:(c + 1) * 512], POOL[4][:, :], POOL[3][:, :],
               ALU.mult, inc="ve")
            xh_tok[c] = p.val("ve")
        xh_ready = p.val("ve")
        if do_dump and "d_xh" in dumps:
            p.wait("sync", "ve")
            DMA("sync", dumps["d_xh"], xh[:, :])

        t4_tok = p.t4_tok
        for u in range(64):
            og, g4 = divmod(u, 4)
            buf = u % 3
            q = ("sync", "dS") if u % 2 == 0 else ("scalar", "dC")
            p.wait(q[0], "pe", t4_tok[buf])
            DMA(q[0], T4[buf][:, :].rearrange("p (j i q) -> p j i q", j=4, i=4),
                bass.AP(tfc_scr.tensor, (og * 4 + g4) * 262144,
                        [[128, 128], [65536, 4], [16384, 4], [1, 128]]),
                sem=q[1])
            dtok = p.val(q[1])
            p.wait("tensor", q[1], dtok)
            p.wait("tensor", "ve", xh_tok[g4 * 4 + 3])
            if g4 == 0:
                p.wait("tensor", "se")
            for j in range(4):
                d = g4 * 4 + j
                for i in range(4):
                    MM(PS[i][:, :], T4[buf][:, j * 512 + i * 128:j * 512 + (i + 1) * 128],
                       xh[:, d * 512:(d + 1) * 512],
                       first=(d == 0), last=(d == NCH - 1),
                       inc=("pe" if i == 3 else None))
            t4_tok[buf] = p.val("pe")
            if g4 == 3:
                p.wait("scalar", "pe")
                for i in range(4):
                    oc = og * 4 + i
                    ACT(gbuf[:, oc * 512:(oc + 1) * 512], PS[i][:, :], AF.Gelu,
                        scale=gam[:, 2:3], inc=("se" if i == 3 else None))
                gelu_tok = p.val("se")
                p.wait("scalar", "se", gelu_tok)
                p.wait("vector", "se", gelu_tok)
                for i in range(4):
                    oc = og * 4 + i
                    csl = i % 2
                    if og > 0 or i >= 2:
                        p.wait("scalar", "pe", p.gstat_tok)
                    ACT(SCB[:, 1024 + csl * 512:1024 + (csl + 1) * 512],
                        gbuf[:, oc * 512:(oc + 1) * 512], AF.Square, inc="se")
                    if oc == 0:
                        COPY("vector", Pb[:, 0:512], gbuf[:, 0:512])
                        COPY("vector", Pb[:, 512:1024], gbuf[:, 0:512])
                    else:
                        TT("vector", Pb[:, 0:512], Pb[:, 0:512],
                           gbuf[:, oc * 512:(oc + 1) * 512], ALU.max)
                        TT("vector", Pb[:, 512:1024], Pb[:, 512:1024],
                           gbuf[:, oc * 512:(oc + 1) * 512], ALU.min)
                    p.wait("tensor", "se")
                    MM(PS[4][0:32, :], ones32[:, :], gbuf[:, oc * 512:(oc + 1) * 512],
                       first=(oc == 0), last=(oc == NOC - 1))
                    MM(PS[5][0:32, :], ones32[:, :],
                       SCB[:, 1024 + csl * 512:1024 + (csl + 1) * 512],
                       first=(oc == 0), last=(oc == NOC - 1), inc="pe")
                    p.gstat_tok = p.val("pe")
        if do_dump and "d_g" in dumps:
            p.wait("sync", "se")
            DMA("sync", dumps["d_g"], gbuf[:, 0:NOC * 512])

        rowmath_bcast(PS[4], PS[5], DFF, smal[:, 3:4], POOL[2][:, :], POOL[3][:, :])
        TT("vector", POOL[4][:, :], Pb[:, 0:512], POOL[2][:, :], ALU.subtract)
        TT("vector", POOL[0][:, :], POOL[2][:, :], Pb[:, 512:1024], ALU.subtract)
        TT("vector", POOL[4][:, :], POOL[4][:, :], POOL[0][:, :], ALU.max, inc="ve")
        p.wait("gpsimd", "ve")
        p.emit("gpsimd", lambda e: e.partition_all_reduce(
            POOL[1][:, :], POOL[4][:, :], 128, ReduceOp.max), inc="ge")
        p.wait("vector", "ge")
        COPY("vector", SCB[:, 1024:1536], POOL[2][:, :])
        TT("vector", POOL[3][:, :], POOL[3][:, :], POOL[1][:, :], ALU.mult)
        TS("vector", POOL[3][:, :], POOL[3][:, :], 1.0 / 128.0, None, ALU.mult)
        COPY("vector", SCB[:, 0:512], POOL[3][:, :])
        p.emit("vector", lambda e: e.reciprocal(POOL[0][:, :], POOL[1][:, :]))
        TS("vector", POOL[0][:, :], POOL[0][:, :], 128.0, None, ALU.mult)
        COPY("vector", SCB[:, 512:1024], POOL[0][:, :], inc="ve")

        xh2_tok = [0] * NOC
        for oc in range(NOC):
            TT("vector", gbuf[:, oc * 512:(oc + 1) * 512],
               gbuf[:, oc * 512:(oc + 1) * 512], SCB[:, 1024:1536], ALU.subtract)
            TT("vector", gbuf[:, oc * 512:(oc + 1) * 512],
               gbuf[:, oc * 512:(oc + 1) * 512], SCB[:, 512:1024], ALU.mult)
            TS("vector", gbuf[:, oc * 512:(oc + 1) * 512],
               gbuf[:, oc * 512:(oc + 1) * 512], CLIP_HI, -CLIP_HI, ALU.min, ALU.max)
            TT("vector", gbuf[:, oc * 512:(oc + 1) * 512],
               gbuf[:, oc * 512:(oc + 1) * 512], SCB[:, 0:512], ALU.mult,
               inc="ve")
            xh2_tok[oc] = p.val("ve")

        t4_tok = p.t4_tok
        for u in range(64):
            ocg, k4 = divmod(u, 16)
            buf = u % 3
            q = ("sync", "dS") if u % 2 == 0 else ("scalar", "dC")
            p.wait(q[0], "pe", t4_tok[buf])
            DMA(q[0], T4[buf][:, :].rearrange("p (j i q) -> p j i q", j=4, i=4),
                bass.AP(tpj_scr.tensor, (ocg * 16 + k4) * 262144,
                        [[128, 128], [65536, 4], [16384, 4], [1, 128]]),
                sem=q[1])
            dtok = p.val(q[1])
            p.wait("tensor", q[1], dtok)
            p.wait("tensor", "ve", xh2_tok[k4 * 4 + 3])
            if k4 == 0:
                p.wait("tensor", "ve")
            for j in range(4):
                kk = k4 * 4 + j
                for i in range(4):
                    MM(PS[i][:, :], T4[buf][:, j * 512 + i * 128:j * 512 + (i + 1) * 128],
                       gbuf[:, kk * 512:(kk + 1) * 512],
                       first=(kk == 0), last=(kk == 4 * NCH - 1),
                       inc=("pe" if i == 3 else None))
            t4_tok[buf] = p.val("pe")
            if k4 == 15:
                p.wait("vector", "pe")
                for i in range(4):
                    oc = ocg * 4 + i
                    p.emit("vector", lambda e, i_=i, oc_=oc: e.scalar_tensor_tensor(
                        x1t[:, oc_ * 512:(oc_ + 1) * 512], PS[i_][:, :],
                        modc[:, 192 + oc_ * 2 + s:192 + oc_ * 2 + s + 1],
                        x1t[:, oc_ * 512:(oc_ + 1) * 512],
                        ALU.mult, ALU.add), inc=("ve" if i == 3 else None))
        out_dma_tok = 0
        p.wait("tensor", "ve")
        for rt in range(4):
            bsl = rt % 2
            for ocg4 in range(4):
                for i in range(4):
                    oc = ocg4 * 4 + i
                    TR(PS[4 + bsl][:, i * 128:(i + 1) * 128],
                       x1t[:, oc * 512 + rt * 128:oc * 512 + (rt + 1) * 128],
                       inc=("pe" if i == 3 else None))
                p.wait("vector", "pe")
                if ocg4 == 0:
                    p.wait("vector", "dS", out_dma_tok)
                COPY("vector", otm[:, ocg4 * 512:(ocg4 + 1) * 512], PS[4 + bsl][:, :],
                     inc="ve")
                p.wait("tensor", "ve")
            p.wait("sync", "ve")
            DMA("sync", out_hbm[s, t0 + rt * 128:t0 + (rt + 1) * 128, :], otm[:, :])
            out_dma_tok = p.val("dS")

    PS = []

    if "M" in phases:
        phase_M()
    ctxM.close()
    PS.extend(ctx.enter_context(nc.psum_tensor(f"ps{i}", [128, 512], F32))
              for i in range(8))
    if "A" in phases:
        for s in range(NSEQ):
            p.barrier()
            for qc in range(NQC):
                attn_qc(s, qc, s == 0)
            p.barrier()
            if s == 0 and "d_x1scr" in dumps:
                p.wait("sync", "dG")
                DMA("sync", dumps["d_x1scr"], x1_scr[0])
            if "B" in phases:
                for tt in range(NQC):
                    ffn_tt(s, tt, s == 0 and tt == 0)
                p.barrier()

    p.barrier()
    for e in ENGS:
        p.wait(e, "dS")
        p.wait(e, "dG")
        p.wait(e, "dC")

    with nc.Block() as block:
        p.replay(block)
    ctx.close()
    nc.compile()
    return nc, dumps


def _consts():
    pos = np.arange(S, dtype=np.float64)
    inv = 1.0 / (10000.0 ** (np.arange(0, ROPE_D, 2, dtype=np.float64) / ROPE_D))
    ang = np.outer(inv, pos)
    cosT = np.tile(np.cos(ang), (4, 1)).astype(np.float32)
    sinT = np.tile(np.sin(ang), (4, 1)).astype(np.float32)
    return np.ascontiguousarray(cosT), np.ascontiguousarray(sinT)


def make_in_maps(x, c, wq, wkva, wkvb, wo, kvs, wfc, wpj, wmod):
    cosT, sinT = _consts()
    base = dict(
        wq=np.ascontiguousarray(np.asarray(wq, np.float32)),
        wkva=np.ascontiguousarray(np.asarray(wkva, np.float32)),
        wkvb=np.ascontiguousarray(np.asarray(wkvb, np.float32)),
        wo=np.ascontiguousarray(np.asarray(wo, np.float32)),
        kvs=np.ascontiguousarray(np.asarray(kvs, np.float32)).reshape(1, LORA),
        wfc=np.ascontiguousarray(np.asarray(wfc, np.float32)),
        wpj=np.ascontiguousarray(np.asarray(wpj, np.float32)),
        wmod=np.ascontiguousarray(np.asarray(wmod, np.float32)),
        cosT=cosT, sinT=sinT,
    )
    x = np.ascontiguousarray(np.asarray(x, np.float32))
    c = np.ascontiguousarray(np.asarray(c, np.float32))
    in_maps = []
    for core in range(8):
        m = dict(base)
        m["x"] = np.ascontiguousarray(x[2 * core:2 * core + 2])
        m["c"] = np.ascontiguousarray(c[2 * core:2 * core + 2])
        in_maps.append(m)
    return in_maps


def kernel(L0_attn_x, L0_attn_wq, L0_attn_wkv_a, L0_attn_wkv_b, L0_attn_wo,
           L0_attn_kv_norm_scale, L0_ffn_w_fc, L0_ffn_w_proj, L0_c, L0_wmod):
    in_maps = make_in_maps(L0_attn_x, L0_c, L0_attn_wq, L0_attn_wkv_a, L0_attn_wkv_b,
                           L0_attn_wo, L0_attn_kv_norm_scale, L0_ffn_w_fc,
                           L0_ffn_w_proj, L0_wmod)
    nc, _ = build_nc()
    res = run_bass_kernel_spmd(nc, in_maps, core_ids=list(range(8)))
    out = np.concatenate([np.asarray(r["out"]) for r in res.results], axis=0)
    return out.astype(np.float32)


# version 10
# speedup vs baseline: 1.0170x; 1.0170x over previous
import contextlib
import numpy as np
import ml_dtypes

import concourse.bacc as bacc
import concourse.bass as bass
import concourse.mybir as mybir
from concourse.bass_utils import run_bass_kernel_spmd
from concourse.bass_isa import ReduceOp

F32 = mybir.dt.float32
BF16 = mybir.dt.bfloat16
U8 = mybir.dt.uint8
AF = mybir.ActivationFunctionType
ALU = mybir.AluOpType
AX = mybir.AxisListType

B, S, E, H = 16, 2048, 2048, 16
NOPE, ROPE_D, VD, QH, LORA = 128, 64, 128, 192, 512
DFF = 8192
NSEQ = 2
QCW = 512
NQC = S // QCW
NCH = E // 128
NKB = S // 128
NOC = DFF // 128
SCALE = float(QH) ** -0.5
EXPB = -8.0
EPS_LN = 1e-5
EPS_ATT = 1e-6
EPS_FFN = 1e-6
CLIP_HI = 128.0 - 1e-6

ENGS = ("sync", "gpsimd", "scalar", "tensor", "vector")


class Plan:
    def __init__(self, nc):
        self.nc = nc
        self.prog = {e: [] for e in ENGS}
        self.sems = {}

    def sem(self, name):
        h = self.nc.alloc_semaphore(name=name)
        self.sems[name] = [h, 0]

    def emit(self, eng, fn, inc=None, amt=1):
        if inc is not None:
            rec = self.sems[inc]
            rec[1] += amt
            self.prog[eng].append((fn, rec[0], amt))
        else:
            self.prog[eng].append((fn, None, 0))

    def wait(self, eng, sem, val=None):
        rec = self.sems[sem]
        h = rec[0]
        v = rec[1] if val is None else val
        if v <= 0:
            return
        self.prog[eng].append((lambda e, h=h, v=v: e.wait_ge(h, v), None, 0))

    def val(self, sem):
        return self.sems[sem][1]

    def barrier(self):
        bar = self.sems["bar"]
        h = bar[0]
        for e in ENGS:
            self.prog[e].append((lambda eng, h=h: eng.sem_inc(h, 1), None, 0))
        bar[1] += len(ENGS)
        v = bar[1]
        for e in ENGS:
            self.prog[e].append((lambda eng, h=h, v=v: eng.wait_ge(h, v), None, 0))

    def replay(self, block):
        def runner(items):
            def run(e):
                for fn, semh, amt in items:
                    ins = fn(e)
                    if semh is not None:
                        ins.then_inc(semh, amt)
            return run

        block.sync(runner(self.prog["sync"]))
        block.gpsimd(runner(self.prog["gpsimd"]))
        block.scalar(runner(self.prog["scalar"]))
        block.tensor(runner(self.prog["tensor"]))
        block.vector(runner(self.prog["vector"]))


def mcol(v, c, s_):
    return v * 32 + c * 2 + s_


def build_nc(phases=("M", "A", "B"), dump=(), nogamma=False):
    nc = bacc.Bacc("TRN2", target_bir_lowering=False)
    dump = set(dump)

    def din(name, shape, dt=F32):
        return nc.dram_tensor(name, list(shape), dt, kind="ExternalInput").ap()

    x_in = din("x", (NSEQ, S, E))
    c_in = din("c", (NSEQ, E))
    wq_in = din("wq", (E, H * QH))
    wkva_in = din("wkva", (E, LORA + ROPE_D))
    wkvb_in = din("wkvb", (LORA, H * (NOPE + VD)))
    wo_in = din("wo", (H * VD, E))
    kvs_in = din("kvs", (1, LORA))
    wfc_in = din("wfc", (DFF, E))
    wpj_in = din("wpj", (E, DFF))
    wmod_in = din("wmod", (E, 6 * E))
    cos_in = din("cosT", (128, S))
    sin_in = din("sinT", (128, S))

    out_hbm = nc.dram_tensor("out", [NSEQ, S, E], F32, kind="ExternalOutput").ap()

    def scr(name, shape, dt):
        return nc.dram_tensor(name, list(shape), dt, kind="Internal").ap()

    x1_scr = scr("x1s", (NSEQ, E, S), F32)
    wqbf_scr = scr("wqbf", (24 * NCH, 128 * 128), BF16)
    wobf_scr = scr("wobf", (NCH * NCH, 128 * 128), BF16)
    wkvbf_scr = scr("wkvbf", (LORA, H * (NOPE + VD)), BF16)
    wkvabf_scr = scr("wkvabf", (E, LORA + 256), BF16)
    tfc_scr = scr("tfc", (NCH * NOC, 128 * 128), BF16)
    tpj_scr = scr("tpj", (NOC * NCH, 128 * 128), BF16)

    dumps = {}

    def mkdump(name, shape, dt):
        dumps[name] = nc.dram_tensor(name, list(shape), dt, kind="ExternalOutput").ap()

    if "mod" in dump:
        mkdump("d_mod", (128, 224), F32)
    if "gam" in dump:
        mkdump("d_gam", (128, 8), F32)
    if "h" in dump:
        mkdump("d_h", (128, NCH * QCW), BF16)
    if "ckv" in dump:
        mkdump("d_ckv", (128, 4 * QCW), BF16)
    if "kpe" in dump:
        mkdump("d_kpe", (128, 2 * S), BF16)
    if "kn" in dump:
        mkdump("d_kn", (128, H * S), BF16)
    if "vt" in dump:
        mkdump("d_vt", (128, NKB * 2048), BF16)
    if "qpe" in dump:
        mkdump("d_qpe", (128, 8 * QCW), BF16)
    if "ob" in dump:
        mkdump("d_ob", (128, 4 * QCW), BF16)
    if "xh" in dump:
        mkdump("d_xh", (128, NCH * QCW), BF16)
    if "g" in dump:
        mkdump("d_g", (128, NOC * QCW), BF16)
    if "wscr" in dump:
        mkdump("d_wscr", (LORA, H * (NOPE + VD)), BF16)
    if "wqscr" in dump:
        mkdump("d_wqscr", (24 * NCH, 128 * 128), BF16)
    if "x1scr" in dump:
        mkdump("d_x1scr", (E, S), F32)

    ctx = contextlib.ExitStack()

    def sbt(name, shape, dt=F32):
        return ctx.enter_context(nc.sbuf_tensor(name, list(shape), dt))

    modc = sbt("modc", (128, 224))
    smal = sbt("smal", (128, 8))
    kvsc = sbt("kvsc", (128, 4))
    gam = sbt("gam", (128, 8))
    ones32 = sbt("ones32", (128, 32), BF16)
    id128 = sbt("id128", (128, 128))
    id128b = sbt("id128b", (128, 128), BF16)
    POOL = [sbt(f"pool{i}", (128, 512)) for i in range(5)]
    SCB = sbt("scb", (128, 2048), BF16)
    STRM = sbt("strm", (128, 2048), BF16)
    STRF = [sbt(f"strf{i}", (128, 2048)) for i in range(2)]

    ar1 = sbt("ar1", (128, 65536), U8)
    ar2 = sbt("ar2", (128, 4096), U8)
    ar3 = sbt("ar3", (128, 8192), U8)
    a1 = nc.lookup_mloc(ar1).addr
    a2 = nc.lookup_mloc(ar2).addr
    a3 = nc.lookup_mloc(ar3).addr

    def at(name, shape, dt, base, off):
        return nc.alloc_sbuf_tensor_at(name, list(shape), dt, offset=base + off)

    Kn = at("Kn", (128, H * S), BF16, a1, 0)
    x1t = at("x1t", (128, NCH * 512), F32, a1, 0)
    h2t = at("h2t", (128, NCH * 512), F32, a1, 32768)
    cosq = at("cosq", (128, 2 * QCW), F32, a2, 0)
    T4 = [at("t4a", (128, 2048), BF16, a2, 0)]
    qpe = at("qpe", (128, 8 * QCW), BF16, a3, 0)
    otm = at("otm", (128, 2048), F32, a3, 0)

    Vt = sbt("Vt", (128, NKB * 2048), BF16)
    kpe1 = sbt("kpe1", (128, S), BF16)
    kpe2 = sbt("kpe2", (128, S), BF16)
    T4.append(at("t4b", (128, 2048), BF16, nc.lookup_mloc(kpe1).addr, 0))
    T4.append(at("t4c", (128, 2048), BF16, nc.lookup_mloc(kpe2).addr, 0))
    TERNB = at("ternb", (128, 6 * 2048), BF16, nc.lookup_mloc(Vt).addr, 0)
    TERNF = [at(f"ternf{i}", (128, 2048), F32, a1, i * 8192) for i in range(4)]
    hq = sbt("hq", (128, NCH * QCW), BF16)
    qn = sbt("qn", (128, QCW), BF16)
    Pb = sbt("Pb", (128, 2 * QCW), BF16)
    Ob = sbt("Ob", (128, 4 * QCW), BF16)
    gbuf = Vt
    xh = hq
    ckvn = SCB

    ctxM = contextlib.ExitStack()
    PSM = [ctxM.enter_context(nc.psum_tensor(f"psm{i}", [128, 512], F32))
           for i in range(3)]
    PSB = [ctxM.enter_context(nc.psum_tensor(f"psb{i}", [128, 1024], BF16))
           for i in range(2)]

    p = Plan(nc)
    p.t4_tok = [0, 0, 0]
    p.gstat_tok = 0
    for s_ in ("bar", "dS", "dG", "dC", "pe", "ve", "se", "ge"):
        p.sem(s_)

    def MM(ps_ap, lhsT, rhs, first, last=True, inc=None, tp=None):
        p.emit("tensor",
               lambda e, a=ps_ap, l=lhsT, r=rhs, f=first, q=last, t=tp:
               e.matmul(a, l, r, start=f, stop=q, tile_position=t), inc=inc)

    def TR(ps_ap, in_ap, inc=None):
        p.emit("tensor",
               lambda e, a=ps_ap, i=in_ap: e.transpose(a, i, id128[:, :]), inc=inc)

    def DMA(eng, dst, src, sem="dS", slow=False):
        p.emit(eng, lambda e, d=dst, s=src, sl=slow: e.dma_start(
            out=d, in_=s, allow_slow_non_contiguous=sl), inc=sem, amt=16)

    def DMAT(eng, dst, src, sem="dS"):
        p.emit(eng, lambda e, d=dst, s=src: e.dma_start_transpose(out=d, in_=s),
               inc=sem, amt=16)

    def DMAACC(dst, src):
        p.emit("gpsimd", lambda e, d=dst, s=src:
               e.dma_start(out=d, in_=s, accum_op=ALU.add), inc="dG", amt=16)

    def COPY(eng, dst, src, inc=None):
        p.emit(eng, lambda e, d=dst, s=src: e.tensor_copy(d, s), inc=inc)

    def TT(eng, dst, a, b, op, inc=None):
        p.emit(eng, lambda e, d=dst, x=a, y=b, o=op: e.tensor_tensor(d, x, y, o),
               inc=inc)

    def TS(eng, dst, a, s1, s2, op0, op1=None, inc=None):
        if op1 is None:
            p.emit(eng, lambda e, d=dst, x=a, u=s1, o0=op0:
                   e.tensor_scalar(d, x, u, None, o0), inc=inc)
        else:
            p.emit(eng, lambda e, d=dst, x=a, u=s1, v=s2, o0=op0, o1=op1:
                   e.tensor_scalar(d, x, u, v, o0, o1), inc=inc)

    def STT(eng, dst, in0, scalar, in1, op0, op1, inc=None):
        p.emit(eng, lambda e, d=dst, a=in0, s=scalar, b=in1, o0=op0, o1=op1:
               e.scalar_tensor_tensor(d, a, s, b, o0, o1), inc=inc)

    def ACT(dst, src, func, bias=None, scale=None, accum=None, inc=None):
        kw = {}
        if bias is not None:
            kw["bias"] = bias
        if scale is not None:
            kw["scale"] = scale
        if accum is not None:
            kw["accum_out"] = accum
        p.emit("scalar", lambda e, d=dst, s=src, f=func, k=kw:
               e.activation(d, s, f, **k), inc=inc)

    def SCOPY2(dst, src, inc=None):
        p.emit("scalar", lambda e, d=dst, s=src: e.copy(d, s), inc=inc)

    def SIGN(dst, src, bias, inc=None):
        p.emit("scalar", lambda e, d=dst, s=src, b=bias: e.sign(d, s, bias=b), inc=inc)

    p.emit("gpsimd", lambda e: e.memset(ones32[:, :], 1.0))
    p.emit("gpsimd", lambda e: e.memset(id128[:, :], 0.0))
    p.emit("gpsimd", lambda e: e.affine_select(
        out=id128[:, :], in_=id128[:, :], pattern=[[-1, 128]],
        compare_op=ALU.not_equal, fill=1.0, base=0, channel_multiplier=1))
    p.emit("gpsimd", lambda e: e.memset(id128b[:, :], 0.0))
    p.emit("gpsimd", lambda e: e.affine_select(
        out=id128b[:, :], in_=id128b[:, :], pattern=[[-1, 128]],
        compare_op=ALU.not_equal, fill=1.0, base=0, channel_multiplier=1))
    p.emit("gpsimd", lambda e: e.memset(smal[:, 0:1], EXPB))
    p.emit("gpsimd", lambda e: e.memset(smal[:, 1:2], EPS_LN))
    p.emit("gpsimd", lambda e: e.memset(smal[:, 2:3], EPS_ATT))
    p.emit("gpsimd", lambda e: e.memset(smal[:, 3:4], EPS_FFN), inc="ge")

    DMA("sync", kvsc[:, 0:4], bass.AP(kvs_in.tensor, 0, [[1, 128], [128, 4]]),
        slow=True)
    for s_ in range(NSEQ):
        DMA("sync", bass.AP(POOL[4], s_, [[512, 128], [2, 16]]),
            bass.AP(c_in.tensor, s_ * E, [[1, 128], [128, 16]]), slow=True)

    def phase_M():
        p.wait("scalar", "dS")
        ACT(POOL[4][:, 32:64], POOL[4][:, 0:32], AF.Silu, inc="se")
        scv = POOL[4]

        strf_tok = [0, 0]
        first = True
        p.wait("tensor", "se")
        for ec in range(NCH):
            for cb in range(8):
                slot = (ec * 8 + cb) % 2
                p.wait("sync", "pe", strf_tok[slot])
                DMA("sync", STRF[slot][:, 0:1536],
                    wmod_in[ec * 128:(ec + 1) * 128, cb * 1536:(cb + 1) * 1536])
                p.wait("tensor", "dS")
                for t in range(12):
                    cc = cb * 12 + t
                    MM(PSM[0][:, 2 * cc:2 * cc + 2],
                       STRF[slot][:, t * 128:(t + 1) * 128],
                       scv[:, 32 + 2 * ec:34 + 2 * ec],
                       first=first, last=(ec == NCH - 1),
                       inc=("pe" if t == 11 else None))
                    first = False
                strf_tok[slot] = p.val("pe")
        p.wait("vector", "pe")
        COPY("vector", modc[:, 0:192], PSM[0][:, 0:192], inc="ve")

        for v in (2, 5):
            cols = modc[:, v * 32:(v + 1) * 32]
            TS("vector", POOL[0][:, 0:32], cols, 0.25, None, ALU.max)
            TS("vector", POOL[1][:, 0:32], cols, -0.25, None, ALU.min)
            TS("vector", POOL[2][:, 0:32], cols, 0.0, None, ALU.is_ge)
            TT("vector", POOL[3][:, 0:32], POOL[0][:, 0:32], POOL[1][:, 0:32],
               ALU.subtract)
            TT("vector", POOL[3][:, 0:32], POOL[2][:, 0:32], POOL[3][:, 0:32], ALU.mult)
            TT("vector", cols, POOL[1][:, 0:32], POOL[3][:, 0:32], ALU.add)
        for v in (1, 4):
            cols = modc[:, v * 32:(v + 1) * 32]
            TS("vector", cols, cols, 1.0, None, ALU.add, inc="ve")

        def gamma_pass(w_in, nslabs, slab_cols, ps_bank, gcols, eps):
            tok = [p.val("se")] * 4
            dtok = [0, 0, 0, 0]
            mm_tok = [0, 0]
            firstg = True
            STG2 = [SCB, STRM]
            p.wait("scalar", "pe")
            p.wait("scalar", "ve")
            for i in range(nslabs):
                slot = i % 4
                p.wait("sync", "se", tok[slot])
                r0 = (i * 2048) // slab_cols
                c0 = (i * 2048) % slab_cols
                DMA("sync", TERNF[slot][:, 0:2048],
                    w_in[r0 * 128:(r0 + 1) * 128, c0:c0 + 2048])
                dtok[slot] = p.val("dS")
                sg = i % 2
                p.wait("scalar", "dS", dtok[slot])
                p.wait("scalar", "pe", mm_tok[sg])
                ACT(STG2[sg][:, 0:2048], TERNF[slot][:, 0:2048], AF.Abs, inc="se")
                tok[slot] = p.val("se")
                p.wait("tensor", "se")
                for t in range(4):
                    MM(ps_bank[0:32, :], ones32[:, :],
                       STG2[sg][:, t * 512:(t + 1) * 512],
                       first=firstg, last=(i == nslabs - 1 and t == 3),
                       inc=("pe" if t == 3 else None))
                    firstg = False
                mm_tok[sg] = p.val("pe")
            p.wait("vector", "pe")
            p.emit("vector", lambda e, pb=ps_bank: e.reduce_sum(
                POOL[0][0:1, 0:1], pb[0:1, :], AX.X), inc="ve")
            n_el = nslabs * 2048 * 128
            p.wait("gpsimd", "ve")
            TS("gpsimd", POOL[0][0:1, 1:2], POOL[0][0:1, 0:1], 1.0 / n_el, None, ALU.mult)
            TS("gpsimd", POOL[0][0:1, 2:3], POOL[0][0:1, 1:2], eps, 0.5, ALU.add, ALU.mult)
            TS("gpsimd", POOL[0][0:1, 3:4], POOL[0][0:1, 2:3], -1.0, None, ALU.mult)
            TS("gpsimd", POOL[0][0:1, 4:5], POOL[0][0:1, 1:2], 0.5, None, ALU.mult)
            p.emit("gpsimd", lambda e, g=gcols[0]: e.partition_broadcast(
                gam[:, g:g + 1], POOL[0][0:1, 3:4]))
            p.emit("gpsimd", lambda e, g=gcols[1]: e.partition_broadcast(
                gam[:, g:g + 1], POOL[0][0:1, 2:3]))
            p.emit("gpsimd", lambda e, g=gcols[2]: e.partition_broadcast(
                gam[:, g:g + 1], POOL[0][0:1, 4:5]), inc="ge")
            p.wait("vector", "ge")

        if not nogamma:
            gamma_pass(wfc_in, 64, E, PSM[1], (0, 1, 2), EPS_FFN)
            gamma_pass(wpj_in, 64, DFF, PSM[2], (3, 4, 5), EPS_FFN)

        p.wait("vector", "ge")
        p.emit("vector", lambda e: e.tensor_scalar(
            modc[:, 192:224], modc[:, 160:192], gam[:, 5:6], None, ALU.mult), inc="ve")

        p.wait("vector", "pe")
        p.wait("vector", "ge")
        p.wait("sync", "ge")
        p.wait("sync", "se")
        STG = [STRM, SCB]
        strf_tok = [0, 0, 0, 0]
        strf_dma = [0, 0, 0, 0]
        stg_out = [0, 0]

        def cast_unit(i_u, in_src, n_in, copies_fn, outs_fn):
            slot = i_u % 4
            sg = i_u % 2
            p.wait("sync", "ve", strf_tok[slot])
            DMA("sync", TERNF[slot][:, 0:n_in], in_src)
            strf_dma[slot] = p.val("dS")
            p.wait("vector", "dS", max(strf_dma[slot], stg_out[sg]))
            copies_fn(TERNF[slot], STG[sg])
            strf_tok[slot] = p.val("ve")
            p.wait("sync", "ve", strf_tok[slot])
            outs_fn(STG[sg])
            stg_out[sg] = p.val("dS")

        i_u = 0
        for c in range(NCH):
            for half in range(2):
                def cps(sf, sg):
                    COPY("vector", sg[:, 0:1024].rearrange("p (h d) -> p h d", d=128),
                         bass.AP(sf, 0, [[2048, 128], [192, 8], [1, 128]]))
                    COPY("vector", sg[:, 1024:1280].rearrange("p (h f) -> p h f", f=32),
                         bass.AP(sf, 128, [[2048, 128], [192, 8], [1, 32]]))
                    COPY("vector", sg[:, 1280:1536].rearrange("p (h f) -> p h f", f=32),
                         bass.AP(sf, 160, [[2048, 128], [192, 8], [1, 32]]), inc="ve")

                def outs(sg, c=c, half=half):
                    h0 = half * 8
                    DMA("sync",
                        bass.AP(wqbf_scr.tensor, (h0 * NCH + c) * 16384,
                                [[128, 128], [NCH * 16384, 8], [1, 128]]),
                        sg[:, 0:1024].rearrange("p (h d) -> p h d", d=128))
                    DMA("sync",
                        bass.AP(wqbf_scr.tensor, ((16 + 2 * half) * NCH + c) * 16384,
                                [[128, 128], [NCH * 16384, 2], [1, 128]]),
                        sg[:, 1024:1280].rearrange("p (j d) -> p j d", d=128))
                    DMA("sync",
                        bass.AP(wqbf_scr.tensor, ((20 + 2 * half) * NCH + c) * 16384,
                                [[128, 128], [NCH * 16384, 2], [1, 128]]),
                        sg[:, 1280:1536].rearrange("p (j d) -> p j d", d=128))
                cast_unit(i_u, wq_in[c * 128:(c + 1) * 128,
                                    half * 1536:(half + 1) * 1536], 1536, cps, outs)
                i_u += 1

        for hv in range(NCH):
            def cps(sf, sg):
                COPY("vector", sg[:, 0:2048], sf[:, 0:2048], inc="ve")

            def outs(sg, hv=hv):
                DMA("sync",
                    bass.AP(wobf_scr.tensor, hv * 16384,
                            [[128, 128], [NCH * 16384, 16], [1, 128]]),
                    sg[:, 0:2048].rearrange("p (o d) -> p o d", d=128))
            cast_unit(i_u, wo_in[hv * 128:(hv + 1) * 128, :], 2048, cps, outs)
            i_u += 1

        for lc in range(4):
            for piece in range(2):
                def cps(sf, sg):
                    COPY("vector", sg[:, 0:1024].rearrange("p (h d) -> p h d", d=128),
                         bass.AP(sf, 0, [[2048, 128], [256, 8], [1, 128]]))
                    COPY("vector", sg[:, 1024:2048].rearrange("p (h d) -> p h d", d=128),
                         bass.AP(sf, 128, [[2048, 128], [256, 8], [1, 128]]), inc="ve")

                def outs(sg, lc=lc, piece=piece):
                    DMA("sync", wkvbf_scr[lc * 128:(lc + 1) * 128,
                                          piece * 1024:piece * 1024 + 1024],
                        sg[:, 0:1024])
                    DMA("sync", wkvbf_scr[lc * 128:(lc + 1) * 128,
                                          2048 + piece * 1024:2048 + piece * 1024 + 1024],
                        sg[:, 1024:2048])
                cast_unit(i_u, wkvb_in[lc * 128:(lc + 1) * 128,
                                       piece * 2048:(piece + 1) * 2048], 2048, cps, outs)
                i_u += 1

        for ec in range(NCH):
            def cps(sf, sg):
                COPY("vector", sg[:, 0:512], sf[:, 0:512])
                for r in range(4):
                    COPY("vector", sg[:, 512 + r * 32:544 + r * 32], sf[:, 512:544])
                    COPY("vector", sg[:, 640 + r * 32:672 + r * 32], sf[:, 544:576],
                         inc=("ve" if r == 3 else None))

            def outs(sg, ec=ec):
                DMA("sync", wkvabf_scr[ec * 128:(ec + 1) * 128, :], sg[:, 0:768])
            cast_unit(i_u, wkva_in[ec * 128:(ec + 1) * 128, :], 576, cps, outs)
            i_u += 1


        def tern_pass(w_in, rows_chunks, col_pieces, gneg_col, gpos_col, unit_fn):
            slab_tok = [p.val("ve")] * 4
            slab_dma = [0, 0, 0, 0]
            t2_tok = [p.val("pe")] * 2
            psb_tok = [p.val("se")] * 2
            pb_dma_tok = [p.val("dS")] * 2
            PbW = [STRM[:, 0:1024], STRM[:, 1024:2048]]
            units = [(rc, cp) for rc in range(rows_chunks) for cp in range(col_pieces)]
            n_u = len(units)
            for i_s in range(n_u):
                rc, cp = units[i_s]
                slot = i_s % 4
                p.wait("sync", "ve", slab_tok[slot])
                DMA("sync", TERNF[slot][:, 0:2048],
                    w_in[rc * 128:(rc + 1) * 128, cp * 2048:(cp + 1) * 2048])
                slab_dma[slot] = p.val("dS")
                t2s = i_s % 2
                p.wait("vector", "dS", slab_dma[slot])
                p.wait("vector", "pe", t2_tok[t2s])
                TS("vector", TERNB[:, 4096 + t2s * 2048:4096 + (t2s + 1) * 2048],
                   TERNF[slot][:, 0:2048], gam[:, gpos_col:gpos_col + 1], 2.0,
                   ALU.is_ge, ALU.mult)
                TS("vector", TERNB[:, 8192 + t2s * 2048:8192 + (t2s + 1) * 2048],
                   TERNF[slot][:, 0:2048], gam[:, gneg_col:gneg_col + 1], 2.0,
                   ALU.is_ge, ALU.mult, inc="ve")
                slab_tok[slot] = p.val("ve")
                STT("vector", TERNB[:, t2s * 2048:(t2s + 1) * 2048],
                    TERNB[:, 4096 + t2s * 2048:4096 + (t2s + 1) * 2048], -2.0,
                    TERNB[:, 8192 + t2s * 2048:8192 + (t2s + 1) * 2048],
                    ALU.add, ALU.add, inc="ve")
                p.wait("tensor", "ve")
                for h in range(2):
                    p.wait("tensor", "se", psb_tok[h])
                    for j in range(8):
                        p.emit("tensor", lambda e, h_=h, j_=j, t_=t2s:
                               e.transpose(PSB[h_][:, j_ * 128:(j_ + 1) * 128],
                                           TERNB[:, t_ * 2048 + (h_ * 8 + j_) * 128:
                                                 t_ * 2048 + (h_ * 8 + j_ + 1) * 128],
                                           id128b[:, :]),
                               inc=("pe" if j == 7 else None))
                    p.wait("scalar", "pe")
                    p.wait("scalar", "dS", pb_dma_tok[h])
                    SCOPY2(PbW[h][:, :], PSB[h][:, :], inc="se")
                    psb_tok[h] = p.val("se")
                    p.wait("sync", "se")
                    for gg in (2 * h, 2 * h + 1):
                        DMA("sync", unit_fn(rc, cp, gg),
                            PbW[h][:, (gg % 2) * 512:(gg % 2) * 512 + 512].rearrange(
                                "p (j q) -> p j q", q=128))
                    pb_dma_tok[h] = p.val("dS")
                t2_tok[t2s] = p.val("pe")

        def fc_unit(rc, cp, g):
            return bass.AP(tfc_scr.tensor,
                           ((rc // 4) * 4 + g) * 262144 + (rc % 4) * 16384,
                           [[128, 128], [65536, 4], [1, 128]])

        def pj_unit(rc, cp, g):
            return bass.AP(tpj_scr.tensor,
                           ((rc // 4) * 16 + cp * 4 + g) * 262144 + (rc % 4) * 16384,
                           [[128, 128], [65536, 4], [1, 128]])

        tern_pass(wfc_in, NOC, 1, 0, 1, fc_unit)
        tern_pass(wpj_in, NCH, 4, 3, 4, pj_unit)

        if "d_mod" in dumps:
            p.wait("sync", "ve")
            DMA("sync", dumps["d_mod"], modc[:, 0:224])
        if "d_gam" in dumps:
            p.wait("sync", "ge")
            DMA("sync", dumps["d_gam"], gam[:, 0:8])
        if "d_wscr" in dumps:
            p.wait("sync", "dS")
            DMA("sync", dumps["d_wscr"], wkvbf_scr[:, :])
        if "d_wqscr" in dumps:
            p.wait("sync", "dS")
            DMA("sync", dumps["d_wqscr"], wqbf_scr[:, :])

    def attn_qc(s, qc, do_dump):
        q0 = qc * QCW
        p.wait("sync", "ve")
        DMA("sync", cosq[:, 0:512], cos_in[:, q0:q0 + QCW])
        DMA("sync", cosq[:, 512:1024], sin_in[:, q0:q0 + QCW])

        stat_tok = [0, 0]
        stat_ve_tok = 0
        p.wait("scalar", "pe")
        for rt in range(4):
            slot = rt % 2
            if rt < 2:
                p.wait("sync", "se")
                p.wait("sync", "ve")
                p.wait("sync", "pe")
            else:
                p.wait("sync", "se", stat_tok[slot])
                p.wait("sync", "ve", stat_ve_tok)
            DMA("sync", STRF[slot][:, 0:2048],
                x_in[s, q0 + rt * 128:q0 + (rt + 1) * 128, :])
            p.wait("scalar", "dS")
            ACT(SCB[:, 0:2048], STRF[slot][:, 0:2048], AF.Square,
                accum=POOL[4][:, rt:rt + 1], inc="se")
            p.wait("vector", "dS")
            p.emit("vector", lambda e, sl=slot, r=rt: e.reduce_sum(
                POOL[4][:, 4 + r:5 + r], STRF[sl][:, 0:2048], AX.X), inc="ve")
            stat_tok[slot] = p.val("se")
            stat_ve_tok = p.val("ve")
        p.wait("tensor", "se")
        p.wait("tensor", "ve")
        for rt in range(4):
            TR(PS[4][0:1, rt * 128:(rt + 1) * 128], POOL[4][:, 4 + rt:5 + rt])
            TR(PS[5][0:1, rt * 128:(rt + 1) * 128], POOL[4][:, rt:rt + 1],
               inc=("pe" if rt == 3 else None))
        p.wait("vector", "pe")
        TS("vector", POOL[0][0:1, :], PS[4][0:1, :], 1.0 / E, None, ALU.mult)
        TT("vector", POOL[1][0:1, :], POOL[0][0:1, :], POOL[0][0:1, :], ALU.mult)
        STT("vector", POOL[1][0:1, :], PS[5][0:1, :], 1.0 / E, POOL[1][0:1, :],
            ALU.mult, ALU.subtract, inc="ve")
        p.wait("scalar", "ve")
        ACT(POOL[1][0:1, :], POOL[1][0:1, :], AF.Ln, bias=smal[0:1, 1:2])
        ACT(POOL[1][0:1, :], POOL[1][0:1, :], AF.Exp, scale=-0.5, inc="se")
        p.wait("gpsimd", "ve")
        p.wait("gpsimd", "se")
        p.emit("gpsimd", lambda e: e.partition_broadcast(POOL[2][:, :], POOL[0][0:1, :]))
        p.emit("gpsimd", lambda e: e.partition_broadcast(POOL[3][:, :], POOL[1][0:1, :]),
               inc="ge")

        ap_tok = [0, 0]
        xts_tok = [0, 0, 0]
        xts_dma_tok = [0, 0, 0]
        XTS3 = True
        p.wait("tensor", "ge")
        p.wait("vector", "ge")
        for rt in range(4):
            slot = rt % 2
            p.wait("sync", "pe", ap_tok[slot])
            DMA("sync", STRF[slot][:, 0:2048],
                x_in[s, q0 + rt * 128:q0 + (rt + 1) * 128, :])
            p.wait("tensor", "dS")
            for c in range(NCH):
                bank = PS[6 + (c % 2)]
                xsl = (rt * NCH + c) % 2
                p.wait("tensor", "ve", xts_tok[xsl])
                TR(bank[:, 0:128], STRF[slot][:, c * 128:(c + 1) * 128], inc="pe")
                p.wait("vector", "pe")
                p.wait("vector", "dS", xts_dma_tok[xsl])
                xts = POOL[4][:, xsl * 128:xsl * 128 + 128]
                COPY("vector", xts, bank[:, 0:128], inc="ve")
                p.wait("sync", "ve")
                DMA("sync",
                    bass.AP(x1_scr.tensor, s * E * S + (c * 128) * S + q0 + rt * 128,
                            [[S, 128], [1, 128]]),
                    xts)
                xts_dma_tok[xsl] = p.val("dS")
                TT("vector", POOL[4][:, 384:512], xts,
                   POOL[2][:, rt * 128:(rt + 1) * 128], ALU.subtract)
                TT("vector", POOL[4][:, 384:512], POOL[4][:, 384:512],
                   POOL[3][:, rt * 128:(rt + 1) * 128], ALU.mult)
                p.emit("vector", lambda e, c_=c, r_=rt: e.tensor_scalar(
                    hq[:, c_ * 512 + r_ * 128:c_ * 512 + (r_ + 1) * 128],
                    POOL[4][:, 384:512],
                    modc[:, mcol(1, c_, s):mcol(1, c_, s) + 1],
                    modc[:, mcol(0, c_, s):mcol(0, c_, s) + 1],
                    ALU.mult, ALU.add), inc="ve")
                xts_tok[xsl] = p.val("ve")
            ap_tok[slot] = p.val("pe")
        hq_ready = p.val("ve")
        xw_tok = p.val("dS")

        if do_dump and qc == 0 and "d_h" in dumps:
            p.wait("sync", "ve")
            DMA("sync", dumps["d_h"], hq[:, :])

        kva_tok = 0
        for ec in range(NCH):
            p.wait("sync", "pe", kva_tok)
            DMA("sync", STRM[:, 0:768], wkvabf_scr[ec * 128:(ec + 1) * 128, :])
            p.wait("tensor", "dS")
            if ec == 0:
                p.wait("tensor", "ve", hq_ready)
            rhs = hq[:, ec * 512:(ec + 1) * 512]
            for t in range(4):
                MM(PS[t][:, :], STRM[:, t * 128:(t + 1) * 128], rhs,
                   first=(ec == 0), last=(ec == NCH - 1))
            MM(PS[4][:, :], STRM[:, 512:640], rhs,
               first=(ec == 0), last=(ec == NCH - 1))
            MM(PS[5][:, :], STRM[:, 640:768], rhs,
               first=(ec == 0), last=(ec == NCH - 1), inc="pe")
            kva_tok = p.val("pe")

        p.wait("vector", "pe")
        TT("vector", POOL[0][:, :], PS[4][:, :], cosq[:, 0:512], ALU.mult)
        TT("vector", POOL[1][:, :], PS[5][:, :], cosq[:, 512:1024], ALU.mult)
        TT("vector", kpe1[:, q0:q0 + 512], POOL[0][:, :], POOL[1][:, :], ALU.subtract)
        TT("vector", POOL[0][:, :], PS[5][:, :], cosq[:, 0:512], ALU.mult)
        TT("vector", POOL[1][:, :], PS[4][:, :], cosq[:, 512:1024], ALU.mult)
        TT("vector", kpe2[:, q0:q0 + 512], POOL[0][:, :], POOL[1][:, :], ALU.add,
           inc="ve")

        p.wait("scalar", "ve")
        for i in range(4):
            ACT(SCB[:, i * 512:(i + 1) * 512], PS[i][:, :], AF.Square,
                inc=("se" if i == 3 else None))
        p.wait("tensor", "se")
        for i in range(4):
            MM(PS[6][0:32, :], ones32[:, :], SCB[:, i * 512:(i + 1) * 512],
               first=(i == 0), last=(i == 3), inc=("pe" if i == 3 else None))
        p.wait("vector", "pe")
        TS("vector", POOL[4][0:1, :], PS[6][0:1, :], 1.0 / LORA, None, ALU.mult,
           inc="ve")
        p.wait("scalar", "ve")
        ACT(POOL[4][0:1, :], POOL[4][0:1, :], AF.Ln, bias=smal[0:1, 2:3])
        ACT(POOL[4][0:1, :], POOL[4][0:1, :], AF.Exp, scale=-0.5, inc="se")
        p.wait("gpsimd", "se")
        p.emit("gpsimd", lambda e: e.partition_broadcast(POOL[2][:, :], POOL[4][0:1, :]),
               inc="ge")
        p.wait("vector", "ge")
        for i in range(4):
            TT("vector", POOL[3][:, :], PS[i][:, :], POOL[2][:, :], ALU.mult)
            p.emit("vector", lambda e, i_=i: e.tensor_scalar(
                ckvn[:, i_ * 512:(i_ + 1) * 512], POOL[3][:, :],
                kvsc[:, i_:i_ + 1], None, ALU.mult), inc="ve")
        ckv_ready = p.val("ve")

        if do_dump and qc == 0 and "d_ckv" in dumps:
            p.wait("sync", "ve")
            DMA("sync", dumps["d_ckv"], SCB[:, 0:2048])

        kb_tok = 0
        for hg in range(4):
            p.wait("sync", "pe", kb_tok)
            p.wait("sync", "ve")
            for lc in range(4):
                DMA("sync", STRM[:, lc * 512:(lc + 1) * 512],
                    wkvbf_scr[lc * 128:(lc + 1) * 128, hg * 512:(hg + 1) * 512])
            p.wait("tensor", "dS")
            if hg == 0:
                p.wait("tensor", "ve", ckv_ready)
            for hh in range(4):
                for lc in range(4):
                    MM(PS[4 + hh][:, :],
                       STRM[:, lc * 512 + hh * 128:lc * 512 + (hh + 1) * 128],
                       ckvn[:, lc * 512:(lc + 1) * 512],
                       first=(lc == 0), last=(lc == 3),
                       inc=("pe" if (hh == 3 and lc == 3) else None))
            kb_tok = p.val("pe")
            p.wait("vector", "pe")
            for hh in range(4):
                h = hg * 4 + hh
                COPY("vector", Kn[:, h * S + q0:h * S + q0 + 512], PS[4 + hh][:, :],
                     inc=("ve" if hh == 3 else None))

        v_tok = kb_tok
        for vg in range(4):
            p.wait("sync", "pe", v_tok)
            p.wait("sync", "ve")
            for lc in range(4):
                DMA("sync", STRM[:, lc * 512:(lc + 1) * 512],
                    wkvbf_scr[lc * 128:(lc + 1) * 128,
                              2048 + vg * 512:2048 + (vg + 1) * 512])
            p.wait("tensor", "dS")
            p.wait("tensor", "ve")
            for j in range(4):
                for lc in range(4):
                    MM(PS[4 + j][:, :],
                       ckvn[:, lc * 512 + j * 128:lc * 512 + (j + 1) * 128],
                       STRM[:, lc * 512:(lc + 1) * 512],
                       first=(lc == 0), last=(lc == 3),
                       inc=("pe" if (j == 3 and lc == 3) else None))
            v_tok = p.val("pe")
            p.wait("vector", "pe")
            for j in range(4):
                kb = qc * 4 + j
                COPY("vector", Vt[:, kb * 2048 + vg * 512:kb * 2048 + (vg + 1) * 512],
                     PS[4 + j][:, :], inc=("ve" if j == 3 else None))

        for j in range(4):
            p.wait("sync", "pe")
            p.wait("sync", "ve")
            DMA("sync", STRM[:, 0:2048],
                bass.AP(wqbf_scr.tensor, (16 + j) * NCH * 16384,
                        [[128, 128], [16384, 16], [1, 128]]))
            p.wait("tensor", "dS")
            p.wait("tensor", "ve")
            for c in range(NCH):
                MM(PS[0][:, :], STRM[:, c * 128:(c + 1) * 128],
                   hq[:, c * 512:(c + 1) * 512],
                   first=(c == 0), last=(c == NCH - 1),
                   inc=("pe" if c == NCH - 1 else None))
            p.wait("sync", "pe")
            DMA("sync", STRM[:, 0:2048],
                bass.AP(wqbf_scr.tensor, (20 + j) * NCH * 16384,
                        [[128, 128], [16384, 16], [1, 128]]))
            p.wait("tensor", "dS")
            for c in range(NCH):
                MM(PS[1][:, :], STRM[:, c * 128:(c + 1) * 128],
                   hq[:, c * 512:(c + 1) * 512],
                   first=(c == 0), last=(c == NCH - 1),
                   inc=("pe" if c == NCH - 1 else None))
            p.wait("vector", "pe")
            TT("vector", POOL[0][:, :], PS[0][:, :], cosq[:, 0:512], ALU.mult)
            TT("vector", POOL[1][:, :], PS[1][:, :], cosq[:, 512:1024], ALU.mult)
            TT("vector", qpe[:, j * 512:(j + 1) * 512], POOL[0][:, :], POOL[1][:, :],
               ALU.subtract)
            TT("vector", POOL[0][:, :], PS[1][:, :], cosq[:, 0:512], ALU.mult)
            TT("vector", POOL[1][:, :], PS[0][:, :], cosq[:, 512:1024], ALU.mult)
            TT("vector", qpe[:, (4 + j) * 512:(5 + j) * 512], POOL[0][:, :],
               POOL[1][:, :], ALU.add, inc="ve")

        if do_dump and qc == 0 and "d_qpe" in dumps:
            p.wait("sync", "ve")
            DMA("sync", dumps["d_qpe"], qpe[:, :])

        nkb = 4 * (qc + 1)
        qn_strm_tok = 0
        ps2_tok = 0
        qnbuf_tok = 0
        zo_tok = 0
        ob_free_tok = 0
        wo_strm_tok = [0, 0]
        acc_pool_tok = [0, 0]
        prev_acc_round = 0
        for h in range(16):
            j4 = h % 4
            jg = h // 4
            p.wait("sync", "pe", qn_strm_tok)
            DMA("sync", STRM[:, 0:2048],
                bass.AP(wqbf_scr.tensor, h * NCH * 16384,
                        [[128, 128], [16384, 16], [1, 128]]))
            p.wait("tensor", "dS")
            p.wait("tensor", "ve", ps2_tok)
            for c in range(NCH):
                MM(PS[2][:, :], STRM[:, c * 128:(c + 1) * 128],
                   hq[:, c * 512:(c + 1) * 512],
                   first=(c == 0), last=(c == NCH - 1),
                   inc=("pe" if c == NCH - 1 else None))
            qn_strm_tok = p.val("pe")
            p.wait("vector", "pe")
            p.wait("vector", "pe", qnbuf_tok)
            COPY("vector", qn[:, 0:512], PS[2][:, :], inc="ve")
            ps2_tok = p.val("ve")
            qn_ready = p.val("ve")

            sexp_tok = [0, 0]
            zpv_tok = [0, 0]
            for step in range(nkb + 1):
                if step < nkb:
                    kb = step
                    sbank = kb % 2
                    pslot = kb % 2
                    p.wait("tensor", "se", sexp_tok[sbank])
                    if kb == 0:
                        p.wait("tensor", "ve", qn_ready)
                        if h > 0 and j4 == 0:
                            p.wait("tensor", "pe", ob_free_tok)
                    tp96 = (96, 0) if j4 == 3 else None
                    MM(PS[sbank][:, :], Kn[:, h * S + kb * 128:h * S + (kb + 1) * 128],
                       qn[:, 0:512], first=True, last=False)
                    MM(PS[sbank][:, :],
                       kpe1[j4 * 32:(j4 + 1) * 32, kb * 128:(kb + 1) * 128],
                       qpe[j4 * 32:(j4 + 1) * 32, jg * 512:(jg + 1) * 512],
                       first=False, last=False, tp=tp96)
                    MM(PS[sbank][:, :],
                       kpe2[j4 * 32:(j4 + 1) * 32, kb * 128:(kb + 1) * 128],
                       qpe[j4 * 32:(j4 + 1) * 32, (4 + jg) * 512:(5 + jg) * 512],
                       first=False, last=True, inc="pe", tp=tp96)
                    score_tok = p.val("pe")
                    p.wait("scalar", "pe", score_tok)
                    if kb >= 2:
                        p.wait("scalar", "pe", zpv_tok[pslot])
                    ACT(Pb[:, pslot * 512:(pslot + 1) * 512], PS[sbank][:, :], AF.Exp,
                        bias=smal[:, 0:1], scale=SCALE, inc="se")
                    sexp_tok[sbank] = p.val("se")
                    if kb >= qc * 4:
                        p.wait("gpsimd", "se", sexp_tok[sbank])
                        p.emit("gpsimd", lambda e, ps_=pslot, b_=q0 - kb * 128:
                               e.affine_select(
                                   out=Pb[:, ps_ * 512:(ps_ + 1) * 512],
                                   in_=Pb[:, ps_ * 512:(ps_ + 1) * 512],
                                   pattern=[[1, 512]], compare_op=ALU.is_ge,
                                   fill=0.0, base=b_, channel_multiplier=-1),
                               inc="ge")
                if step >= 1:
                    kb = step - 1
                    pslot = kb % 2
                    if kb >= qc * 4:
                        p.wait("tensor", "ge")
                    else:
                        p.wait("tensor", "se", sexp_tok[kb % 2])
                    if kb == 0:
                        p.wait("tensor", "ve", zo_tok)
                    MM(PS[6][0:32, :], ones32[:, :],
                       Pb[:, pslot * 512:(pslot + 1) * 512],
                       first=(kb == 0), last=(kb == nkb - 1))
                    MM(PS[7][:, :],
                       Vt[:, kb * 2048 + h * 128:kb * 2048 + (h + 1) * 128],
                       Pb[:, pslot * 512:(pslot + 1) * 512],
                       first=(kb == 0), last=(kb == nkb - 1), inc="pe")
                    zpv_tok[pslot] = p.val("pe")
            qnbuf_tok = p.val("pe")
            p.wait("vector", "pe")
            p.emit("vector", lambda e: e.reciprocal(POOL[0][0:1, :], PS[6][0:1, :]),
                   inc="ve")
            p.wait("gpsimd", "ve")
            p.emit("gpsimd", lambda e: e.partition_broadcast(POOL[1][:, :],
                                                             POOL[0][0:1, :]), inc="ge")
            p.wait("vector", "ge")
            if h >= 4:
                p.wait("vector", "pe", ob_free_tok)
            TT("vector", Ob[:, j4 * 512:(j4 + 1) * 512], PS[7][:, :],
               POOL[1][:, :], ALU.mult, inc="ve")
            zo_tok = p.val("ve")

            if j4 == 3:
                quarter = h // 4
                if do_dump and qc == 0 and quarter == 0 and "d_ob" in dumps:
                    p.wait("sync", "ve")
                    DMA("sync", dumps["d_ob"], Ob[:, :])
                if quarter > 0:
                    p.wait("gpsimd", "dG", prev_acc_round)
                p.wait("gpsimd", "dS", xw_tok)
                ob_done = p.val("ve")
                wo_ts_tok = [0, 0]
                for oc in range(NCH):
                    slot2 = oc % 2
                    p.wait("sync", "pe", wo_strm_tok[slot2])
                    DMA("sync", STRM[:, slot2 * 1024:slot2 * 1024 + 512],
                        bass.AP(wobf_scr.tensor, (oc * NCH + quarter * 4) * 16384,
                                [[128, 128], [16384, 4], [1, 128]]))
                    p.wait("tensor", "dS")
                    if oc == 0:
                        p.wait("tensor", "ve", ob_done)
                    p.wait("tensor", "ve", wo_ts_tok[slot2])
                    for t in range(4):
                        MM(PS[2 + slot2][:, :],
                           STRM[:, slot2 * 1024 + t * 128:slot2 * 1024 + (t + 1) * 128],
                           Ob[:, t * 512:(t + 1) * 512],
                           first=(t == 0), last=(t == 3),
                           inc=("pe" if t == 3 else None))
                    wo_strm_tok[slot2] = p.val("pe")
                    p.wait("vector", "pe")
                    p.wait("vector", "dG", acc_pool_tok[slot2])
                    p.emit("vector", lambda e, sl=slot2, oc_=oc: e.tensor_scalar(
                        POOL[2 + sl][:, :], PS[2 + sl][:, :],
                        modc[:, mcol(2, oc_, s):mcol(2, oc_, s) + 1], None, ALU.mult),
                           inc="ve")
                    wo_ts_tok[slot2] = p.val("ve")
                    p.wait("gpsimd", "ve")
                    DMAACC(
                        bass.AP(x1_scr.tensor, s * E * S + (oc * 128) * S + q0,
                                [[S, 128], [1, 512]]),
                        POOL[2 + slot2][:, :])
                    acc_pool_tok[slot2] = p.val("dG")
                prev_acc_round = p.val("dG")
                ob_free_tok = p.val("pe")
                ps2_tok = p.val("ve")

        if do_dump and qc == NQC - 1:
            if "d_kn" in dumps:
                p.wait("sync", "ve")
                DMA("sync", dumps["d_kn"], Kn[:, :])
            if "d_vt" in dumps:
                DMA("sync", dumps["d_vt"], Vt[:, :])
            if "d_kpe" in dumps:
                DMA("sync", dumps["d_kpe"][:, 0:S], kpe1[:, :])
                DMA("sync", dumps["d_kpe"][:, S:2 * S], kpe2[:, :])

    def ffn_tt(s, tt, do_dump):
        t0 = tt * 512

        p.wait("sync", "dG")
        p.wait("sync", "dS")
        p.wait("sync", "ve")
        p.wait("sync", "pe")
        for c in range(NCH):
            DMA("sync", x1t[:, c * 512:(c + 1) * 512],
                bass.AP(x1_scr.tensor, s * E * S + (c * 128) * S + t0,
                        [[S, 128], [1, 512]]))

        p.wait("vector", "dS")
        p.wait("scalar", "dS")
        first_tok = 0
        for c in range(NCH):
            csl = c % 2
            if c >= 2:
                p.wait("vector", "pe", first_tok)
                p.wait("scalar", "pe", first_tok)
            COPY("vector", SCB[:, csl * 512:(csl + 1) * 512],
                 x1t[:, c * 512:(c + 1) * 512], inc="ve")
            ACT(SCB[:, 1024 + csl * 512:1024 + (csl + 1) * 512],
                x1t[:, c * 512:(c + 1) * 512], AF.Square, inc="se")
            p.wait("tensor", "ve")
            p.wait("tensor", "se")
            MM(PS[0][0:32, :], ones32[:, :], SCB[:, csl * 512:(csl + 1) * 512],
               first=(c == 0), last=(c == NCH - 1))
            MM(PS[1][0:32, :], ones32[:, :],
               SCB[:, 1024 + csl * 512:1024 + (csl + 1) * 512],
               first=(c == 0), last=(c == NCH - 1), inc="pe")
            first_tok = p.val("pe")

        def rowmath_bcast(ps_sum, ps_sq, n, eps_col, mu_dst, r_dst):
            p.wait("vector", "pe")
            TS("vector", POOL[0][0:1, :], ps_sum[0:1, :], 1.0 / n, None, ALU.mult)
            TT("vector", POOL[1][0:1, :], POOL[0][0:1, :], POOL[0][0:1, :], ALU.mult)
            STT("vector", POOL[1][0:1, :], ps_sq[0:1, :], 1.0 / n, POOL[1][0:1, :],
                ALU.mult, ALU.subtract, inc="ve")
            p.wait("scalar", "ve")
            ACT(POOL[1][0:1, :], POOL[1][0:1, :], AF.Ln, bias=eps_col[0:1, :])
            ACT(POOL[1][0:1, :], POOL[1][0:1, :], AF.Exp, scale=-0.5, inc="se")
            p.wait("gpsimd", "ve")
            p.wait("gpsimd", "se")
            p.emit("gpsimd", lambda e, d=mu_dst: e.partition_broadcast(d, POOL[0][0:1, :]))
            p.emit("gpsimd", lambda e, d=r_dst: e.partition_broadcast(d, POOL[1][0:1, :]),
                   inc="ge")
            p.wait("vector", "ge")

        rowmath_bcast(PS[0], PS[1], E, smal[:, 1:2], POOL[2][:, :], POOL[3][:, :])

        first_tok = 0
        for c in range(NCH):
            csl = c % 2
            if c >= 2:
                p.wait("vector", "pe", first_tok)
                p.wait("scalar", "pe", first_tok)
            TT("vector", POOL[4][:, :], x1t[:, c * 512:(c + 1) * 512], POOL[2][:, :],
               ALU.subtract)
            TT("vector", POOL[4][:, :], POOL[4][:, :], POOL[3][:, :], ALU.mult)
            p.emit("vector", lambda e, c_=c: e.tensor_scalar(
                h2t[:, c_ * 512:(c_ + 1) * 512], POOL[4][:, :],
                modc[:, mcol(4, c_, s):mcol(4, c_, s) + 1],
                modc[:, mcol(3, c_, s):mcol(3, c_, s) + 1],
                ALU.mult, ALU.add))
            COPY("vector", SCB[:, csl * 512:(csl + 1) * 512],
                 h2t[:, c * 512:(c + 1) * 512], inc="ve")
            if c == 0:
                COPY("vector", Pb[:, 0:512], SCB[:, csl * 512:(csl + 1) * 512])
                COPY("vector", Pb[:, 512:1024], SCB[:, csl * 512:(csl + 1) * 512])
            else:
                TT("vector", Pb[:, 0:512], Pb[:, 0:512],
                   SCB[:, csl * 512:(csl + 1) * 512], ALU.max)
                TT("vector", Pb[:, 512:1024], Pb[:, 512:1024],
                   SCB[:, csl * 512:(csl + 1) * 512], ALU.min)
            p.wait("scalar", "ve")
            ACT(SCB[:, 1024 + csl * 512:1024 + (csl + 1) * 512],
                h2t[:, c * 512:(c + 1) * 512], AF.Square, inc="se")
            p.wait("tensor", "ve")
            p.wait("tensor", "se")
            MM(PS[2][0:32, :], ones32[:, :], SCB[:, csl * 512:(csl + 1) * 512],
               first=(c == 0), last=(c == NCH - 1))
            MM(PS[3][0:32, :], ones32[:, :],
               SCB[:, 1024 + csl * 512:1024 + (csl + 1) * 512],
               first=(c == 0), last=(c == NCH - 1), inc="pe")
            first_tok = p.val("pe")
        rowmath_bcast(PS[2], PS[3], E, smal[:, 3:4], POOL[2][:, :], POOL[3][:, :])

        TT("vector", POOL[4][:, :], Pb[:, 0:512], POOL[2][:, :], ALU.subtract)
        TT("vector", POOL[0][:, :], POOL[2][:, :], Pb[:, 512:1024], ALU.subtract)
        TT("vector", POOL[4][:, :], POOL[4][:, :], POOL[0][:, :], ALU.max, inc="ve")
        p.wait("gpsimd", "ve")
        p.emit("gpsimd", lambda e: e.partition_all_reduce(
            POOL[1][:, :], POOL[4][:, :], 128, ReduceOp.max), inc="ge")
        p.wait("vector", "ge")
        p.emit("vector", lambda e: e.reciprocal(POOL[0][:, :], POOL[1][:, :]))
        TS("vector", POOL[0][:, :], POOL[0][:, :], 128.0, None, ALU.mult)
        TT("vector", POOL[3][:, :], POOL[3][:, :], POOL[1][:, :], ALU.mult)
        TS("vector", POOL[3][:, :], POOL[3][:, :], 1.0 / 128.0, None, ALU.mult, inc="ve")

        xh_tok = [0] * NCH
        for c in range(NCH):
            TT("vector", POOL[4][:, :], h2t[:, c * 512:(c + 1) * 512], POOL[2][:, :],
               ALU.subtract)
            TT("vector", POOL[4][:, :], POOL[4][:, :], POOL[0][:, :], ALU.mult)
            TS("vector", POOL[4][:, :], POOL[4][:, :], CLIP_HI, -CLIP_HI, ALU.min,
               ALU.max)
            TT("vector", xh[:, c * 512:(c + 1) * 512], POOL[4][:, :], POOL[3][:, :],
               ALU.mult, inc="ve")
            xh_tok[c] = p.val("ve")
        xh_ready = p.val("ve")
        if do_dump and "d_xh" in dumps:
            p.wait("sync", "ve")
            DMA("sync", dumps["d_xh"], xh[:, :])

        t4_tok = p.t4_tok
        for u in range(64):
            og, g4 = divmod(u, 4)
            buf = u % 3
            q = ("sync", "dS") if u % 2 == 0 else ("scalar", "dC")
            p.wait(q[0], "pe", t4_tok[buf])
            DMA(q[0], T4[buf][:, :].rearrange("p (j i q) -> p j i q", j=4, i=4),
                bass.AP(tfc_scr.tensor, (og * 4 + g4) * 262144,
                        [[128, 128], [65536, 4], [16384, 4], [1, 128]]),
                sem=q[1])
            dtok = p.val(q[1])
            p.wait("tensor", q[1], dtok)
            p.wait("tensor", "ve", xh_tok[g4 * 4 + 3])
            if g4 == 0:
                p.wait("tensor", "se")
            for j in range(4):
                d = g4 * 4 + j
                for i in range(4):
                    MM(PS[i][:, :], T4[buf][:, j * 512 + i * 128:j * 512 + (i + 1) * 128],
                       xh[:, d * 512:(d + 1) * 512],
                       first=(d == 0), last=(d == NCH - 1),
                       inc=("pe" if i == 3 else None))
            t4_tok[buf] = p.val("pe")
            if g4 == 3:
                p.wait("scalar", "pe")
                for i in range(4):
                    oc = og * 4 + i
                    ACT(gbuf[:, oc * 512:(oc + 1) * 512], PS[i][:, :], AF.Gelu,
                        scale=gam[:, 2:3], inc=("se" if i == 3 else None))
                gelu_tok = p.val("se")
                p.wait("scalar", "se", gelu_tok)
                p.wait("vector", "se", gelu_tok)
                for i in range(4):
                    oc = og * 4 + i
                    csl = i % 2
                    if og > 0 or i >= 2:
                        p.wait("scalar", "pe", p.gstat_tok)
                    ACT(SCB[:, 1024 + csl * 512:1024 + (csl + 1) * 512],
                        gbuf[:, oc * 512:(oc + 1) * 512], AF.Square, inc="se")
                    if oc == 0:
                        COPY("vector", Pb[:, 0:512], gbuf[:, 0:512])
                        COPY("vector", Pb[:, 512:1024], gbuf[:, 0:512])
                    else:
                        TT("vector", Pb[:, 0:512], Pb[:, 0:512],
                           gbuf[:, oc * 512:(oc + 1) * 512], ALU.max)
                        TT("vector", Pb[:, 512:1024], Pb[:, 512:1024],
                           gbuf[:, oc * 512:(oc + 1) * 512], ALU.min)
                    p.wait("tensor", "se")
                    MM(PS[4][0:32, :], ones32[:, :], gbuf[:, oc * 512:(oc + 1) * 512],
                       first=(oc == 0), last=(oc == NOC - 1))
                    MM(PS[5][0:32, :], ones32[:, :],
                       SCB[:, 1024 + csl * 512:1024 + (csl + 1) * 512],
                       first=(oc == 0), last=(oc == NOC - 1), inc="pe")
                    p.gstat_tok = p.val("pe")
        if do_dump and "d_g" in dumps:
            p.wait("sync", "se")
            DMA("sync", dumps["d_g"], gbuf[:, 0:NOC * 512])

        rowmath_bcast(PS[4], PS[5], DFF, smal[:, 3:4], POOL[2][:, :], POOL[3][:, :])
        TT("vector", POOL[4][:, :], Pb[:, 0:512], POOL[2][:, :], ALU.subtract)
        TT("vector", POOL[0][:, :], POOL[2][:, :], Pb[:, 512:1024], ALU.subtract)
        TT("vector", POOL[4][:, :], POOL[4][:, :], POOL[0][:, :], ALU.max, inc="ve")
        p.wait("gpsimd", "ve")
        p.emit("gpsimd", lambda e: e.partition_all_reduce(
            POOL[1][:, :], POOL[4][:, :], 128, ReduceOp.max), inc="ge")
        p.wait("vector", "ge")
        COPY("vector", SCB[:, 1024:1536], POOL[2][:, :])
        TT("vector", POOL[3][:, :], POOL[3][:, :], POOL[1][:, :], ALU.mult)
        TS("vector", POOL[3][:, :], POOL[3][:, :], 1.0 / 128.0, None, ALU.mult)
        COPY("vector", SCB[:, 0:512], POOL[3][:, :])
        p.emit("vector", lambda e: e.reciprocal(POOL[0][:, :], POOL[1][:, :]))
        TS("vector", POOL[0][:, :], POOL[0][:, :], 128.0, None, ALU.mult)
        COPY("vector", SCB[:, 512:1024], POOL[0][:, :], inc="ve")

        xh2_tok = [0] * NOC
        for oc in range(NOC):
            TT("vector", gbuf[:, oc * 512:(oc + 1) * 512],
               gbuf[:, oc * 512:(oc + 1) * 512], SCB[:, 1024:1536], ALU.subtract)
            TT("vector", gbuf[:, oc * 512:(oc + 1) * 512],
               gbuf[:, oc * 512:(oc + 1) * 512], SCB[:, 512:1024], ALU.mult)
            TS("vector", gbuf[:, oc * 512:(oc + 1) * 512],
               gbuf[:, oc * 512:(oc + 1) * 512], CLIP_HI, -CLIP_HI, ALU.min, ALU.max)
            TT("vector", gbuf[:, oc * 512:(oc + 1) * 512],
               gbuf[:, oc * 512:(oc + 1) * 512], SCB[:, 0:512], ALU.mult,
               inc="ve")
            xh2_tok[oc] = p.val("ve")

        t4_tok = p.t4_tok
        for u in range(64):
            ocg, k4 = divmod(u, 16)
            buf = u % 3
            q = ("sync", "dS") if u % 2 == 0 else ("scalar", "dC")
            p.wait(q[0], "pe", t4_tok[buf])
            DMA(q[0], T4[buf][:, :].rearrange("p (j i q) -> p j i q", j=4, i=4),
                bass.AP(tpj_scr.tensor, (ocg * 16 + k4) * 262144,
                        [[128, 128], [65536, 4], [16384, 4], [1, 128]]),
                sem=q[1])
            dtok = p.val(q[1])
            p.wait("tensor", q[1], dtok)
            p.wait("tensor", "ve", xh2_tok[k4 * 4 + 3])
            if k4 == 0:
                p.wait("tensor", "ve")
            for j in range(4):
                kk = k4 * 4 + j
                for i in range(4):
                    MM(PS[i][:, :], T4[buf][:, j * 512 + i * 128:j * 512 + (i + 1) * 128],
                       gbuf[:, kk * 512:(kk + 1) * 512],
                       first=(kk == 0), last=(kk == 4 * NCH - 1),
                       inc=("pe" if i == 3 else None))
            t4_tok[buf] = p.val("pe")
            if k4 == 15:
                p.wait("vector", "pe")
                for i in range(4):
                    oc = ocg * 4 + i
                    p.emit("vector", lambda e, i_=i, oc_=oc: e.scalar_tensor_tensor(
                        x1t[:, oc_ * 512:(oc_ + 1) * 512], PS[i_][:, :],
                        modc[:, 192 + oc_ * 2 + s:192 + oc_ * 2 + s + 1],
                        x1t[:, oc_ * 512:(oc_ + 1) * 512],
                        ALU.mult, ALU.add), inc=("ve" if i == 3 else None))
        out_dma_tok = 0
        p.wait("tensor", "ve")
        for rt in range(4):
            bsl = rt % 2
            for ocg4 in range(4):
                for i in range(4):
                    oc = ocg4 * 4 + i
                    TR(PS[4 + bsl][:, i * 128:(i + 1) * 128],
                       x1t[:, oc * 512 + rt * 128:oc * 512 + (rt + 1) * 128],
                       inc=("pe" if i == 3 else None))
                p.wait("vector", "pe")
                if ocg4 == 0:
                    p.wait("vector", "dS", out_dma_tok)
                COPY("vector", otm[:, ocg4 * 512:(ocg4 + 1) * 512], PS[4 + bsl][:, :],
                     inc="ve")
                p.wait("tensor", "ve")
            p.wait("sync", "ve")
            DMA("sync", out_hbm[s, t0 + rt * 128:t0 + (rt + 1) * 128, :], otm[:, :])
            out_dma_tok = p.val("dS")

    PS = []

    if "M" in phases:
        phase_M()
    ctxM.close()
    PS.extend(ctx.enter_context(nc.psum_tensor(f"ps{i}", [128, 512], F32))
              for i in range(8))
    if "A" in phases:
        for s in range(NSEQ):
            p.barrier()
            for qc in range(NQC):
                attn_qc(s, qc, s == 0)
            p.barrier()
            if s == 0 and "d_x1scr" in dumps:
                p.wait("sync", "dG")
                DMA("sync", dumps["d_x1scr"], x1_scr[0])
            if "B" in phases:
                for tt in range(NQC):
                    ffn_tt(s, tt, s == 0 and tt == 0)
                p.barrier()

    p.barrier()
    for e in ENGS:
        p.wait(e, "dS")
        p.wait(e, "dG")
        p.wait(e, "dC")

    with nc.Block() as block:
        p.replay(block)
    ctx.close()
    nc.compile()
    return nc, dumps


def _consts():
    pos = np.arange(S, dtype=np.float64)
    inv = 1.0 / (10000.0 ** (np.arange(0, ROPE_D, 2, dtype=np.float64) / ROPE_D))
    ang = np.outer(inv, pos)
    cosT = np.tile(np.cos(ang), (4, 1)).astype(np.float32)
    sinT = np.tile(np.sin(ang), (4, 1)).astype(np.float32)
    return np.ascontiguousarray(cosT), np.ascontiguousarray(sinT)


def make_in_maps(x, c, wq, wkva, wkvb, wo, kvs, wfc, wpj, wmod):
    cosT, sinT = _consts()
    base = dict(
        wq=np.ascontiguousarray(np.asarray(wq, np.float32)),
        wkva=np.ascontiguousarray(np.asarray(wkva, np.float32)),
        wkvb=np.ascontiguousarray(np.asarray(wkvb, np.float32)),
        wo=np.ascontiguousarray(np.asarray(wo, np.float32)),
        kvs=np.ascontiguousarray(np.asarray(kvs, np.float32)).reshape(1, LORA),
        wfc=np.ascontiguousarray(np.asarray(wfc, np.float32)),
        wpj=np.ascontiguousarray(np.asarray(wpj, np.float32)),
        wmod=np.ascontiguousarray(np.asarray(wmod, np.float32)),
        cosT=cosT, sinT=sinT,
    )
    x = np.ascontiguousarray(np.asarray(x, np.float32))
    c = np.ascontiguousarray(np.asarray(c, np.float32))
    in_maps = []
    for core in range(8):
        m = dict(base)
        m["x"] = np.ascontiguousarray(x[2 * core:2 * core + 2])
        m["c"] = np.ascontiguousarray(c[2 * core:2 * core + 2])
        in_maps.append(m)
    return in_maps


def kernel(L0_attn_x, L0_attn_wq, L0_attn_wkv_a, L0_attn_wkv_b, L0_attn_wo,
           L0_attn_kv_norm_scale, L0_ffn_w_fc, L0_ffn_w_proj, L0_c, L0_wmod):
    in_maps = make_in_maps(L0_attn_x, L0_c, L0_attn_wq, L0_attn_wkv_a, L0_attn_wkv_b,
                           L0_attn_wo, L0_attn_kv_norm_scale, L0_ffn_w_fc,
                           L0_ffn_w_proj, L0_wmod)
    nc, _ = build_nc()
    res = run_bass_kernel_spmd(nc, in_maps, core_ids=list(range(8)))
    out = np.concatenate([np.asarray(r["out"]) for r in res.results], axis=0)
    return out.astype(np.float32)


# version 11
# speedup vs baseline: 1.0284x; 1.0112x over previous
import contextlib
import numpy as np
import ml_dtypes

import concourse.bacc as bacc
import concourse.bass as bass
import concourse.mybir as mybir
from concourse.bass_utils import run_bass_kernel_spmd
from concourse.bass_isa import ReduceOp

F32 = mybir.dt.float32
BF16 = mybir.dt.bfloat16
U8 = mybir.dt.uint8
AF = mybir.ActivationFunctionType
ALU = mybir.AluOpType
AX = mybir.AxisListType

B, S, E, H = 16, 2048, 2048, 16
NOPE, ROPE_D, VD, QH, LORA = 128, 64, 128, 192, 512
DFF = 8192
NSEQ = 2
QCW = 512
NQC = S // QCW
NCH = E // 128
NKB = S // 128
NOC = DFF // 128
SCALE = float(QH) ** -0.5
EXPB = -8.0
EPS_LN = 1e-5
EPS_ATT = 1e-6
EPS_FFN = 1e-6
CLIP_HI = 128.0 - 1e-6

ENGS = ("sync", "gpsimd", "scalar", "tensor", "vector")


class Plan:
    def __init__(self, nc):
        self.nc = nc
        self.prog = {e: [] for e in ENGS}
        self.sems = {}

    def sem(self, name):
        h = self.nc.alloc_semaphore(name=name)
        self.sems[name] = [h, 0]

    def emit(self, eng, fn, inc=None, amt=1):
        if inc is not None:
            rec = self.sems[inc]
            rec[1] += amt
            self.prog[eng].append((fn, rec[0], amt))
        else:
            self.prog[eng].append((fn, None, 0))

    def wait(self, eng, sem, val=None):
        rec = self.sems[sem]
        h = rec[0]
        v = rec[1] if val is None else val
        if v <= 0:
            return
        self.prog[eng].append((lambda e, h=h, v=v: e.wait_ge(h, v), None, 0))

    def val(self, sem):
        return self.sems[sem][1]

    def barrier(self):
        bar = self.sems["bar"]
        h = bar[0]
        for e in ENGS:
            self.prog[e].append((lambda eng, h=h: eng.sem_inc(h, 1), None, 0))
        bar[1] += len(ENGS)
        v = bar[1]
        for e in ENGS:
            self.prog[e].append((lambda eng, h=h, v=v: eng.wait_ge(h, v), None, 0))

    def replay(self, block):
        def runner(items):
            def run(e):
                for fn, semh, amt in items:
                    ins = fn(e)
                    if semh is not None:
                        ins.then_inc(semh, amt)
            return run

        block.sync(runner(self.prog["sync"]))
        block.gpsimd(runner(self.prog["gpsimd"]))
        block.scalar(runner(self.prog["scalar"]))
        block.tensor(runner(self.prog["tensor"]))
        block.vector(runner(self.prog["vector"]))


def mcol(v, c, s_):
    return v * 32 + c * 2 + s_


def build_nc(phases=("M", "A", "B"), dump=(), nogamma=False):
    nc = bacc.Bacc("TRN2", target_bir_lowering=False)
    dump = set(dump)

    def din(name, shape, dt=F32):
        return nc.dram_tensor(name, list(shape), dt, kind="ExternalInput").ap()

    x_in = din("x", (NSEQ, S, E))
    c_in = din("c", (NSEQ, E))
    wq_in = din("wq", (E, H * QH))
    wkva_in = din("wkva", (E, LORA + ROPE_D))
    wkvb_in = din("wkvb", (LORA, H * (NOPE + VD)))
    wo_in = din("wo", (H * VD, E))
    kvs_in = din("kvs", (1, LORA))
    wfc_in = din("wfc", (DFF, E))
    wpj_in = din("wpj", (E, DFF))
    wmod_in = din("wmod", (E, 6 * E))
    cos_in = din("cosT", (128, S))
    sin_in = din("sinT", (128, S))

    out_hbm = nc.dram_tensor("out", [NSEQ, S, E], F32, kind="ExternalOutput").ap()

    def scr(name, shape, dt):
        return nc.dram_tensor(name, list(shape), dt, kind="Internal").ap()

    x1_scr = scr("x1s", (NSEQ, E, S), F32)
    wqbf_scr = scr("wqbf", (24 * NCH, 128 * 128), BF16)
    wobf_scr = scr("wobf", (NCH * NCH, 128 * 128), BF16)
    wkvbf_scr = scr("wkvbf", (LORA, H * (NOPE + VD)), BF16)
    wkvabf_scr = scr("wkvabf", (E, LORA + 256), BF16)
    tfc_scr = scr("tfc", (NCH * NOC, 128 * 128), BF16)
    tpj_scr = scr("tpj", (NOC * NCH, 128 * 128), BF16)

    dumps = {}

    def mkdump(name, shape, dt):
        dumps[name] = nc.dram_tensor(name, list(shape), dt, kind="ExternalOutput").ap()

    if "mod" in dump:
        mkdump("d_mod", (128, 224), F32)
    if "gam" in dump:
        mkdump("d_gam", (128, 8), F32)
    if "h" in dump:
        mkdump("d_h", (128, NCH * QCW), BF16)
    if "ckv" in dump:
        mkdump("d_ckv", (128, 4 * QCW), BF16)
    if "kpe" in dump:
        mkdump("d_kpe", (128, 2 * S), BF16)
    if "kn" in dump:
        mkdump("d_kn", (128, H * S), BF16)
    if "vt" in dump:
        mkdump("d_vt", (128, NKB * 2048), BF16)
    if "qpe" in dump:
        mkdump("d_qpe", (128, 8 * QCW), BF16)
    if "ob" in dump:
        mkdump("d_ob", (128, 4 * QCW), BF16)
    if "xh" in dump:
        mkdump("d_xh", (128, NCH * QCW), BF16)
    if "g" in dump:
        mkdump("d_g", (128, NOC * QCW), BF16)
    if "wscr" in dump:
        mkdump("d_wscr", (LORA, H * (NOPE + VD)), BF16)
    if "wqscr" in dump:
        mkdump("d_wqscr", (24 * NCH, 128 * 128), BF16)
    if "x1scr" in dump:
        mkdump("d_x1scr", (E, S), F32)

    ctx = contextlib.ExitStack()

    def sbt(name, shape, dt=F32):
        return ctx.enter_context(nc.sbuf_tensor(name, list(shape), dt))

    modc = sbt("modc", (128, 224))
    smal = sbt("smal", (128, 8))
    kvsc = sbt("kvsc", (128, 4))
    gam = sbt("gam", (128, 8))
    ones32 = sbt("ones32", (128, 32), BF16)
    id128 = sbt("id128", (128, 128))
    id128b = sbt("id128b", (128, 128), BF16)
    POOL = [sbt(f"pool{i}", (128, 512)) for i in range(5)]
    SCB = sbt("scb", (128, 2048), BF16)
    STRM = sbt("strm", (128, 2048), BF16)
    STRF = [sbt(f"strf{i}", (128, 2048)) for i in range(2)]

    ar1 = sbt("ar1", (128, 65536), U8)
    ar2 = sbt("ar2", (128, 4096), U8)
    ar3 = sbt("ar3", (128, 8192), U8)
    a1 = nc.lookup_mloc(ar1).addr
    a2 = nc.lookup_mloc(ar2).addr
    a3 = nc.lookup_mloc(ar3).addr

    def at(name, shape, dt, base, off):
        return nc.alloc_sbuf_tensor_at(name, list(shape), dt, offset=base + off)

    Kn = at("Kn", (128, H * S), BF16, a1, 0)
    x1t = at("x1t", (128, NCH * 512), F32, a1, 0)
    h2t = at("h2t", (128, NCH * 512), F32, a1, 32768)
    cosq = at("cosq", (128, 2 * QCW), F32, a2, 0)
    T4 = [at("t4a", (128, 2048), BF16, a2, 0)]
    qpe = at("qpe", (128, 8 * QCW), BF16, a3, 0)
    otm = at("otm", (128, 2048), F32, a3, 0)

    Vt = sbt("Vt", (128, NKB * 2048), BF16)
    kpe1 = sbt("kpe1", (128, S), BF16)
    kpe2 = sbt("kpe2", (128, S), BF16)
    T4.append(at("t4b", (128, 2048), BF16, nc.lookup_mloc(kpe1).addr, 0))
    T4.append(at("t4c", (128, 2048), BF16, nc.lookup_mloc(kpe2).addr, 0))
    TERNB = at("ternb", (128, 6 * 2048), BF16, nc.lookup_mloc(Vt).addr, 0)
    TERNF = [at(f"ternf{i}", (128, 2048), F32, a1, i * 8192) for i in range(4)]
    hq = sbt("hq", (128, NCH * QCW), BF16)
    qn = sbt("qn", (128, QCW), BF16)
    Pb = sbt("Pb", (128, 2 * QCW), BF16)
    Ob = sbt("Ob", (128, 4 * QCW), BF16)
    gbuf = Vt
    xh = hq
    ckvn = SCB

    ctxM = contextlib.ExitStack()
    PSM = [ctxM.enter_context(nc.psum_tensor(f"psm{i}", [128, 512], F32))
           for i in range(3)]
    PSB = [ctxM.enter_context(nc.psum_tensor(f"psb{i}", [128, 1024], BF16))
           for i in range(2)]

    p = Plan(nc)
    p.t4_tok = [0, 0, 0]
    p.gstat_tok = 0
    for s_ in ("bar", "dS", "dG", "dC", "pe", "ve", "se", "ge"):
        p.sem(s_)

    def MM(ps_ap, lhsT, rhs, first, last=True, inc=None, tp=None):
        p.emit("tensor",
               lambda e, a=ps_ap, l=lhsT, r=rhs, f=first, q=last, t=tp:
               e.matmul(a, l, r, start=f, stop=q, tile_position=t), inc=inc)

    def TR(ps_ap, in_ap, inc=None):
        p.emit("tensor",
               lambda e, a=ps_ap, i=in_ap: e.transpose(a, i, id128[:, :]), inc=inc)

    def DMA(eng, dst, src, sem="dS", slow=False):
        p.emit(eng, lambda e, d=dst, s=src, sl=slow: e.dma_start(
            out=d, in_=s, allow_slow_non_contiguous=sl), inc=sem, amt=16)

    def DMAT(eng, dst, src, sem="dS"):
        p.emit(eng, lambda e, d=dst, s=src: e.dma_start_transpose(out=d, in_=s),
               inc=sem, amt=16)

    def DMAACC(dst, src):
        p.emit("gpsimd", lambda e, d=dst, s=src:
               e.dma_start(out=d, in_=s, accum_op=ALU.add), inc="dG", amt=16)

    def COPY(eng, dst, src, inc=None):
        p.emit(eng, lambda e, d=dst, s=src: e.tensor_copy(d, s), inc=inc)

    def TT(eng, dst, a, b, op, inc=None):
        p.emit(eng, lambda e, d=dst, x=a, y=b, o=op: e.tensor_tensor(d, x, y, o),
               inc=inc)

    def TS(eng, dst, a, s1, s2, op0, op1=None, inc=None):
        if op1 is None:
            p.emit(eng, lambda e, d=dst, x=a, u=s1, o0=op0:
                   e.tensor_scalar(d, x, u, None, o0), inc=inc)
        else:
            p.emit(eng, lambda e, d=dst, x=a, u=s1, v=s2, o0=op0, o1=op1:
                   e.tensor_scalar(d, x, u, v, o0, o1), inc=inc)

    def STT(eng, dst, in0, scalar, in1, op0, op1, inc=None):
        p.emit(eng, lambda e, d=dst, a=in0, s=scalar, b=in1, o0=op0, o1=op1:
               e.scalar_tensor_tensor(d, a, s, b, o0, o1), inc=inc)

    def ACT(dst, src, func, bias=None, scale=None, accum=None, inc=None):
        kw = {}
        if bias is not None:
            kw["bias"] = bias
        if scale is not None:
            kw["scale"] = scale
        if accum is not None:
            kw["accum_out"] = accum
        p.emit("scalar", lambda e, d=dst, s=src, f=func, k=kw:
               e.activation(d, s, f, **k), inc=inc)

    def SCOPY2(dst, src, inc=None):
        p.emit("scalar", lambda e, d=dst, s=src: e.copy(d, s), inc=inc)

    def SIGN(dst, src, bias, inc=None):
        p.emit("scalar", lambda e, d=dst, s=src, b=bias: e.sign(d, s, bias=b), inc=inc)

    p.emit("gpsimd", lambda e: e.memset(ones32[:, :], 1.0))
    p.emit("gpsimd", lambda e: e.memset(id128[:, :], 0.0))
    p.emit("gpsimd", lambda e: e.affine_select(
        out=id128[:, :], in_=id128[:, :], pattern=[[-1, 128]],
        compare_op=ALU.not_equal, fill=1.0, base=0, channel_multiplier=1))
    p.emit("gpsimd", lambda e: e.memset(id128b[:, :], 0.0))
    p.emit("gpsimd", lambda e: e.affine_select(
        out=id128b[:, :], in_=id128b[:, :], pattern=[[-1, 128]],
        compare_op=ALU.not_equal, fill=1.0, base=0, channel_multiplier=1))
    p.emit("gpsimd", lambda e: e.memset(smal[:, 0:1], EXPB))
    p.emit("gpsimd", lambda e: e.memset(smal[:, 1:2], EPS_LN))
    p.emit("gpsimd", lambda e: e.memset(smal[:, 2:3], EPS_ATT))
    p.emit("gpsimd", lambda e: e.memset(smal[:, 3:4], EPS_FFN), inc="ge")

    DMA("sync", kvsc[:, 0:4], bass.AP(kvs_in.tensor, 0, [[1, 128], [128, 4]]),
        slow=True)
    for s_ in range(NSEQ):
        DMA("sync", bass.AP(POOL[4], s_, [[512, 128], [2, 16]]),
            bass.AP(c_in.tensor, s_ * E, [[1, 128], [128, 16]]), slow=True)

    def phase_M():
        p.wait("scalar", "dS")
        ACT(POOL[4][:, 32:64], POOL[4][:, 0:32], AF.Silu, inc="se")
        scv = POOL[4]

        strf_tok = [0, 0]
        first = True
        p.wait("tensor", "se")
        for ec in range(NCH):
            for cb in range(8):
                slot = (ec * 8 + cb) % 2
                p.wait("sync", "pe", strf_tok[slot])
                DMA("sync", STRF[slot][:, 0:1536],
                    wmod_in[ec * 128:(ec + 1) * 128, cb * 1536:(cb + 1) * 1536])
                p.wait("tensor", "dS")
                for t in range(12):
                    cc = cb * 12 + t
                    MM(PSM[0][:, 2 * cc:2 * cc + 2],
                       STRF[slot][:, t * 128:(t + 1) * 128],
                       scv[:, 32 + 2 * ec:34 + 2 * ec],
                       first=first, last=(ec == NCH - 1),
                       inc=("pe" if t == 11 else None))
                    first = False
                strf_tok[slot] = p.val("pe")
        p.wait("vector", "pe")
        COPY("vector", modc[:, 0:192], PSM[0][:, 0:192], inc="ve")

        for v in (2, 5):
            cols = modc[:, v * 32:(v + 1) * 32]
            TS("vector", POOL[0][:, 0:32], cols, 0.25, None, ALU.max)
            TS("vector", POOL[1][:, 0:32], cols, -0.25, None, ALU.min)
            TS("vector", POOL[2][:, 0:32], cols, 0.0, None, ALU.is_ge)
            TT("vector", POOL[3][:, 0:32], POOL[0][:, 0:32], POOL[1][:, 0:32],
               ALU.subtract)
            TT("vector", POOL[3][:, 0:32], POOL[2][:, 0:32], POOL[3][:, 0:32], ALU.mult)
            TT("vector", cols, POOL[1][:, 0:32], POOL[3][:, 0:32], ALU.add)
        for v in (1, 4):
            cols = modc[:, v * 32:(v + 1) * 32]
            TS("vector", cols, cols, 1.0, None, ALU.add, inc="ve")

        def gamma_pass(w_in, nslabs, slab_cols, ps_bank, gcols, eps):
            tok = [p.val("se")] * 4
            dtok = [0, 0, 0, 0]
            mm_tok = [0, 0]
            firstg = True
            STG2 = [SCB, STRM]
            p.wait("scalar", "pe")
            p.wait("scalar", "ve")
            for i in range(nslabs):
                slot = i % 4
                p.wait("sync", "se", tok[slot])
                r0 = (i * 2048) // slab_cols
                c0 = (i * 2048) % slab_cols
                DMA("sync", TERNF[slot][:, 0:2048],
                    w_in[r0 * 128:(r0 + 1) * 128, c0:c0 + 2048])
                dtok[slot] = p.val("dS")
                sg = i % 2
                p.wait("scalar", "dS", dtok[slot])
                p.wait("scalar", "pe", mm_tok[sg])
                ACT(STG2[sg][:, 0:2048], TERNF[slot][:, 0:2048], AF.Abs, inc="se")
                tok[slot] = p.val("se")
                p.wait("tensor", "se")
                for t in range(4):
                    MM(ps_bank[0:32, :], ones32[:, :],
                       STG2[sg][:, t * 512:(t + 1) * 512],
                       first=firstg, last=(i == nslabs - 1 and t == 3),
                       inc=("pe" if t == 3 else None))
                    firstg = False
                mm_tok[sg] = p.val("pe")
            p.wait("vector", "pe")
            p.emit("vector", lambda e, pb=ps_bank: e.reduce_sum(
                POOL[0][0:1, 0:1], pb[0:1, :], AX.X), inc="ve")
            n_el = nslabs * 2048 * 128
            p.wait("gpsimd", "ve")
            TS("gpsimd", POOL[0][0:1, 1:2], POOL[0][0:1, 0:1], 1.0 / n_el, None, ALU.mult)
            TS("gpsimd", POOL[0][0:1, 2:3], POOL[0][0:1, 1:2], eps, 0.5, ALU.add, ALU.mult)
            TS("gpsimd", POOL[0][0:1, 3:4], POOL[0][0:1, 2:3], -1.0, None, ALU.mult)
            TS("gpsimd", POOL[0][0:1, 4:5], POOL[0][0:1, 1:2], 0.5, None, ALU.mult)
            p.emit("gpsimd", lambda e, g=gcols[0]: e.partition_broadcast(
                gam[:, g:g + 1], POOL[0][0:1, 3:4]))
            p.emit("gpsimd", lambda e, g=gcols[1]: e.partition_broadcast(
                gam[:, g:g + 1], POOL[0][0:1, 2:3]))
            p.emit("gpsimd", lambda e, g=gcols[2]: e.partition_broadcast(
                gam[:, g:g + 1], POOL[0][0:1, 4:5]), inc="ge")
            p.wait("vector", "ge")

        if not nogamma:
            gamma_pass(wfc_in, 64, E, PSM[1], (0, 1, 2), EPS_FFN)
            gamma_pass(wpj_in, 64, DFF, PSM[2], (3, 4, 5), EPS_FFN)

        p.wait("vector", "ge")
        p.emit("vector", lambda e: e.tensor_scalar(
            modc[:, 192:224], modc[:, 160:192], gam[:, 5:6], None, ALU.mult), inc="ve")

        p.wait("vector", "pe")
        p.wait("vector", "ge")
        p.wait("sync", "ge")
        p.wait("sync", "se")
        STG = [STRM, SCB]
        strf_tok = [0, 0, 0, 0]
        strf_dma = [0, 0, 0, 0]
        stg_out = [0, 0]

        def cast_unit(i_u, in_src, n_in, copies_fn, outs_fn):
            slot = i_u % 4
            sg = i_u % 2
            p.wait("sync", "ve", strf_tok[slot])
            DMA("sync", TERNF[slot][:, 0:n_in], in_src)
            strf_dma[slot] = p.val("dS")
            p.wait("vector", "dS", max(strf_dma[slot], stg_out[sg]))
            copies_fn(TERNF[slot], STG[sg])
            strf_tok[slot] = p.val("ve")
            p.wait("sync", "ve", strf_tok[slot])
            outs_fn(STG[sg])
            stg_out[sg] = p.val("dS")

        i_u = 0
        for c in range(NCH):
            for half in range(2):
                def cps(sf, sg):
                    COPY("vector", sg[:, 0:1024].rearrange("p (h d) -> p h d", d=128),
                         bass.AP(sf, 0, [[2048, 128], [192, 8], [1, 128]]))
                    COPY("vector", sg[:, 1024:1280].rearrange("p (h f) -> p h f", f=32),
                         bass.AP(sf, 128, [[2048, 128], [192, 8], [1, 32]]))
                    COPY("vector", sg[:, 1280:1536].rearrange("p (h f) -> p h f", f=32),
                         bass.AP(sf, 160, [[2048, 128], [192, 8], [1, 32]]), inc="ve")

                def outs(sg, c=c, half=half):
                    h0 = half * 8
                    DMA("sync",
                        bass.AP(wqbf_scr.tensor, h0 * 262144 + c * 128,
                                [[2048, 128], [262144, 8], [1, 128]]),
                        sg[:, 0:1024].rearrange("p (h d) -> p h d", d=128))
                    DMA("sync",
                        bass.AP(wqbf_scr.tensor, (16 + 2 * half) * 262144 + c * 128,
                                [[2048, 128], [262144, 2], [1, 128]]),
                        sg[:, 1024:1280].rearrange("p (j d) -> p j d", d=128))
                    DMA("sync",
                        bass.AP(wqbf_scr.tensor, (20 + 2 * half) * 262144 + c * 128,
                                [[2048, 128], [262144, 2], [1, 128]]),
                        sg[:, 1280:1536].rearrange("p (j d) -> p j d", d=128))
                cast_unit(i_u, wq_in[c * 128:(c + 1) * 128,
                                    half * 1536:(half + 1) * 1536], 1536, cps, outs)
                i_u += 1

        for hv in range(NCH):
            def cps(sf, sg):
                COPY("vector", sg[:, 0:2048], sf[:, 0:2048], inc="ve")

            def outs(sg, hv=hv):
                DMA("sync",
                    bass.AP(wobf_scr.tensor, hv * 128,
                            [[2048, 128], [262144, 16], [1, 128]]),
                    sg[:, 0:2048].rearrange("p (o d) -> p o d", d=128))
            cast_unit(i_u, wo_in[hv * 128:(hv + 1) * 128, :], 2048, cps, outs)
            i_u += 1

        for lc in range(4):
            for piece in range(2):
                def cps(sf, sg):
                    COPY("vector", sg[:, 0:1024].rearrange("p (h d) -> p h d", d=128),
                         bass.AP(sf, 0, [[2048, 128], [256, 8], [1, 128]]))
                    COPY("vector", sg[:, 1024:2048].rearrange("p (h d) -> p h d", d=128),
                         bass.AP(sf, 128, [[2048, 128], [256, 8], [1, 128]]), inc="ve")

                def outs(sg, lc=lc, piece=piece):
                    DMA("sync", wkvbf_scr[lc * 128:(lc + 1) * 128,
                                          piece * 1024:piece * 1024 + 1024],
                        sg[:, 0:1024])
                    DMA("sync", wkvbf_scr[lc * 128:(lc + 1) * 128,
                                          2048 + piece * 1024:2048 + piece * 1024 + 1024],
                        sg[:, 1024:2048])
                cast_unit(i_u, wkvb_in[lc * 128:(lc + 1) * 128,
                                       piece * 2048:(piece + 1) * 2048], 2048, cps, outs)
                i_u += 1

        for ec in range(NCH):
            def cps(sf, sg):
                COPY("vector", sg[:, 0:512], sf[:, 0:512])
                for r in range(4):
                    COPY("vector", sg[:, 512 + r * 32:544 + r * 32], sf[:, 512:544])
                    COPY("vector", sg[:, 640 + r * 32:672 + r * 32], sf[:, 544:576],
                         inc=("ve" if r == 3 else None))

            def outs(sg, ec=ec):
                DMA("sync", wkvabf_scr[ec * 128:(ec + 1) * 128, :], sg[:, 0:768])
            cast_unit(i_u, wkva_in[ec * 128:(ec + 1) * 128, :], 576, cps, outs)
            i_u += 1


        def tern_pass(w_in, rows_chunks, col_pieces, gneg_col, gpos_col, unit_fn):
            slab_tok = [p.val("ve")] * 4
            slab_dma = [0, 0, 0, 0]
            t2_tok = [p.val("pe")] * 2
            psb_tok = [p.val("se")] * 2
            pb_dma_tok = [p.val("dS")] * 2
            PbW = [STRM[:, 0:1024], STRM[:, 1024:2048]]
            units = [(rc, cp) for rc in range(rows_chunks) for cp in range(col_pieces)]
            n_u = len(units)
            for i_s in range(n_u):
                rc, cp = units[i_s]
                slot = i_s % 4
                p.wait("sync", "ve", slab_tok[slot])
                DMA("sync", TERNF[slot][:, 0:2048],
                    w_in[rc * 128:(rc + 1) * 128, cp * 2048:(cp + 1) * 2048])
                slab_dma[slot] = p.val("dS")
                t2s = i_s % 2
                p.wait("vector", "dS", slab_dma[slot])
                p.wait("vector", "pe", t2_tok[t2s])
                TS("vector", TERNB[:, 4096 + t2s * 2048:4096 + (t2s + 1) * 2048],
                   TERNF[slot][:, 0:2048], gam[:, gpos_col:gpos_col + 1], 2.0,
                   ALU.is_ge, ALU.mult)
                TS("vector", TERNB[:, 8192 + t2s * 2048:8192 + (t2s + 1) * 2048],
                   TERNF[slot][:, 0:2048], gam[:, gneg_col:gneg_col + 1], 2.0,
                   ALU.is_ge, ALU.mult, inc="ve")
                slab_tok[slot] = p.val("ve")
                STT("vector", TERNB[:, t2s * 2048:(t2s + 1) * 2048],
                    TERNB[:, 4096 + t2s * 2048:4096 + (t2s + 1) * 2048], -2.0,
                    TERNB[:, 8192 + t2s * 2048:8192 + (t2s + 1) * 2048],
                    ALU.add, ALU.add, inc="ve")
                p.wait("tensor", "ve")
                for h in range(2):
                    p.wait("tensor", "se", psb_tok[h])
                    for j in range(8):
                        p.emit("tensor", lambda e, h_=h, j_=j, t_=t2s:
                               e.transpose(PSB[h_][:, j_ * 128:(j_ + 1) * 128],
                                           TERNB[:, t_ * 2048 + (h_ * 8 + j_) * 128:
                                                 t_ * 2048 + (h_ * 8 + j_ + 1) * 128],
                                           id128b[:, :]),
                               inc=("pe" if j == 7 else None))
                    p.wait("scalar", "pe")
                    p.wait("scalar", "dS", pb_dma_tok[h])
                    SCOPY2(PbW[h][:, :], PSB[h][:, :], inc="se")
                    psb_tok[h] = p.val("se")
                    p.wait("sync", "se")
                    for gg in (2 * h, 2 * h + 1):
                        DMA("sync", unit_fn(rc, cp, gg),
                            PbW[h][:, (gg % 2) * 512:(gg % 2) * 512 + 512].rearrange(
                                "p (j q) -> p j q", q=128))
                    pb_dma_tok[h] = p.val("dS")
                t2_tok[t2s] = p.val("pe")

        def fc_unit(rc, cp, g):
            return bass.AP(tfc_scr.tensor,
                           ((rc // 4) * 4 + g) * 262144 + (rc % 4) * 512,
                           [[2048, 128], [128, 4], [1, 128]])

        def pj_unit(rc, cp, g):
            return bass.AP(tpj_scr.tensor,
                           ((rc // 4) * 16 + cp * 4 + g) * 262144 + (rc % 4) * 512,
                           [[2048, 128], [128, 4], [1, 128]])

        tern_pass(wfc_in, NOC, 1, 0, 1, fc_unit)
        tern_pass(wpj_in, NCH, 4, 3, 4, pj_unit)

        if "d_mod" in dumps:
            p.wait("sync", "ve")
            DMA("sync", dumps["d_mod"], modc[:, 0:224])
        if "d_gam" in dumps:
            p.wait("sync", "ge")
            DMA("sync", dumps["d_gam"], gam[:, 0:8])
        if "d_wscr" in dumps:
            p.wait("sync", "dS")
            DMA("sync", dumps["d_wscr"], wkvbf_scr[:, :])
        if "d_wqscr" in dumps:
            p.wait("sync", "dS")
            DMA("sync", dumps["d_wqscr"], wqbf_scr[:, :])

    def attn_qc(s, qc, do_dump):
        q0 = qc * QCW
        p.wait("sync", "ve")
        DMA("sync", cosq[:, 0:512], cos_in[:, q0:q0 + QCW])
        DMA("sync", cosq[:, 512:1024], sin_in[:, q0:q0 + QCW])

        stat_tok = [0, 0]
        stat_ve_tok = 0
        p.wait("scalar", "pe")
        for rt in range(4):
            slot = rt % 2
            if rt < 2:
                p.wait("sync", "se")
                p.wait("sync", "ve")
                p.wait("sync", "pe")
            else:
                p.wait("sync", "se", stat_tok[slot])
                p.wait("sync", "ve", stat_ve_tok)
            DMA("sync", STRF[slot][:, 0:2048],
                x_in[s, q0 + rt * 128:q0 + (rt + 1) * 128, :])
            p.wait("scalar", "dS")
            ACT(SCB[:, 0:2048], STRF[slot][:, 0:2048], AF.Square,
                accum=POOL[4][:, rt:rt + 1], inc="se")
            p.wait("vector", "dS")
            p.emit("vector", lambda e, sl=slot, r=rt: e.reduce_sum(
                POOL[4][:, 4 + r:5 + r], STRF[sl][:, 0:2048], AX.X), inc="ve")
            stat_tok[slot] = p.val("se")
            stat_ve_tok = p.val("ve")
        p.wait("tensor", "se")
        p.wait("tensor", "ve")
        for rt in range(4):
            TR(PS[4][0:1, rt * 128:(rt + 1) * 128], POOL[4][:, 4 + rt:5 + rt])
            TR(PS[5][0:1, rt * 128:(rt + 1) * 128], POOL[4][:, rt:rt + 1],
               inc=("pe" if rt == 3 else None))
        p.wait("vector", "pe")
        TS("vector", POOL[0][0:1, :], PS[4][0:1, :], 1.0 / E, None, ALU.mult)
        TT("vector", POOL[1][0:1, :], POOL[0][0:1, :], POOL[0][0:1, :], ALU.mult)
        STT("vector", POOL[1][0:1, :], PS[5][0:1, :], 1.0 / E, POOL[1][0:1, :],
            ALU.mult, ALU.subtract, inc="ve")
        p.wait("scalar", "ve")
        ACT(POOL[1][0:1, :], POOL[1][0:1, :], AF.Ln, bias=smal[0:1, 1:2])
        ACT(POOL[1][0:1, :], POOL[1][0:1, :], AF.Exp, scale=-0.5, inc="se")
        p.wait("gpsimd", "ve")
        p.wait("gpsimd", "se")
        p.emit("gpsimd", lambda e: e.partition_broadcast(POOL[2][:, :], POOL[0][0:1, :]))
        p.emit("gpsimd", lambda e: e.partition_broadcast(POOL[3][:, :], POOL[1][0:1, :]),
               inc="ge")

        ap_tok = [0, 0]
        xts_tok = [0, 0, 0]
        xts_dma_tok = [0, 0, 0]
        XTS3 = True
        p.wait("tensor", "ge")
        p.wait("vector", "ge")
        for rt in range(4):
            slot = rt % 2
            p.wait("sync", "pe", ap_tok[slot])
            DMA("sync", STRF[slot][:, 0:2048],
                x_in[s, q0 + rt * 128:q0 + (rt + 1) * 128, :])
            p.wait("tensor", "dS")
            for c in range(NCH):
                bank = PS[6 + (c % 2)]
                xsl = (rt * NCH + c) % 2
                p.wait("tensor", "ve", xts_tok[xsl])
                TR(bank[:, 0:128], STRF[slot][:, c * 128:(c + 1) * 128], inc="pe")
                p.wait("vector", "pe")
                p.wait("vector", "dS", xts_dma_tok[xsl])
                xts = POOL[4][:, xsl * 128:xsl * 128 + 128]
                COPY("vector", xts, bank[:, 0:128], inc="ve")
                p.wait("sync", "ve")
                DMA("sync",
                    bass.AP(x1_scr.tensor, s * E * S + (c * 128) * S + q0 + rt * 128,
                            [[S, 128], [1, 128]]),
                    xts)
                xts_dma_tok[xsl] = p.val("dS")
                TT("vector", POOL[4][:, 384:512], xts,
                   POOL[2][:, rt * 128:(rt + 1) * 128], ALU.subtract)
                TT("vector", POOL[4][:, 384:512], POOL[4][:, 384:512],
                   POOL[3][:, rt * 128:(rt + 1) * 128], ALU.mult)
                p.emit("vector", lambda e, c_=c, r_=rt: e.tensor_scalar(
                    hq[:, c_ * 512 + r_ * 128:c_ * 512 + (r_ + 1) * 128],
                    POOL[4][:, 384:512],
                    modc[:, mcol(1, c_, s):mcol(1, c_, s) + 1],
                    modc[:, mcol(0, c_, s):mcol(0, c_, s) + 1],
                    ALU.mult, ALU.add), inc="ve")
                xts_tok[xsl] = p.val("ve")
            ap_tok[slot] = p.val("pe")
        hq_ready = p.val("ve")
        xw_tok = p.val("dS")

        if do_dump and qc == 0 and "d_h" in dumps:
            p.wait("sync", "ve")
            DMA("sync", dumps["d_h"], hq[:, :])

        kva_tok = 0
        for ec in range(NCH):
            p.wait("sync", "pe", kva_tok)
            DMA("sync", STRM[:, 0:768], wkvabf_scr[ec * 128:(ec + 1) * 128, :])
            p.wait("tensor", "dS")
            if ec == 0:
                p.wait("tensor", "ve", hq_ready)
            rhs = hq[:, ec * 512:(ec + 1) * 512]
            for t in range(4):
                MM(PS[t][:, :], STRM[:, t * 128:(t + 1) * 128], rhs,
                   first=(ec == 0), last=(ec == NCH - 1))
            MM(PS[4][:, :], STRM[:, 512:640], rhs,
               first=(ec == 0), last=(ec == NCH - 1))
            MM(PS[5][:, :], STRM[:, 640:768], rhs,
               first=(ec == 0), last=(ec == NCH - 1), inc="pe")
            kva_tok = p.val("pe")

        p.wait("vector", "pe")
        TT("vector", POOL[0][:, :], PS[4][:, :], cosq[:, 0:512], ALU.mult)
        TT("vector", POOL[1][:, :], PS[5][:, :], cosq[:, 512:1024], ALU.mult)
        TT("vector", kpe1[:, q0:q0 + 512], POOL[0][:, :], POOL[1][:, :], ALU.subtract)
        TT("vector", POOL[0][:, :], PS[5][:, :], cosq[:, 0:512], ALU.mult)
        TT("vector", POOL[1][:, :], PS[4][:, :], cosq[:, 512:1024], ALU.mult)
        TT("vector", kpe2[:, q0:q0 + 512], POOL[0][:, :], POOL[1][:, :], ALU.add,
           inc="ve")

        p.wait("scalar", "ve")
        for i in range(4):
            ACT(SCB[:, i * 512:(i + 1) * 512], PS[i][:, :], AF.Square,
                inc=("se" if i == 3 else None))
        p.wait("tensor", "se")
        for i in range(4):
            MM(PS[6][0:32, :], ones32[:, :], SCB[:, i * 512:(i + 1) * 512],
               first=(i == 0), last=(i == 3), inc=("pe" if i == 3 else None))
        p.wait("vector", "pe")
        TS("vector", POOL[4][0:1, :], PS[6][0:1, :], 1.0 / LORA, None, ALU.mult,
           inc="ve")
        p.wait("scalar", "ve")
        ACT(POOL[4][0:1, :], POOL[4][0:1, :], AF.Ln, bias=smal[0:1, 2:3])
        ACT(POOL[4][0:1, :], POOL[4][0:1, :], AF.Exp, scale=-0.5, inc="se")
        p.wait("gpsimd", "se")
        p.emit("gpsimd", lambda e: e.partition_broadcast(POOL[2][:, :], POOL[4][0:1, :]),
               inc="ge")
        p.wait("vector", "ge")
        for i in range(4):
            TT("vector", POOL[3][:, :], PS[i][:, :], POOL[2][:, :], ALU.mult)
            p.emit("vector", lambda e, i_=i: e.tensor_scalar(
                ckvn[:, i_ * 512:(i_ + 1) * 512], POOL[3][:, :],
                kvsc[:, i_:i_ + 1], None, ALU.mult), inc="ve")
        ckv_ready = p.val("ve")

        if do_dump and qc == 0 and "d_ckv" in dumps:
            p.wait("sync", "ve")
            DMA("sync", dumps["d_ckv"], SCB[:, 0:2048])

        kb_tok = 0
        for hg in range(4):
            p.wait("sync", "pe", kb_tok)
            p.wait("sync", "ve")
            for lc in range(4):
                DMA("sync", STRM[:, lc * 512:(lc + 1) * 512],
                    wkvbf_scr[lc * 128:(lc + 1) * 128, hg * 512:(hg + 1) * 512])
            p.wait("tensor", "dS")
            if hg == 0:
                p.wait("tensor", "ve", ckv_ready)
            for hh in range(4):
                for lc in range(4):
                    MM(PS[4 + hh][:, :],
                       STRM[:, lc * 512 + hh * 128:lc * 512 + (hh + 1) * 128],
                       ckvn[:, lc * 512:(lc + 1) * 512],
                       first=(lc == 0), last=(lc == 3),
                       inc=("pe" if (hh == 3 and lc == 3) else None))
            kb_tok = p.val("pe")
            p.wait("vector", "pe")
            for hh in range(4):
                h = hg * 4 + hh
                COPY("vector", Kn[:, h * S + q0:h * S + q0 + 512], PS[4 + hh][:, :],
                     inc=("ve" if hh == 3 else None))

        v_tok = kb_tok
        for vg in range(4):
            p.wait("sync", "pe", v_tok)
            p.wait("sync", "ve")
            for lc in range(4):
                DMA("sync", STRM[:, lc * 512:(lc + 1) * 512],
                    wkvbf_scr[lc * 128:(lc + 1) * 128,
                              2048 + vg * 512:2048 + (vg + 1) * 512])
            p.wait("tensor", "dS")
            p.wait("tensor", "ve")
            for j in range(4):
                for lc in range(4):
                    MM(PS[4 + j][:, :],
                       ckvn[:, lc * 512 + j * 128:lc * 512 + (j + 1) * 128],
                       STRM[:, lc * 512:(lc + 1) * 512],
                       first=(lc == 0), last=(lc == 3),
                       inc=("pe" if (j == 3 and lc == 3) else None))
            v_tok = p.val("pe")
            p.wait("vector", "pe")
            for j in range(4):
                kb = qc * 4 + j
                COPY("vector", Vt[:, kb * 2048 + vg * 512:kb * 2048 + (vg + 1) * 512],
                     PS[4 + j][:, :], inc=("ve" if j == 3 else None))

        for j in range(4):
            p.wait("sync", "pe")
            p.wait("sync", "ve")
            DMA("sync", STRM[:, 0:2048],
                bass.AP(wqbf_scr.tensor, (16 + j) * 262144,
                        [[2048, 128], [1, 2048]]))
            p.wait("tensor", "dS")
            p.wait("tensor", "ve")
            for c in range(NCH):
                MM(PS[0][:, :], STRM[:, c * 128:(c + 1) * 128],
                   hq[:, c * 512:(c + 1) * 512],
                   first=(c == 0), last=(c == NCH - 1),
                   inc=("pe" if c == NCH - 1 else None))
            p.wait("sync", "pe")
            DMA("sync", STRM[:, 0:2048],
                bass.AP(wqbf_scr.tensor, (20 + j) * 262144,
                        [[2048, 128], [1, 2048]]))
            p.wait("tensor", "dS")
            for c in range(NCH):
                MM(PS[1][:, :], STRM[:, c * 128:(c + 1) * 128],
                   hq[:, c * 512:(c + 1) * 512],
                   first=(c == 0), last=(c == NCH - 1),
                   inc=("pe" if c == NCH - 1 else None))
            p.wait("vector", "pe")
            TT("vector", POOL[0][:, :], PS[0][:, :], cosq[:, 0:512], ALU.mult)
            TT("vector", POOL[1][:, :], PS[1][:, :], cosq[:, 512:1024], ALU.mult)
            TT("vector", qpe[:, j * 512:(j + 1) * 512], POOL[0][:, :], POOL[1][:, :],
               ALU.subtract)
            TT("vector", POOL[0][:, :], PS[1][:, :], cosq[:, 0:512], ALU.mult)
            TT("vector", POOL[1][:, :], PS[0][:, :], cosq[:, 512:1024], ALU.mult)
            TT("vector", qpe[:, (4 + j) * 512:(5 + j) * 512], POOL[0][:, :],
               POOL[1][:, :], ALU.add, inc="ve")

        if do_dump and qc == 0 and "d_qpe" in dumps:
            p.wait("sync", "ve")
            DMA("sync", dumps["d_qpe"], qpe[:, :])

        nkb = 4 * (qc + 1)
        qn_strm_tok = 0
        ps2_tok = 0
        qnbuf_tok = 0
        zo_tok = 0
        ob_free_tok = 0
        wo_strm_tok = [0, 0]
        acc_pool_tok = [0, 0]
        prev_acc_round = 0
        for h in range(16):
            j4 = h % 4
            jg = h // 4
            p.wait("sync", "pe", qn_strm_tok)
            DMA("sync", STRM[:, 0:2048],
                bass.AP(wqbf_scr.tensor, h * 262144,
                        [[2048, 128], [1, 2048]]))
            p.wait("tensor", "dS")
            p.wait("tensor", "ve", ps2_tok)
            for c in range(NCH):
                MM(PS[2][:, :], STRM[:, c * 128:(c + 1) * 128],
                   hq[:, c * 512:(c + 1) * 512],
                   first=(c == 0), last=(c == NCH - 1),
                   inc=("pe" if c == NCH - 1 else None))
            qn_strm_tok = p.val("pe")
            p.wait("vector", "pe")
            p.wait("vector", "pe", qnbuf_tok)
            COPY("vector", qn[:, 0:512], PS[2][:, :], inc="ve")
            ps2_tok = p.val("ve")
            qn_ready = p.val("ve")

            sexp_tok = [0, 0]
            zpv_tok = [0, 0]
            for step in range(nkb + 1):
                if step < nkb:
                    kb = step
                    sbank = kb % 2
                    pslot = kb % 2
                    p.wait("tensor", "se", sexp_tok[sbank])
                    if kb == 0:
                        p.wait("tensor", "ve", qn_ready)
                        if h > 0 and j4 == 0:
                            p.wait("tensor", "pe", ob_free_tok)
                    tp96 = (96, 0) if j4 == 3 else None
                    MM(PS[sbank][:, :], Kn[:, h * S + kb * 128:h * S + (kb + 1) * 128],
                       qn[:, 0:512], first=True, last=False)
                    MM(PS[sbank][:, :],
                       kpe1[j4 * 32:(j4 + 1) * 32, kb * 128:(kb + 1) * 128],
                       qpe[j4 * 32:(j4 + 1) * 32, jg * 512:(jg + 1) * 512],
                       first=False, last=False, tp=tp96)
                    MM(PS[sbank][:, :],
                       kpe2[j4 * 32:(j4 + 1) * 32, kb * 128:(kb + 1) * 128],
                       qpe[j4 * 32:(j4 + 1) * 32, (4 + jg) * 512:(5 + jg) * 512],
                       first=False, last=True, inc="pe", tp=tp96)
                    score_tok = p.val("pe")
                    p.wait("scalar", "pe", score_tok)
                    if kb >= 2:
                        p.wait("scalar", "pe", zpv_tok[pslot])
                    ACT(Pb[:, pslot * 512:(pslot + 1) * 512], PS[sbank][:, :], AF.Exp,
                        bias=smal[:, 0:1], scale=SCALE, inc="se")
                    sexp_tok[sbank] = p.val("se")
                    if kb >= qc * 4:
                        p.wait("gpsimd", "se", sexp_tok[sbank])
                        p.emit("gpsimd", lambda e, ps_=pslot, b_=q0 - kb * 128:
                               e.affine_select(
                                   out=Pb[:, ps_ * 512:(ps_ + 1) * 512],
                                   in_=Pb[:, ps_ * 512:(ps_ + 1) * 512],
                                   pattern=[[1, 512]], compare_op=ALU.is_ge,
                                   fill=0.0, base=b_, channel_multiplier=-1),
                               inc="ge")
                if step >= 1:
                    kb = step - 1
                    pslot = kb % 2
                    if kb >= qc * 4:
                        p.wait("tensor", "ge")
                    else:
                        p.wait("tensor", "se", sexp_tok[kb % 2])
                    if kb == 0:
                        p.wait("tensor", "ve", zo_tok)
                    MM(PS[6][0:32, :], ones32[:, :],
                       Pb[:, pslot * 512:(pslot + 1) * 512],
                       first=(kb == 0), last=(kb == nkb - 1))
                    MM(PS[7][:, :],
                       Vt[:, kb * 2048 + h * 128:kb * 2048 + (h + 1) * 128],
                       Pb[:, pslot * 512:(pslot + 1) * 512],
                       first=(kb == 0), last=(kb == nkb - 1), inc="pe")
                    zpv_tok[pslot] = p.val("pe")
            qnbuf_tok = p.val("pe")
            p.wait("vector", "pe")
            p.emit("vector", lambda e: e.reciprocal(POOL[0][0:1, :], PS[6][0:1, :]),
                   inc="ve")
            p.wait("gpsimd", "ve")
            p.emit("gpsimd", lambda e: e.partition_broadcast(POOL[1][:, :],
                                                             POOL[0][0:1, :]), inc="ge")
            p.wait("vector", "ge")
            if h >= 4:
                p.wait("vector", "pe", ob_free_tok)
            TT("vector", Ob[:, j4 * 512:(j4 + 1) * 512], PS[7][:, :],
               POOL[1][:, :], ALU.mult, inc="ve")
            zo_tok = p.val("ve")

            if j4 == 3:
                quarter = h // 4
                if do_dump and qc == 0 and quarter == 0 and "d_ob" in dumps:
                    p.wait("sync", "ve")
                    DMA("sync", dumps["d_ob"], Ob[:, :])
                if quarter > 0:
                    p.wait("gpsimd", "dG", prev_acc_round)
                p.wait("gpsimd", "dS", xw_tok)
                ob_done = p.val("ve")
                wo_ts_tok = [0, 0]
                for oc in range(NCH):
                    slot2 = oc % 2
                    p.wait("sync", "pe", wo_strm_tok[slot2])
                    DMA("sync", STRM[:, slot2 * 1024:slot2 * 1024 + 512],
                        bass.AP(wobf_scr.tensor, oc * 262144 + quarter * 512,
                                [[2048, 128], [1, 512]]))
                    p.wait("tensor", "dS")
                    if oc == 0:
                        p.wait("tensor", "ve", ob_done)
                    p.wait("tensor", "ve", wo_ts_tok[slot2])
                    for t in range(4):
                        MM(PS[2 + slot2][:, :],
                           STRM[:, slot2 * 1024 + t * 128:slot2 * 1024 + (t + 1) * 128],
                           Ob[:, t * 512:(t + 1) * 512],
                           first=(t == 0), last=(t == 3),
                           inc=("pe" if t == 3 else None))
                    wo_strm_tok[slot2] = p.val("pe")
                    p.wait("vector", "pe")
                    p.wait("vector", "dG", acc_pool_tok[slot2])
                    p.emit("vector", lambda e, sl=slot2, oc_=oc: e.tensor_scalar(
                        POOL[2 + sl][:, :], PS[2 + sl][:, :],
                        modc[:, mcol(2, oc_, s):mcol(2, oc_, s) + 1], None, ALU.mult),
                           inc="ve")
                    wo_ts_tok[slot2] = p.val("ve")
                    p.wait("gpsimd", "ve")
                    DMAACC(
                        bass.AP(x1_scr.tensor, s * E * S + (oc * 128) * S + q0,
                                [[S, 128], [1, 512]]),
                        POOL[2 + slot2][:, :])
                    acc_pool_tok[slot2] = p.val("dG")
                prev_acc_round = p.val("dG")
                ob_free_tok = p.val("pe")
                ps2_tok = p.val("ve")

        if do_dump and qc == NQC - 1:
            if "d_kn" in dumps:
                p.wait("sync", "ve")
                DMA("sync", dumps["d_kn"], Kn[:, :])
            if "d_vt" in dumps:
                DMA("sync", dumps["d_vt"], Vt[:, :])
            if "d_kpe" in dumps:
                DMA("sync", dumps["d_kpe"][:, 0:S], kpe1[:, :])
                DMA("sync", dumps["d_kpe"][:, S:2 * S], kpe2[:, :])

    def ffn_tt(s, tt, do_dump):
        t0 = tt * 512

        p.wait("sync", "dG")
        p.wait("sync", "dS")
        p.wait("sync", "ve")
        p.wait("sync", "pe")
        for c in range(NCH):
            DMA("sync", x1t[:, c * 512:(c + 1) * 512],
                bass.AP(x1_scr.tensor, s * E * S + (c * 128) * S + t0,
                        [[S, 128], [1, 512]]))

        p.wait("vector", "dS")
        p.wait("scalar", "dS")
        first_tok = 0
        for c in range(NCH):
            csl = c % 2
            if c >= 2:
                p.wait("vector", "pe", first_tok)
                p.wait("scalar", "pe", first_tok)
            COPY("vector", SCB[:, csl * 512:(csl + 1) * 512],
                 x1t[:, c * 512:(c + 1) * 512], inc="ve")
            ACT(SCB[:, 1024 + csl * 512:1024 + (csl + 1) * 512],
                x1t[:, c * 512:(c + 1) * 512], AF.Square, inc="se")
            p.wait("tensor", "ve")
            p.wait("tensor", "se")
            MM(PS[0][0:32, :], ones32[:, :], SCB[:, csl * 512:(csl + 1) * 512],
               first=(c == 0), last=(c == NCH - 1))
            MM(PS[1][0:32, :], ones32[:, :],
               SCB[:, 1024 + csl * 512:1024 + (csl + 1) * 512],
               first=(c == 0), last=(c == NCH - 1), inc="pe")
            first_tok = p.val("pe")

        def rowmath_bcast(ps_sum, ps_sq, n, eps_col, mu_dst, r_dst):
            p.wait("vector", "pe")
            TS("vector", POOL[0][0:1, :], ps_sum[0:1, :], 1.0 / n, None, ALU.mult)
            TT("vector", POOL[1][0:1, :], POOL[0][0:1, :], POOL[0][0:1, :], ALU.mult)
            STT("vector", POOL[1][0:1, :], ps_sq[0:1, :], 1.0 / n, POOL[1][0:1, :],
                ALU.mult, ALU.subtract, inc="ve")
            p.wait("scalar", "ve")
            ACT(POOL[1][0:1, :], POOL[1][0:1, :], AF.Ln, bias=eps_col[0:1, :])
            ACT(POOL[1][0:1, :], POOL[1][0:1, :], AF.Exp, scale=-0.5, inc="se")
            p.wait("gpsimd", "ve")
            p.wait("gpsimd", "se")
            p.emit("gpsimd", lambda e, d=mu_dst: e.partition_broadcast(d, POOL[0][0:1, :]))
            p.emit("gpsimd", lambda e, d=r_dst: e.partition_broadcast(d, POOL[1][0:1, :]),
                   inc="ge")
            p.wait("vector", "ge")

        rowmath_bcast(PS[0], PS[1], E, smal[:, 1:2], POOL[2][:, :], POOL[3][:, :])

        first_tok = 0
        for c in range(NCH):
            csl = c % 2
            if c >= 2:
                p.wait("vector", "pe", first_tok)
                p.wait("scalar", "pe", first_tok)
            TT("vector", POOL[4][:, :], x1t[:, c * 512:(c + 1) * 512], POOL[2][:, :],
               ALU.subtract)
            TT("vector", POOL[4][:, :], POOL[4][:, :], POOL[3][:, :], ALU.mult)
            p.emit("vector", lambda e, c_=c: e.tensor_scalar(
                h2t[:, c_ * 512:(c_ + 1) * 512], POOL[4][:, :],
                modc[:, mcol(4, c_, s):mcol(4, c_, s) + 1],
                modc[:, mcol(3, c_, s):mcol(3, c_, s) + 1],
                ALU.mult, ALU.add))
            COPY("vector", SCB[:, csl * 512:(csl + 1) * 512],
                 h2t[:, c * 512:(c + 1) * 512], inc="ve")
            if c == 0:
                COPY("vector", Pb[:, 0:512], SCB[:, csl * 512:(csl + 1) * 512])
                COPY("vector", Pb[:, 512:1024], SCB[:, csl * 512:(csl + 1) * 512])
            else:
                TT("vector", Pb[:, 0:512], Pb[:, 0:512],
                   SCB[:, csl * 512:(csl + 1) * 512], ALU.max)
                TT("vector", Pb[:, 512:1024], Pb[:, 512:1024],
                   SCB[:, csl * 512:(csl + 1) * 512], ALU.min)
            p.wait("scalar", "ve")
            ACT(SCB[:, 1024 + csl * 512:1024 + (csl + 1) * 512],
                h2t[:, c * 512:(c + 1) * 512], AF.Square, inc="se")
            p.wait("tensor", "ve")
            p.wait("tensor", "se")
            MM(PS[2][0:32, :], ones32[:, :], SCB[:, csl * 512:(csl + 1) * 512],
               first=(c == 0), last=(c == NCH - 1))
            MM(PS[3][0:32, :], ones32[:, :],
               SCB[:, 1024 + csl * 512:1024 + (csl + 1) * 512],
               first=(c == 0), last=(c == NCH - 1), inc="pe")
            first_tok = p.val("pe")
        rowmath_bcast(PS[2], PS[3], E, smal[:, 3:4], POOL[2][:, :], POOL[3][:, :])

        TT("vector", POOL[4][:, :], Pb[:, 0:512], POOL[2][:, :], ALU.subtract)
        TT("vector", POOL[0][:, :], POOL[2][:, :], Pb[:, 512:1024], ALU.subtract)
        TT("vector", POOL[4][:, :], POOL[4][:, :], POOL[0][:, :], ALU.max, inc="ve")
        p.wait("gpsimd", "ve")
        p.emit("gpsimd", lambda e: e.partition_all_reduce(
            POOL[1][:, :], POOL[4][:, :], 128, ReduceOp.max), inc="ge")
        p.wait("vector", "ge")
        p.emit("vector", lambda e: e.reciprocal(POOL[0][:, :], POOL[1][:, :]))
        TS("vector", POOL[0][:, :], POOL[0][:, :], 128.0, None, ALU.mult)
        TT("vector", POOL[3][:, :], POOL[3][:, :], POOL[1][:, :], ALU.mult)
        TS("vector", POOL[3][:, :], POOL[3][:, :], 1.0 / 128.0, None, ALU.mult, inc="ve")

        xh_tok = [0] * NCH
        for c in range(NCH):
            TT("vector", POOL[4][:, :], h2t[:, c * 512:(c + 1) * 512], POOL[2][:, :],
               ALU.subtract)
            TT("vector", POOL[4][:, :], POOL[4][:, :], POOL[0][:, :], ALU.mult)
            TS("vector", POOL[4][:, :], POOL[4][:, :], CLIP_HI, -CLIP_HI, ALU.min,
               ALU.max)
            TT("vector", xh[:, c * 512:(c + 1) * 512], POOL[4][:, :], POOL[3][:, :],
               ALU.mult, inc="ve")
            xh_tok[c] = p.val("ve")
        xh_ready = p.val("ve")
        if do_dump and "d_xh" in dumps:
            p.wait("sync", "ve")
            DMA("sync", dumps["d_xh"], xh[:, :])

        t4_tok = p.t4_tok
        for u in range(64):
            og, g4 = divmod(u, 4)
            buf = u % 3
            q = ("sync", "dS") if u % 2 == 0 else ("scalar", "dC")
            p.wait(q[0], "pe", t4_tok[buf])
            DMA(q[0], T4[buf][:, :],
                bass.AP(tfc_scr.tensor, (og * 4 + g4) * 262144,
                        [[2048, 128], [1, 2048]]),
                sem=q[1])
            dtok = p.val(q[1])
            p.wait("tensor", q[1], dtok)
            p.wait("tensor", "ve", xh_tok[g4 * 4 + 3])
            if g4 == 0:
                p.wait("tensor", "se")
            for j in range(4):
                d = g4 * 4 + j
                for i in range(4):
                    MM(PS[i][:, :], T4[buf][:, i * 512 + j * 128:i * 512 + (j + 1) * 128],
                       xh[:, d * 512:(d + 1) * 512],
                       first=(d == 0), last=(d == NCH - 1),
                       inc=("pe" if i == 3 else None))
            t4_tok[buf] = p.val("pe")
            if g4 == 3:
                p.wait("scalar", "pe")
                for i in range(4):
                    oc = og * 4 + i
                    ACT(gbuf[:, oc * 512:(oc + 1) * 512], PS[i][:, :], AF.Gelu,
                        scale=gam[:, 2:3], inc=("se" if i == 3 else None))
                gelu_tok = p.val("se")
                p.wait("scalar", "se", gelu_tok)
                p.wait("vector", "se", gelu_tok)
                for i in range(4):
                    oc = og * 4 + i
                    csl = i % 2
                    if og > 0 or i >= 2:
                        p.wait("scalar", "pe", p.gstat_tok)
                    ACT(SCB[:, 1024 + csl * 512:1024 + (csl + 1) * 512],
                        gbuf[:, oc * 512:(oc + 1) * 512], AF.Square, inc="se")
                    if oc == 0:
                        COPY("vector", Pb[:, 0:512], gbuf[:, 0:512])
                        COPY("vector", Pb[:, 512:1024], gbuf[:, 0:512])
                    else:
                        TT("vector", Pb[:, 0:512], Pb[:, 0:512],
                           gbuf[:, oc * 512:(oc + 1) * 512], ALU.max)
                        TT("vector", Pb[:, 512:1024], Pb[:, 512:1024],
                           gbuf[:, oc * 512:(oc + 1) * 512], ALU.min)
                    p.wait("tensor", "se")
                    MM(PS[4][0:32, :], ones32[:, :], gbuf[:, oc * 512:(oc + 1) * 512],
                       first=(oc == 0), last=(oc == NOC - 1))
                    MM(PS[5][0:32, :], ones32[:, :],
                       SCB[:, 1024 + csl * 512:1024 + (csl + 1) * 512],
                       first=(oc == 0), last=(oc == NOC - 1), inc="pe")
                    p.gstat_tok = p.val("pe")
        if do_dump and "d_g" in dumps:
            p.wait("sync", "se")
            DMA("sync", dumps["d_g"], gbuf[:, 0:NOC * 512])

        rowmath_bcast(PS[4], PS[5], DFF, smal[:, 3:4], POOL[2][:, :], POOL[3][:, :])
        TT("vector", POOL[4][:, :], Pb[:, 0:512], POOL[2][:, :], ALU.subtract)
        TT("vector", POOL[0][:, :], POOL[2][:, :], Pb[:, 512:1024], ALU.subtract)
        TT("vector", POOL[4][:, :], POOL[4][:, :], POOL[0][:, :], ALU.max, inc="ve")
        p.wait("gpsimd", "ve")
        p.emit("gpsimd", lambda e: e.partition_all_reduce(
            POOL[1][:, :], POOL[4][:, :], 128, ReduceOp.max), inc="ge")
        p.wait("vector", "ge")
        COPY("vector", SCB[:, 1024:1536], POOL[2][:, :])
        TT("vector", POOL[3][:, :], POOL[3][:, :], POOL[1][:, :], ALU.mult)
        TS("vector", POOL[3][:, :], POOL[3][:, :], 1.0 / 128.0, None, ALU.mult)
        COPY("vector", SCB[:, 0:512], POOL[3][:, :])
        p.emit("vector", lambda e: e.reciprocal(POOL[0][:, :], POOL[1][:, :]))
        TS("vector", POOL[0][:, :], POOL[0][:, :], 128.0, None, ALU.mult)
        COPY("vector", SCB[:, 512:1024], POOL[0][:, :], inc="ve")

        xh2_tok = [0] * NOC
        for oc in range(NOC):
            TT("vector", gbuf[:, oc * 512:(oc + 1) * 512],
               gbuf[:, oc * 512:(oc + 1) * 512], SCB[:, 1024:1536], ALU.subtract)
            TT("vector", gbuf[:, oc * 512:(oc + 1) * 512],
               gbuf[:, oc * 512:(oc + 1) * 512], SCB[:, 512:1024], ALU.mult)
            TS("vector", gbuf[:, oc * 512:(oc + 1) * 512],
               gbuf[:, oc * 512:(oc + 1) * 512], CLIP_HI, -CLIP_HI, ALU.min, ALU.max)
            TT("vector", gbuf[:, oc * 512:(oc + 1) * 512],
               gbuf[:, oc * 512:(oc + 1) * 512], SCB[:, 0:512], ALU.mult,
               inc="ve")
            xh2_tok[oc] = p.val("ve")

        t4_tok = p.t4_tok
        for u in range(64):
            ocg, k4 = divmod(u, 16)
            buf = u % 3
            q = ("sync", "dS") if u % 2 == 0 else ("scalar", "dC")
            p.wait(q[0], "pe", t4_tok[buf])
            DMA(q[0], T4[buf][:, :],
                bass.AP(tpj_scr.tensor, (ocg * 16 + k4) * 262144,
                        [[2048, 128], [1, 2048]]),
                sem=q[1])
            dtok = p.val(q[1])
            p.wait("tensor", q[1], dtok)
            p.wait("tensor", "ve", xh2_tok[k4 * 4 + 3])
            if k4 == 0:
                p.wait("tensor", "ve")
            for j in range(4):
                kk = k4 * 4 + j
                for i in range(4):
                    MM(PS[i][:, :], T4[buf][:, i * 512 + j * 128:i * 512 + (j + 1) * 128],
                       gbuf[:, kk * 512:(kk + 1) * 512],
                       first=(kk == 0), last=(kk == 4 * NCH - 1),
                       inc=("pe" if i == 3 else None))
            t4_tok[buf] = p.val("pe")
            if k4 == 15:
                p.wait("vector", "pe")
                for i in range(4):
                    oc = ocg * 4 + i
                    p.emit("vector", lambda e, i_=i, oc_=oc: e.scalar_tensor_tensor(
                        x1t[:, oc_ * 512:(oc_ + 1) * 512], PS[i_][:, :],
                        modc[:, 192 + oc_ * 2 + s:192 + oc_ * 2 + s + 1],
                        x1t[:, oc_ * 512:(oc_ + 1) * 512],
                        ALU.mult, ALU.add), inc=("ve" if i == 3 else None))
        out_dma_tok = 0
        p.wait("tensor", "ve")
        for rt in range(4):
            bsl = rt % 2
            for ocg4 in range(4):
                for i in range(4):
                    oc = ocg4 * 4 + i
                    TR(PS[4 + bsl][:, i * 128:(i + 1) * 128],
                       x1t[:, oc * 512 + rt * 128:oc * 512 + (rt + 1) * 128],
                       inc=("pe" if i == 3 else None))
                p.wait("vector", "pe")
                if ocg4 == 0:
                    p.wait("vector", "dS", out_dma_tok)
                COPY("vector", otm[:, ocg4 * 512:(ocg4 + 1) * 512], PS[4 + bsl][:, :],
                     inc="ve")
                p.wait("tensor", "ve")
            p.wait("sync", "ve")
            DMA("sync", out_hbm[s, t0 + rt * 128:t0 + (rt + 1) * 128, :], otm[:, :])
            out_dma_tok = p.val("dS")

    PS = []

    if "M" in phases:
        phase_M()
    ctxM.close()
    PS.extend(ctx.enter_context(nc.psum_tensor(f"ps{i}", [128, 512], F32))
              for i in range(8))
    if "A" in phases:
        for s in range(NSEQ):
            p.barrier()
            for qc in range(NQC):
                attn_qc(s, qc, s == 0)
            p.barrier()
            if s == 0 and "d_x1scr" in dumps:
                p.wait("sync", "dG")
                DMA("sync", dumps["d_x1scr"], x1_scr[0])
            if "B" in phases:
                for tt in range(NQC):
                    ffn_tt(s, tt, s == 0 and tt == 0)
                p.barrier()

    p.barrier()
    for e in ENGS:
        p.wait(e, "dS")
        p.wait(e, "dG")
        p.wait(e, "dC")

    with nc.Block() as block:
        p.replay(block)
    ctx.close()
    nc.compile()
    return nc, dumps


def _consts():
    pos = np.arange(S, dtype=np.float64)
    inv = 1.0 / (10000.0 ** (np.arange(0, ROPE_D, 2, dtype=np.float64) / ROPE_D))
    ang = np.outer(inv, pos)
    cosT = np.tile(np.cos(ang), (4, 1)).astype(np.float32)
    sinT = np.tile(np.sin(ang), (4, 1)).astype(np.float32)
    return np.ascontiguousarray(cosT), np.ascontiguousarray(sinT)


def make_in_maps(x, c, wq, wkva, wkvb, wo, kvs, wfc, wpj, wmod):
    cosT, sinT = _consts()
    base = dict(
        wq=np.ascontiguousarray(np.asarray(wq, np.float32)),
        wkva=np.ascontiguousarray(np.asarray(wkva, np.float32)),
        wkvb=np.ascontiguousarray(np.asarray(wkvb, np.float32)),
        wo=np.ascontiguousarray(np.asarray(wo, np.float32)),
        kvs=np.ascontiguousarray(np.asarray(kvs, np.float32)).reshape(1, LORA),
        wfc=np.ascontiguousarray(np.asarray(wfc, np.float32)),
        wpj=np.ascontiguousarray(np.asarray(wpj, np.float32)),
        wmod=np.ascontiguousarray(np.asarray(wmod, np.float32)),
        cosT=cosT, sinT=sinT,
    )
    x = np.ascontiguousarray(np.asarray(x, np.float32))
    c = np.ascontiguousarray(np.asarray(c, np.float32))
    in_maps = []
    for core in range(8):
        m = dict(base)
        m["x"] = np.ascontiguousarray(x[2 * core:2 * core + 2])
        m["c"] = np.ascontiguousarray(c[2 * core:2 * core + 2])
        in_maps.append(m)
    return in_maps


def kernel(L0_attn_x, L0_attn_wq, L0_attn_wkv_a, L0_attn_wkv_b, L0_attn_wo,
           L0_attn_kv_norm_scale, L0_ffn_w_fc, L0_ffn_w_proj, L0_c, L0_wmod):
    in_maps = make_in_maps(L0_attn_x, L0_c, L0_attn_wq, L0_attn_wkv_a, L0_attn_wkv_b,
                           L0_attn_wo, L0_attn_kv_norm_scale, L0_ffn_w_fc,
                           L0_ffn_w_proj, L0_wmod)
    nc, _ = build_nc()
    res = run_bass_kernel_spmd(nc, in_maps, core_ids=list(range(8)))
    out = np.concatenate([np.asarray(r["out"]) for r in res.results], axis=0)
    return out.astype(np.float32)


# version 22
# speedup vs baseline: 1.0397x; 1.0111x over previous
import contextlib
import numpy as np
import ml_dtypes

import concourse.bacc as bacc
import concourse.bass as bass
import concourse.mybir as mybir
from concourse.bass_utils import run_bass_kernel_spmd
from concourse.bass_isa import ReduceOp

F32 = mybir.dt.float32
BF16 = mybir.dt.bfloat16
U8 = mybir.dt.uint8
AF = mybir.ActivationFunctionType
ALU = mybir.AluOpType
AX = mybir.AxisListType

B, S, E, H = 16, 2048, 2048, 16
NOPE, ROPE_D, VD, QH, LORA = 128, 64, 128, 192, 512
DFF = 8192
NSEQ = 2
QCW = 512
NQC = S // QCW
NCH = E // 128
NKB = S // 128
NOC = DFF // 128
SCALE = float(QH) ** -0.5
EXPB = -8.0
EPS_LN = 1e-5
EPS_ATT = 1e-6
EPS_FFN = 1e-6
CLIP_HI = 128.0 - 1e-6

ENGS = ("sync", "gpsimd", "scalar", "tensor", "vector")


class Plan:
    def __init__(self, nc):
        self.nc = nc
        self.prog = {e: [] for e in ENGS}
        self.sems = {}

    def sem(self, name):
        h = self.nc.alloc_semaphore(name=name)
        self.sems[name] = [h, 0]

    def emit(self, eng, fn, inc=None, amt=1):
        if inc is not None:
            rec = self.sems[inc]
            rec[1] += amt
            self.prog[eng].append((fn, rec[0], amt))
        else:
            self.prog[eng].append((fn, None, 0))

    def wait(self, eng, sem, val=None):
        rec = self.sems[sem]
        h = rec[0]
        v = rec[1] if val is None else val
        if v <= 0:
            return
        self.prog[eng].append((lambda e, h=h, v=v: e.wait_ge(h, v), None, 0))

    def val(self, sem):
        return self.sems[sem][1]

    def barrier(self):
        bar = self.sems["bar"]
        h = bar[0]
        for e in ENGS:
            self.prog[e].append((lambda eng, h=h: eng.sem_inc(h, 1), None, 0))
        bar[1] += len(ENGS)
        v = bar[1]
        for e in ENGS:
            self.prog[e].append((lambda eng, h=h, v=v: eng.wait_ge(h, v), None, 0))

    def replay(self, block):
        def runner(items):
            def run(e):
                for fn, semh, amt in items:
                    ins = fn(e)
                    if semh is not None:
                        ins.then_inc(semh, amt)
            return run

        block.sync(runner(self.prog["sync"]))
        block.gpsimd(runner(self.prog["gpsimd"]))
        block.scalar(runner(self.prog["scalar"]))
        block.tensor(runner(self.prog["tensor"]))
        block.vector(runner(self.prog["vector"]))


def mcol(v, c, s_):
    return v * 32 + c * 2 + s_


def build_nc(phases=("M", "A", "B"), dump=(), nogamma=False):
    nc = bacc.Bacc("TRN2", target_bir_lowering=False)
    dump = set(dump)

    def din(name, shape, dt=F32):
        return nc.dram_tensor(name, list(shape), dt, kind="ExternalInput").ap()

    x_in = din("x", (NSEQ, S, E))
    c_in = din("c", (NSEQ, E))
    wq_in = din("wq", (E, H * QH))
    wkva_in = din("wkva", (E, LORA + ROPE_D))
    wkvb_in = din("wkvb", (LORA, H * (NOPE + VD)))
    wo_in = din("wo", (H * VD, E))
    kvs_in = din("kvs", (1, LORA))
    wfc_in = din("wfc", (DFF, E))
    wpj_in = din("wpj", (E, DFF))
    wmod_in = din("wmod", (E, 6 * E))
    cos_in = din("cosT", (128, S))
    sin_in = din("sinT", (128, S))

    out_hbm = nc.dram_tensor("out", [NSEQ, S, E], F32, kind="ExternalOutput").ap()

    def scr(name, shape, dt):
        return nc.dram_tensor(name, list(shape), dt, kind="Internal").ap()

    x1_scr = scr("x1s", (NSEQ, E, S), F32)
    wqbf_scr = scr("wqbf", (24 * NCH, 128 * 128), BF16)
    wobf_scr = scr("wobf", (NCH * NCH, 128 * 128), BF16)
    wkvbf_scr = scr("wkvbf", (LORA, H * (NOPE + VD)), BF16)
    wkvabf_scr = scr("wkvabf", (E, LORA + 256), BF16)
    tfc_scr = scr("tfc", (NCH * NOC, 128 * 128), BF16)
    tpj_scr = scr("tpj", (NOC * NCH, 128 * 128), BF16)

    dumps = {}

    def mkdump(name, shape, dt):
        dumps[name] = nc.dram_tensor(name, list(shape), dt, kind="ExternalOutput").ap()

    if "mod" in dump:
        mkdump("d_mod", (128, 224), F32)
    if "gam" in dump:
        mkdump("d_gam", (128, 8), F32)
    if "h" in dump:
        mkdump("d_h", (128, NCH * QCW), BF16)
    if "ckv" in dump:
        mkdump("d_ckv", (128, 4 * QCW), BF16)
    if "kpe" in dump:
        mkdump("d_kpe", (128, 2 * S), BF16)
    if "kn" in dump:
        mkdump("d_kn", (128, H * S), BF16)
    if "vt" in dump:
        mkdump("d_vt", (128, NKB * 2048), BF16)
    if "qpe" in dump:
        mkdump("d_qpe", (128, 8 * QCW), BF16)
    if "ob" in dump:
        mkdump("d_ob", (128, 4 * QCW), BF16)
    if "xh" in dump:
        mkdump("d_xh", (128, NCH * QCW), BF16)
    if "g" in dump:
        mkdump("d_g", (128, NOC * QCW), BF16)
    if "wscr" in dump:
        mkdump("d_wscr", (LORA, H * (NOPE + VD)), BF16)
    if "wqscr" in dump:
        mkdump("d_wqscr", (24 * NCH, 128 * 128), BF16)
    if "x1scr" in dump:
        mkdump("d_x1scr", (E, S), F32)

    ctx = contextlib.ExitStack()

    def sbt(name, shape, dt=F32):
        return ctx.enter_context(nc.sbuf_tensor(name, list(shape), dt))

    modc = sbt("modc", (128, 224))
    smal = sbt("smal", (128, 8))
    kvsc = sbt("kvsc", (128, 4))
    gam = sbt("gam", (128, 8))
    ones32 = sbt("ones32", (128, 32), BF16)
    id128 = sbt("id128", (128, 128))
    id128b = sbt("id128b", (128, 128), BF16)
    POOL = [sbt(f"pool{i}", (128, 512)) for i in range(5)]
    SCB = sbt("scb", (128, 2048), BF16)
    STRM = sbt("strm", (128, 2048), BF16)
    STRF = [sbt(f"strf{i}", (128, 2048)) for i in range(2)]

    ar1 = sbt("ar1", (128, 65536), U8)
    ar2 = sbt("ar2", (128, 4096), U8)
    ar3 = sbt("ar3", (128, 8192), U8)
    a1 = nc.lookup_mloc(ar1).addr
    a2 = nc.lookup_mloc(ar2).addr
    a3 = nc.lookup_mloc(ar3).addr

    def at(name, shape, dt, base, off):
        return nc.alloc_sbuf_tensor_at(name, list(shape), dt, offset=base + off)

    Kn = at("Kn", (128, H * S), BF16, a1, 0)
    x1t = at("x1t", (128, NCH * 512), F32, a1, 0)
    h2t = at("h2t", (128, NCH * 512), F32, a1, 32768)
    cosq = at("cosq", (128, 2 * QCW), F32, a2, 0)
    T4 = [at("t4a", (128, 2048), BF16, a2, 0)]
    qpe = at("qpe", (128, 8 * QCW), BF16, a3, 0)
    otm = at("otm", (128, 2048), F32, a3, 0)

    Vt = sbt("Vt", (128, NKB * 2048), BF16)
    kpe1 = sbt("kpe1", (128, S), BF16)
    kpe2 = sbt("kpe2", (128, S), BF16)
    T4.append(at("t4b", (128, 2048), BF16, nc.lookup_mloc(kpe1).addr, 0))
    T4.append(at("t4c", (128, 2048), BF16, nc.lookup_mloc(kpe2).addr, 0))
    KVSTG = [at("kvstg0", (128, 2048), BF16, nc.lookup_mloc(STRF[0]).addr, 0),
             at("kvstg1", (128, 2048), BF16, nc.lookup_mloc(STRF[0]).addr, 4096),
             at("kvstg2", (128, 2048), BF16, nc.lookup_mloc(STRF[1]).addr, 0),
             at("kvstg3", (128, 2048), BF16, nc.lookup_mloc(STRF[1]).addr, 4096)]
    TERNB = at("ternb", (128, 6 * 2048), BF16, nc.lookup_mloc(Vt).addr, 0)
    TERNF = [at(f"ternf{i}", (128, 2048), F32, a1, i * 8192) for i in range(4)]
    hq = sbt("hq", (128, NCH * QCW), BF16)
    qn = sbt("qn", (128, QCW), BF16)
    Pb = sbt("Pb", (128, 2 * QCW), BF16)
    Ob = sbt("Ob", (128, 4 * QCW), BF16)
    gbuf = Vt
    xh = hq
    ckvn = SCB

    ctxM = contextlib.ExitStack()
    PSM = [ctxM.enter_context(nc.psum_tensor(f"psm{i}", [128, 512], F32))
           for i in range(3)]
    PSB = [ctxM.enter_context(nc.psum_tensor(f"psb{i}", [128, 1024], BF16))
           for i in range(2)]

    p = Plan(nc)
    p.t4_tok = [0, 0, 0]
    p.kv_tok = [0, 0, 0, 0]
    p.kv_i = 0
    p.gstat_tok = 0
    for s_ in ("bar", "dS", "dG", "dC", "pe", "ve", "se", "ge"):
        p.sem(s_)

    def MM(ps_ap, lhsT, rhs, first, last=True, inc=None, tp=None):
        p.emit("tensor",
               lambda e, a=ps_ap, l=lhsT, r=rhs, f=first, q=last, t=tp:
               e.matmul(a, l, r, start=f, stop=q, tile_position=t), inc=inc)

    def TR(ps_ap, in_ap, inc=None):
        p.emit("tensor",
               lambda e, a=ps_ap, i=in_ap: e.transpose(a, i, id128[:, :]), inc=inc)

    def DMA(eng, dst, src, sem="dS", slow=False):
        p.emit(eng, lambda e, d=dst, s=src, sl=slow: e.dma_start(
            out=d, in_=s, allow_slow_non_contiguous=sl), inc=sem, amt=16)

    def DMAT(eng, dst, src, sem="dS"):
        p.emit(eng, lambda e, d=dst, s=src: e.dma_start_transpose(out=d, in_=s),
               inc=sem, amt=16)

    def DMAACC(dst, src):
        p.emit("gpsimd", lambda e, d=dst, s=src:
               e.dma_start(out=d, in_=s, accum_op=ALU.add), inc="dG", amt=16)

    def COPY(eng, dst, src, inc=None):
        p.emit(eng, lambda e, d=dst, s=src: e.tensor_copy(d, s), inc=inc)

    def TT(eng, dst, a, b, op, inc=None):
        p.emit(eng, lambda e, d=dst, x=a, y=b, o=op: e.tensor_tensor(d, x, y, o),
               inc=inc)

    def TS(eng, dst, a, s1, s2, op0, op1=None, inc=None):
        if op1 is None:
            p.emit(eng, lambda e, d=dst, x=a, u=s1, o0=op0:
                   e.tensor_scalar(d, x, u, None, o0), inc=inc)
        else:
            p.emit(eng, lambda e, d=dst, x=a, u=s1, v=s2, o0=op0, o1=op1:
                   e.tensor_scalar(d, x, u, v, o0, o1), inc=inc)

    def STT(eng, dst, in0, scalar, in1, op0, op1, inc=None):
        p.emit(eng, lambda e, d=dst, a=in0, s=scalar, b=in1, o0=op0, o1=op1:
               e.scalar_tensor_tensor(d, a, s, b, o0, o1), inc=inc)

    def ACT(dst, src, func, bias=None, scale=None, accum=None, inc=None):
        kw = {}
        if bias is not None:
            kw["bias"] = bias
        if scale is not None:
            kw["scale"] = scale
        if accum is not None:
            kw["accum_out"] = accum
        p.emit("scalar", lambda e, d=dst, s=src, f=func, k=kw:
               e.activation(d, s, f, **k), inc=inc)

    def SCOPY2(dst, src, inc=None):
        p.emit("scalar", lambda e, d=dst, s=src: e.copy(d, s), inc=inc)

    def SIGN(dst, src, bias, inc=None):
        p.emit("scalar", lambda e, d=dst, s=src, b=bias: e.sign(d, s, bias=b), inc=inc)

    p.emit("gpsimd", lambda e: e.memset(ones32[:, :], 1.0))
    p.emit("gpsimd", lambda e: e.memset(id128[:, :], 0.0))
    p.emit("gpsimd", lambda e: e.affine_select(
        out=id128[:, :], in_=id128[:, :], pattern=[[-1, 128]],
        compare_op=ALU.not_equal, fill=1.0, base=0, channel_multiplier=1))
    p.emit("gpsimd", lambda e: e.memset(id128b[:, :], 0.0))
    p.emit("gpsimd", lambda e: e.affine_select(
        out=id128b[:, :], in_=id128b[:, :], pattern=[[-1, 128]],
        compare_op=ALU.not_equal, fill=1.0, base=0, channel_multiplier=1))
    p.emit("gpsimd", lambda e: e.memset(smal[:, 0:1], EXPB))
    p.emit("gpsimd", lambda e: e.memset(smal[:, 1:2], EPS_LN))
    p.emit("gpsimd", lambda e: e.memset(smal[:, 2:3], EPS_ATT))
    p.emit("gpsimd", lambda e: e.memset(smal[:, 3:4], EPS_FFN))
    p.emit("gpsimd", lambda e: e.memset(smal[:, 4:5], 4.852030263919617), inc="ge")

    DMA("sync", kvsc[:, 0:4], bass.AP(kvs_in.tensor, 0, [[1, 128], [128, 4]]),
        slow=True)
    for s_ in range(NSEQ):
        DMA("sync", bass.AP(POOL[4], s_, [[512, 128], [2, 16]]),
            bass.AP(c_in.tensor, s_ * E, [[1, 128], [128, 16]]), slow=True)

    def phase_M():
        p.wait("scalar", "dS")
        ACT(POOL[4][:, 32:64], POOL[4][:, 0:32], AF.Silu, inc="se")
        scv = POOL[4]

        strf_tok = [0, 0]
        first = True
        p.wait("tensor", "se")
        for ec in range(NCH):
            for cb in range(8):
                slot = (ec * 8 + cb) % 2
                p.wait("sync", "pe", strf_tok[slot])
                DMA("sync", STRF[slot][:, 0:1536],
                    wmod_in[ec * 128:(ec + 1) * 128, cb * 1536:(cb + 1) * 1536])
                p.wait("tensor", "dS")
                for t in range(12):
                    cc = cb * 12 + t
                    MM(PSM[0][:, 2 * cc:2 * cc + 2],
                       STRF[slot][:, t * 128:(t + 1) * 128],
                       scv[:, 32 + 2 * ec:34 + 2 * ec],
                       first=first, last=(ec == NCH - 1),
                       inc=("pe" if t == 11 else None))
                    first = False
                strf_tok[slot] = p.val("pe")
        p.wait("vector", "pe")
        COPY("vector", modc[:, 0:192], PSM[0][:, 0:192], inc="ve")

        for v in (2, 5):
            cols = modc[:, v * 32:(v + 1) * 32]
            TS("vector", POOL[0][:, 0:32], cols, 0.25, None, ALU.max)
            TS("vector", POOL[1][:, 0:32], cols, -0.25, None, ALU.min)
            TS("vector", POOL[2][:, 0:32], cols, 0.0, None, ALU.is_ge)
            TT("vector", POOL[3][:, 0:32], POOL[0][:, 0:32], POOL[1][:, 0:32],
               ALU.subtract)
            TT("vector", POOL[3][:, 0:32], POOL[2][:, 0:32], POOL[3][:, 0:32], ALU.mult)
            TT("vector", cols, POOL[1][:, 0:32], POOL[3][:, 0:32], ALU.add)
        for v in (1, 4):
            cols = modc[:, v * 32:(v + 1) * 32]
            TS("vector", cols, cols, 1.0, None, ALU.add, inc="ve")

        def gamma_pass(w_in, nslabs, slab_cols, ps_bank, gcols, eps):
            tok = [p.val("se")] * 4
            dtok = [0, 0, 0, 0]
            mm_tok = [0, 0]
            firstg = True
            STG2 = [SCB, STRM]
            p.wait("scalar", "pe")
            p.wait("scalar", "ve")
            for i in range(nslabs):
                slot = i % 4
                p.wait("sync", "se", tok[slot])
                r0 = (i * 2048) // slab_cols
                c0 = (i * 2048) % slab_cols
                DMA("sync", TERNF[slot][:, 0:2048],
                    w_in[r0 * 128:(r0 + 1) * 128, c0:c0 + 2048])
                dtok[slot] = p.val("dS")
                sg = i % 2
                p.wait("scalar", "dS", dtok[slot])
                p.wait("scalar", "pe", mm_tok[sg])
                ACT(STG2[sg][:, 0:2048], TERNF[slot][:, 0:2048], AF.Abs, inc="se")
                tok[slot] = p.val("se")
                p.wait("tensor", "se")
                for t in range(4):
                    MM(ps_bank[0:32, :], ones32[:, :],
                       STG2[sg][:, t * 512:(t + 1) * 512],
                       first=firstg, last=(i == nslabs - 1 and t == 3),
                       inc=("pe" if t == 3 else None))
                    firstg = False
                mm_tok[sg] = p.val("pe")
            p.wait("vector", "pe")
            p.emit("vector", lambda e, pb=ps_bank: e.reduce_sum(
                POOL[0][0:1, 0:1], pb[0:1, :], AX.X), inc="ve")
            n_el = nslabs * 2048 * 128
            p.wait("gpsimd", "ve")
            TS("gpsimd", POOL[0][0:1, 1:2], POOL[0][0:1, 0:1], 1.0 / n_el, None, ALU.mult)
            TS("gpsimd", POOL[0][0:1, 2:3], POOL[0][0:1, 1:2], eps, 0.5, ALU.add, ALU.mult)
            TS("gpsimd", POOL[0][0:1, 3:4], POOL[0][0:1, 2:3], -1.0, None, ALU.mult)
            TS("gpsimd", POOL[0][0:1, 4:5], POOL[0][0:1, 1:2], 0.5, None, ALU.mult)
            p.emit("gpsimd", lambda e, g=gcols[0]: e.partition_broadcast(
                gam[:, g:g + 1], POOL[0][0:1, 3:4]))
            p.emit("gpsimd", lambda e, g=gcols[1]: e.partition_broadcast(
                gam[:, g:g + 1], POOL[0][0:1, 2:3]))
            p.emit("gpsimd", lambda e, g=gcols[2]: e.partition_broadcast(
                gam[:, g:g + 1], POOL[0][0:1, 4:5]), inc="ge")
            p.wait("vector", "ge")

        if not nogamma:
            gamma_pass(wfc_in, 64, E, PSM[1], (0, 1, 2), EPS_FFN)
            gamma_pass(wpj_in, 64, DFF, PSM[2], (3, 4, 5), EPS_FFN)

        p.wait("vector", "ge")
        p.emit("vector", lambda e: e.tensor_scalar(
            modc[:, 192:224], modc[:, 160:192], gam[:, 5:6], None, ALU.mult), inc="ve")

        p.wait("vector", "pe")
        p.wait("vector", "ge")
        p.wait("sync", "ge")
        p.wait("sync", "se")
        STG = [STRM, SCB]
        strf_tok = [0, 0, 0, 0]
        strf_dma = [0, 0, 0, 0]
        stg_out = [0, 0]

        def cast_unit(i_u, in_src, n_in, copies_fn, outs_fn):
            slot = i_u % 4
            sg = i_u % 2
            p.wait("sync", "ve", strf_tok[slot])
            DMA("sync", TERNF[slot][:, 0:n_in], in_src)
            strf_dma[slot] = p.val("dS")
            p.wait("vector", "dS", max(strf_dma[slot], stg_out[sg]))
            copies_fn(TERNF[slot], STG[sg])
            strf_tok[slot] = p.val("ve")
            p.wait("sync", "ve", strf_tok[slot])
            outs_fn(STG[sg])
            stg_out[sg] = p.val("dS")

        i_u = 0
        for c in range(NCH):
            for half in range(2):
                def cps(sf, sg):
                    COPY("vector", sg[:, 0:1024].rearrange("p (h d) -> p h d", d=128),
                         bass.AP(sf, 0, [[2048, 128], [192, 8], [1, 128]]))
                    COPY("vector", sg[:, 1024:1280].rearrange("p (h f) -> p h f", f=32),
                         bass.AP(sf, 128, [[2048, 128], [192, 8], [1, 32]]))
                    COPY("vector", sg[:, 1280:1536].rearrange("p (h f) -> p h f", f=32),
                         bass.AP(sf, 160, [[2048, 128], [192, 8], [1, 32]]), inc="ve")

                def outs(sg, c=c, half=half):
                    h0 = half * 8
                    DMA("sync",
                        bass.AP(wqbf_scr.tensor, h0 * 262144 + c * 128,
                                [[2048, 128], [262144, 8], [1, 128]]),
                        sg[:, 0:1024].rearrange("p (h d) -> p h d", d=128))
                    DMA("sync",
                        bass.AP(wqbf_scr.tensor, (16 + 2 * half) * 262144 + c * 128,
                                [[2048, 128], [262144, 2], [1, 128]]),
                        sg[:, 1024:1280].rearrange("p (j d) -> p j d", d=128))
                    DMA("sync",
                        bass.AP(wqbf_scr.tensor, (20 + 2 * half) * 262144 + c * 128,
                                [[2048, 128], [262144, 2], [1, 128]]),
                        sg[:, 1280:1536].rearrange("p (j d) -> p j d", d=128))
                cast_unit(i_u, wq_in[c * 128:(c + 1) * 128,
                                    half * 1536:(half + 1) * 1536], 1536, cps, outs)
                i_u += 1

        for hv in range(NCH):
            def cps(sf, sg):
                COPY("vector", sg[:, 0:2048], sf[:, 0:2048], inc="ve")

            def outs(sg, hv=hv):
                DMA("sync",
                    bass.AP(wobf_scr.tensor, hv * 128,
                            [[2048, 128], [262144, 16], [1, 128]]),
                    sg[:, 0:2048].rearrange("p (o d) -> p o d", d=128))
            cast_unit(i_u, wo_in[hv * 128:(hv + 1) * 128, :], 2048, cps, outs)
            i_u += 1

        for lc in range(4):
            for piece in range(2):
                def cps(sf, sg):
                    COPY("vector", sg[:, 0:1024].rearrange("p (h d) -> p h d", d=128),
                         bass.AP(sf, 0, [[2048, 128], [256, 8], [1, 128]]))
                    COPY("vector", sg[:, 1024:2048].rearrange("p (h d) -> p h d", d=128),
                         bass.AP(sf, 128, [[2048, 128], [256, 8], [1, 128]]), inc="ve")

                def outs(sg, lc=lc, piece=piece):
                    DMA("sync", wkvbf_scr[lc * 128:(lc + 1) * 128,
                                          piece * 1024:piece * 1024 + 1024],
                        sg[:, 0:1024])
                    DMA("sync", wkvbf_scr[lc * 128:(lc + 1) * 128,
                                          2048 + piece * 1024:2048 + piece * 1024 + 1024],
                        sg[:, 1024:2048])
                cast_unit(i_u, wkvb_in[lc * 128:(lc + 1) * 128,
                                       piece * 2048:(piece + 1) * 2048], 2048, cps, outs)
                i_u += 1

        for ec in range(NCH):
            def cps(sf, sg):
                COPY("vector", sg[:, 0:512], sf[:, 0:512])
                for r in range(4):
                    COPY("vector", sg[:, 512 + r * 32:544 + r * 32], sf[:, 512:544])
                    COPY("vector", sg[:, 640 + r * 32:672 + r * 32], sf[:, 544:576],
                         inc=("ve" if r == 3 else None))

            def outs(sg, ec=ec):
                DMA("sync", wkvabf_scr[ec * 128:(ec + 1) * 128, :], sg[:, 0:768])
            cast_unit(i_u, wkva_in[ec * 128:(ec + 1) * 128, :], 576, cps, outs)
            i_u += 1


        def tern_pass(w_in, rows_chunks, col_pieces, gneg_col, gpos_col, unit_fn):
            slab_tok = [p.val("ve")] * 4
            slab_dma = [0, 0, 0, 0]
            t2_tok = [p.val("pe")] * 4
            psb_tok = [p.val("se")] * 2
            pb_dma_tok = [p.val("dS")] * 2
            PbW = [STRM[:, 0:1024], STRM[:, 1024:2048]]
            units = [(rc, cp) for rc in range(rows_chunks) for cp in range(col_pieces)]
            n_u = len(units)
            for i_s in range(n_u):
                rc, cp = units[i_s]
                slot = i_s % 4
                p.wait("sync", "ve", slab_tok[slot])
                DMA("sync", TERNF[slot][:, 0:2048],
                    w_in[rc * 128:(rc + 1) * 128, cp * 2048:(cp + 1) * 2048])
                slab_dma[slot] = p.val("dS")
                t2s = i_s % 4
                p.wait("vector", "dS", slab_dma[slot])
                p.wait("vector", "pe", t2_tok[t2s])
                TS("vector", TERNB[:, 8192:10240],
                   TERNF[slot][:, 0:2048], gam[:, gpos_col:gpos_col + 1], 2.0,
                   ALU.is_ge, ALU.mult)
                TS("vector", TERNB[:, 10240:12288],
                   TERNF[slot][:, 0:2048], gam[:, gneg_col:gneg_col + 1], 2.0,
                   ALU.is_ge, ALU.mult, inc="ve")
                slab_tok[slot] = p.val("ve")
                STT("vector", TERNB[:, t2s * 2048:(t2s + 1) * 2048],
                    TERNB[:, 8192:10240], -2.0,
                    TERNB[:, 10240:12288],
                    ALU.add, ALU.add, inc="ve")
                p.wait("tensor", "ve")
                for h in range(2):
                    p.wait("tensor", "se", psb_tok[h])
                    for j in range(8):
                        p.emit("tensor", lambda e, h_=h, j_=j, t_=t2s:
                               e.transpose(PSB[h_][:, j_ * 128:(j_ + 1) * 128],
                                           TERNB[:, t_ * 2048 + (h_ * 8 + j_) * 128:
                                                 t_ * 2048 + (h_ * 8 + j_ + 1) * 128],
                                           id128b[:, :]),
                               inc=("pe" if j == 7 else None))
                    p.wait("scalar", "pe")
                    p.wait("scalar", "dS", pb_dma_tok[h])
                    SCOPY2(PbW[h][:, :], PSB[h][:, :], inc="se")
                    psb_tok[h] = p.val("se")
                    p.wait("sync", "se")
                    for gg in (2 * h, 2 * h + 1):
                        DMA("sync", unit_fn(rc, cp, gg),
                            PbW[h][:, (gg % 2) * 512:(gg % 2) * 512 + 512].rearrange(
                                "p (j q) -> p j q", q=128))
                    pb_dma_tok[h] = p.val("dS")
                t2_tok[t2s] = p.val("pe")

        def fc_unit(rc, cp, g):
            return bass.AP(tfc_scr.tensor,
                           ((rc // 4) * 4 + g) * 262144 + (rc % 4) * 512,
                           [[2048, 128], [128, 4], [1, 128]])

        def pj_unit(rc, cp, g):
            return bass.AP(tpj_scr.tensor,
                           ((rc // 4) * 16 + cp * 4 + g) * 262144 + (rc % 4) * 512,
                           [[2048, 128], [128, 4], [1, 128]])

        tern_pass(wfc_in, NOC, 1, 0, 1, fc_unit)
        tern_pass(wpj_in, NCH, 4, 3, 4, pj_unit)

        if "d_mod" in dumps:
            p.wait("sync", "ve")
            DMA("sync", dumps["d_mod"], modc[:, 0:224])
        if "d_gam" in dumps:
            p.wait("sync", "ge")
            DMA("sync", dumps["d_gam"], gam[:, 0:8])
        if "d_wscr" in dumps:
            p.wait("sync", "dS")
            DMA("sync", dumps["d_wscr"], wkvbf_scr[:, :])
        if "d_wqscr" in dumps:
            p.wait("sync", "dS")
            DMA("sync", dumps["d_wqscr"], wqbf_scr[:, :])

    def attn_qc(s, qc, do_dump):
        q0 = qc * QCW
        p.wait("sync", "ve")
        DMA("sync", cosq[:, 0:512], cos_in[:, q0:q0 + QCW])
        DMA("sync", cosq[:, 512:1024], sin_in[:, q0:q0 + QCW])

        stat_tok = [0, 0]
        stat_ve_tok = 0
        p.wait("scalar", "pe")
        for rt in range(4):
            slot = rt % 2
            if rt < 2:
                p.wait("sync", "se")
                p.wait("sync", "ve")
                p.wait("sync", "pe")
            else:
                p.wait("sync", "se", stat_tok[slot])
                p.wait("sync", "ve", stat_ve_tok)
            DMA("sync", STRF[slot][:, 0:2048],
                x_in[s, q0 + rt * 128:q0 + (rt + 1) * 128, :])
            p.wait("scalar", "dS")
            ACT(SCB[:, 0:2048], STRF[slot][:, 0:2048], AF.Square,
                accum=POOL[4][:, rt:rt + 1], inc="se")
            p.wait("vector", "dS")
            p.emit("vector", lambda e, sl=slot, r=rt: e.reduce_sum(
                POOL[4][:, 4 + r:5 + r], STRF[sl][:, 0:2048], AX.X), inc="ve")
            stat_tok[slot] = p.val("se")
            stat_ve_tok = p.val("ve")
        p.wait("tensor", "se")
        p.wait("tensor", "ve")
        for rt in range(4):
            TR(PS[4][0:1, rt * 128:(rt + 1) * 128], POOL[4][:, 4 + rt:5 + rt])
            TR(PS[5][0:1, rt * 128:(rt + 1) * 128], POOL[4][:, rt:rt + 1],
               inc=("pe" if rt == 3 else None))
        p.wait("vector", "pe")
        TS("vector", POOL[0][0:1, :], PS[4][0:1, :], 1.0 / E, None, ALU.mult)
        TT("vector", POOL[1][0:1, :], POOL[0][0:1, :], POOL[0][0:1, :], ALU.mult)
        STT("vector", POOL[1][0:1, :], PS[5][0:1, :], 1.0 / E, POOL[1][0:1, :],
            ALU.mult, ALU.subtract, inc="ve")
        p.wait("scalar", "ve")
        ACT(POOL[1][0:1, :], POOL[1][0:1, :], AF.Ln, bias=smal[0:1, 1:2])
        ACT(POOL[1][0:1, :], POOL[1][0:1, :], AF.Exp, scale=-0.5, inc="se")
        p.wait("gpsimd", "ve")
        p.wait("gpsimd", "se")
        p.emit("gpsimd", lambda e: e.partition_broadcast(POOL[2][:, :], POOL[0][0:1, :]))
        p.emit("gpsimd", lambda e: e.partition_broadcast(POOL[3][:, :], POOL[1][0:1, :]),
               inc="ge")

        ap_tok = [0, 0]
        xts_tok = [0, 0, 0]
        xts_dma_tok = [0, 0, 0]
        XTS3 = True
        p.wait("tensor", "ge")
        p.wait("vector", "ge")
        for rt in range(4):
            slot = rt % 2
            p.wait("sync", "pe", ap_tok[slot])
            DMA("sync", STRF[slot][:, 0:2048],
                x_in[s, q0 + rt * 128:q0 + (rt + 1) * 128, :])
            p.wait("tensor", "dS")
            for c in range(NCH):
                bank = PS[6 + (c % 2)]
                xsl = (rt * NCH + c) % 2
                p.wait("tensor", "ve", xts_tok[xsl])
                TR(bank[:, 0:128], STRF[slot][:, c * 128:(c + 1) * 128], inc="pe")
                p.wait("vector", "pe")
                p.wait("vector", "dS", xts_dma_tok[xsl])
                xts = POOL[4][:, xsl * 128:xsl * 128 + 128]
                COPY("vector", xts, bank[:, 0:128], inc="ve")
                p.wait("sync", "ve")
                DMA("sync",
                    bass.AP(x1_scr.tensor, s * E * S + (c * 128) * S + q0 + rt * 128,
                            [[S, 128], [1, 128]]),
                    xts)
                xts_dma_tok[xsl] = p.val("dS")
                TT("vector", POOL[4][:, 384:512], xts,
                   POOL[2][:, rt * 128:(rt + 1) * 128], ALU.subtract)
                TT("vector", POOL[4][:, 384:512], POOL[4][:, 384:512],
                   POOL[3][:, rt * 128:(rt + 1) * 128], ALU.mult)
                p.emit("vector", lambda e, c_=c, r_=rt: e.tensor_scalar(
                    hq[:, c_ * 512 + r_ * 128:c_ * 512 + (r_ + 1) * 128],
                    POOL[4][:, 384:512],
                    modc[:, mcol(1, c_, s):mcol(1, c_, s) + 1],
                    modc[:, mcol(0, c_, s):mcol(0, c_, s) + 1],
                    ALU.mult, ALU.add), inc="ve")
                xts_tok[xsl] = p.val("ve")
            ap_tok[slot] = p.val("pe")
        hq_ready = p.val("ve")
        xw_tok = p.val("dS")

        if do_dump and qc == 0 and "d_h" in dumps:
            p.wait("sync", "ve")
            DMA("sync", dumps["d_h"], hq[:, :])

        kva_tok2 = [0, 0]
        for ec in range(NCH):
            kbase = (ec % 2) * 1024
            p.wait("sync", "pe", kva_tok2[ec % 2])
            DMA("sync", STRM[:, kbase:kbase + 768],
                wkvabf_scr[ec * 128:(ec + 1) * 128, :])
            p.wait("tensor", "dS")
            if ec == 0:
                p.wait("tensor", "ve", hq_ready)
            rhs = hq[:, ec * 512:(ec + 1) * 512]
            for t in range(4):
                MM(PS[t][:, :], STRM[:, kbase + t * 128:kbase + (t + 1) * 128], rhs,
                   first=(ec == 0), last=(ec == NCH - 1))
            MM(PS[4][:, :], STRM[:, kbase + 512:kbase + 640], rhs,
               first=(ec == 0), last=(ec == NCH - 1))
            MM(PS[5][:, :], STRM[:, kbase + 640:kbase + 768], rhs,
               first=(ec == 0), last=(ec == NCH - 1), inc="pe")
            kva_tok2[ec % 2] = p.val("pe")

        p.wait("vector", "pe")
        TT("vector", POOL[0][:, :], PS[4][:, :], cosq[:, 0:512], ALU.mult)
        TT("vector", POOL[1][:, :], PS[5][:, :], cosq[:, 512:1024], ALU.mult)
        TT("vector", kpe1[:, q0:q0 + 512], POOL[0][:, :], POOL[1][:, :], ALU.subtract)
        TT("vector", POOL[0][:, :], PS[5][:, :], cosq[:, 0:512], ALU.mult)
        TT("vector", POOL[1][:, :], PS[4][:, :], cosq[:, 512:1024], ALU.mult)
        TT("vector", kpe2[:, q0:q0 + 512], POOL[0][:, :], POOL[1][:, :], ALU.add,
           inc="ve")

        p.wait("scalar", "ve")
        for i in range(4):
            ACT(SCB[:, i * 512:(i + 1) * 512], PS[i][:, :], AF.Square,
                inc=("se" if i == 3 else None))
        p.wait("tensor", "se")
        for i in range(4):
            MM(PS[6][0:32, :], ones32[:, :], SCB[:, i * 512:(i + 1) * 512],
               first=(i == 0), last=(i == 3), inc=("pe" if i == 3 else None))
        p.wait("vector", "pe")
        TS("vector", POOL[4][0:1, :], PS[6][0:1, :], 1.0 / LORA, None, ALU.mult,
           inc="ve")
        p.wait("scalar", "ve")
        ACT(POOL[4][0:1, :], POOL[4][0:1, :], AF.Ln, bias=smal[0:1, 2:3])
        ACT(POOL[4][0:1, :], POOL[4][0:1, :], AF.Exp, scale=-0.5, inc="se")
        p.wait("gpsimd", "se")
        p.emit("gpsimd", lambda e: e.partition_broadcast(POOL[2][:, :], POOL[4][0:1, :]),
               inc="ge")
        p.wait("vector", "ge")
        for i in range(4):
            TT("vector", POOL[3][:, :], PS[i][:, :], POOL[2][:, :], ALU.mult)
            p.emit("vector", lambda e, i_=i: e.tensor_scalar(
                ckvn[:, i_ * 512:(i_ + 1) * 512], POOL[3][:, :],
                kvsc[:, i_:i_ + 1], None, ALU.mult), inc="ve")
        ckv_ready = p.val("ve")

        if do_dump and qc == 0 and "d_ckv" in dumps:
            p.wait("sync", "ve")
            DMA("sync", dumps["d_ckv"], SCB[:, 0:2048])

        p.wait("sync", "ve", ckv_ready)
        kb_tok = 0
        for hg in range(4):
            slot = p.kv_i % 4
            p.kv_i += 1
            stg = KVSTG[slot]
            p.wait("sync", "pe", p.kv_tok[slot])
            for lc in range(4):
                DMA("sync", stg[:, lc * 512:(lc + 1) * 512],
                    wkvbf_scr[lc * 128:(lc + 1) * 128, hg * 512:(hg + 1) * 512])
            p.wait("tensor", "dS")
            if hg == 0:
                p.wait("tensor", "ve", ckv_ready)
            else:
                p.wait("tensor", "ve", kev_tok)
            for hh in range(4):
                for lc in range(4):
                    MM(PS[4 + hh][:, :],
                       stg[:, lc * 512 + hh * 128:lc * 512 + (hh + 1) * 128],
                       ckvn[:, lc * 512:(lc + 1) * 512],
                       first=(lc == 0), last=(lc == 3),
                       inc=("pe" if (hh == 3 and lc == 3) else None))
            p.kv_tok[slot] = kb_tok = p.val("pe")
            p.wait("vector", "pe")
            for hh in range(4):
                h = hg * 4 + hh
                COPY("vector", Kn[:, h * S + q0:h * S + q0 + 512], PS[4 + hh][:, :],
                     inc=("ve" if hh == 3 else None))
            kev_tok = p.val("ve")

        v_tok = kb_tok
        for vg in range(4):
            slot = p.kv_i % 4
            p.kv_i += 1
            stg = KVSTG[slot]
            p.wait("sync", "pe", max(p.kv_tok[slot], v_tok))
            for lc in range(4):
                DMA("sync", stg[:, lc * 512:(lc + 1) * 512],
                    wkvbf_scr[lc * 128:(lc + 1) * 128,
                              2048 + vg * 512:2048 + (vg + 1) * 512])
            p.wait("tensor", "dS")
            p.wait("tensor", "ve")
            for j in range(4):
                for lc in range(4):
                    MM(PS[4 + j][:, :],
                       ckvn[:, lc * 512 + j * 128:lc * 512 + (j + 1) * 128],
                       stg[:, lc * 512:(lc + 1) * 512],
                       first=(lc == 0), last=(lc == 3),
                       inc=("pe" if (j == 3 and lc == 3) else None))
            p.kv_tok[slot] = v_tok = p.val("pe")
            p.wait("vector", "pe")
            for j in range(4):
                kb = qc * 4 + j
                COPY("vector", Vt[:, kb * 2048 + vg * 512:kb * 2048 + (vg + 1) * 512],
                     PS[4 + j][:, :], inc=("ve" if j == 3 else None))

        for j in range(4):
            p.wait("sync", "pe")
            p.wait("sync", "ve")
            DMA("sync", STRM[:, 0:2048],
                bass.AP(wqbf_scr.tensor, (16 + j) * 262144,
                        [[2048, 128], [1, 2048]]))
            p.wait("tensor", "dS")
            p.wait("tensor", "ve")
            for c in range(NCH):
                MM(PS[0][:, :], STRM[:, c * 128:(c + 1) * 128],
                   hq[:, c * 512:(c + 1) * 512],
                   first=(c == 0), last=(c == NCH - 1),
                   inc=("pe" if c == NCH - 1 else None))
            p.wait("sync", "pe")
            DMA("sync", STRM[:, 0:2048],
                bass.AP(wqbf_scr.tensor, (20 + j) * 262144,
                        [[2048, 128], [1, 2048]]))
            p.wait("tensor", "dS")
            for c in range(NCH):
                MM(PS[1][:, :], STRM[:, c * 128:(c + 1) * 128],
                   hq[:, c * 512:(c + 1) * 512],
                   first=(c == 0), last=(c == NCH - 1),
                   inc=("pe" if c == NCH - 1 else None))
            p.wait("vector", "pe")
            TT("vector", POOL[0][:, :], PS[0][:, :], cosq[:, 0:512], ALU.mult)
            TT("vector", POOL[1][:, :], PS[1][:, :], cosq[:, 512:1024], ALU.mult)
            TT("vector", qpe[:, j * 512:(j + 1) * 512], POOL[0][:, :], POOL[1][:, :],
               ALU.subtract)
            TT("vector", POOL[0][:, :], PS[1][:, :], cosq[:, 0:512], ALU.mult)
            TT("vector", POOL[1][:, :], PS[0][:, :], cosq[:, 512:1024], ALU.mult)
            TT("vector", qpe[:, (4 + j) * 512:(5 + j) * 512], POOL[0][:, :],
               POOL[1][:, :], ALU.add, inc="ve")

        if do_dump and qc == 0 and "d_qpe" in dumps:
            p.wait("sync", "ve")
            DMA("sync", dumps["d_qpe"], qpe[:, :])

        nkb = 4 * (qc + 1)
        qn_strm_tok = 0
        ps2_tok = 0
        qnbuf_tok = 0
        zo_tok = 0
        ob_free_tok = 0
        wo_strm_tok = [0, 0]
        acc_pool_tok = [0, 0]
        prev_acc_round = 0
        for h in range(16):
            j4 = h % 4
            jg = h // 4
            p.wait("sync", "pe", qn_strm_tok)
            DMA("sync", STRM[:, 0:2048],
                bass.AP(wqbf_scr.tensor, h * 262144,
                        [[2048, 128], [1, 2048]]))
            p.wait("tensor", "dS")
            p.wait("tensor", "ve", ps2_tok)
            for c in range(NCH):
                MM(PS[2][:, :], STRM[:, c * 128:(c + 1) * 128],
                   hq[:, c * 512:(c + 1) * 512],
                   first=(c == 0), last=(c == NCH - 1),
                   inc=("pe" if c == NCH - 1 else None))
            qn_strm_tok = p.val("pe")
            p.wait("vector", "pe")
            p.wait("vector", "pe", qnbuf_tok)
            COPY("vector", qn[:, 0:512], PS[2][:, :], inc="ve")
            ps2_tok = p.val("ve")
            qn_ready = p.val("ve")

            sexp_tok = [0, 0]
            zpv_tok = [0, 0]
            for step in range(nkb + 1):
                if step < nkb:
                    kb = step
                    sbank = kb % 2
                    pslot = kb % 2
                    p.wait("tensor", "se", sexp_tok[sbank])
                    if kb == 0:
                        p.wait("tensor", "ve", qn_ready)
                        if h > 0 and j4 == 0:
                            p.wait("tensor", "pe", ob_free_tok)
                    tp96 = (96, 0) if j4 == 3 else None
                    MM(PS[sbank][:, :], Kn[:, h * S + kb * 128:h * S + (kb + 1) * 128],
                       qn[:, 0:512], first=True, last=False)
                    MM(PS[sbank][:, :],
                       kpe1[j4 * 32:(j4 + 1) * 32, kb * 128:(kb + 1) * 128],
                       qpe[j4 * 32:(j4 + 1) * 32, jg * 512:(jg + 1) * 512],
                       first=False, last=False, tp=tp96)
                    MM(PS[sbank][:, :],
                       kpe2[j4 * 32:(j4 + 1) * 32, kb * 128:(kb + 1) * 128],
                       qpe[j4 * 32:(j4 + 1) * 32, (4 + jg) * 512:(5 + jg) * 512],
                       first=False, last=True, inc="pe", tp=tp96)
                    score_tok = p.val("pe")
                    p.wait("scalar", "pe", score_tok)
                    if kb >= 2:
                        p.wait("scalar", "pe", zpv_tok[pslot])
                    ACT(Pb[:, pslot * 512:(pslot + 1) * 512], PS[sbank][:, :], AF.Exp,
                        bias=smal[:, 0:1], scale=SCALE, inc="se")
                    sexp_tok[sbank] = p.val("se")
                    if kb >= qc * 4:
                        p.wait("gpsimd", "se", sexp_tok[sbank])
                        p.emit("gpsimd", lambda e, ps_=pslot, b_=q0 - kb * 128:
                               e.affine_select(
                                   out=Pb[:, ps_ * 512:(ps_ + 1) * 512],
                                   in_=Pb[:, ps_ * 512:(ps_ + 1) * 512],
                                   pattern=[[1, 512]], compare_op=ALU.is_ge,
                                   fill=0.0, base=b_, channel_multiplier=-1),
                               inc="ge")
                if step >= 1:
                    kb = step - 1
                    pslot = kb % 2
                    if kb >= qc * 4:
                        p.wait("tensor", "ge")
                    else:
                        p.wait("tensor", "se", sexp_tok[kb % 2])
                    if kb == 0:
                        p.wait("tensor", "ve", zo_tok)
                    MM(PS[6][0:32, :], ones32[:, :],
                       Pb[:, pslot * 512:(pslot + 1) * 512],
                       first=(kb == 0), last=(kb == nkb - 1))
                    MM(PS[7][:, :],
                       Vt[:, kb * 2048 + h * 128:kb * 2048 + (h + 1) * 128],
                       Pb[:, pslot * 512:(pslot + 1) * 512],
                       first=(kb == 0), last=(kb == nkb - 1), inc="pe")
                    zpv_tok[pslot] = p.val("pe")
            qnbuf_tok = p.val("pe")
            p.wait("scalar", "pe")
            ACT(POOL[0][0:1, :], PS[6][0:1, :], AF.Ln)
            ACT(POOL[0][0:1, :], POOL[0][0:1, :], AF.Exp, scale=-1.0, inc="se")
            p.wait("gpsimd", "se")
            p.emit("gpsimd", lambda e: e.partition_broadcast(POOL[1][:, :],
                                                             POOL[0][0:1, :]), inc="ge")
            p.wait("vector", "ge")
            if h >= 4:
                p.wait("vector", "pe", ob_free_tok)
            TT("vector", Ob[:, j4 * 512:(j4 + 1) * 512], PS[7][:, :],
               POOL[1][:, :], ALU.mult, inc="ve")
            zo_tok = p.val("ve")

            if j4 == 3:
                quarter = h // 4
                if do_dump and qc == 0 and quarter == 0 and "d_ob" in dumps:
                    p.wait("sync", "ve")
                    DMA("sync", dumps["d_ob"], Ob[:, :])
                if quarter > 0:
                    p.wait("gpsimd", "dG", prev_acc_round)
                p.wait("gpsimd", "dS", xw_tok)
                ob_done = p.val("ve")
                wo_ts_tok = [0, 0]
                for oc in range(NCH):
                    slot2 = oc % 2
                    p.wait("sync", "pe", wo_strm_tok[slot2])
                    DMA("sync", STRM[:, slot2 * 1024:slot2 * 1024 + 512],
                        bass.AP(wobf_scr.tensor, oc * 262144 + quarter * 512,
                                [[2048, 128], [1, 512]]))
                    p.wait("tensor", "dS")
                    if oc == 0:
                        p.wait("tensor", "ve", ob_done)
                    p.wait("tensor", "ve", wo_ts_tok[slot2])
                    for t in range(4):
                        MM(PS[2 + slot2][:, :],
                           STRM[:, slot2 * 1024 + t * 128:slot2 * 1024 + (t + 1) * 128],
                           Ob[:, t * 512:(t + 1) * 512],
                           first=(t == 0), last=(t == 3),
                           inc=("pe" if t == 3 else None))
                    wo_strm_tok[slot2] = p.val("pe")
                    p.wait("vector", "pe")
                    p.wait("vector", "dG", acc_pool_tok[slot2])
                    p.emit("vector", lambda e, sl=slot2, oc_=oc: e.tensor_scalar(
                        POOL[2 + sl][:, :], PS[2 + sl][:, :],
                        modc[:, mcol(2, oc_, s):mcol(2, oc_, s) + 1], None, ALU.mult),
                           inc="ve")
                    wo_ts_tok[slot2] = p.val("ve")
                    p.wait("gpsimd", "ve")
                    DMAACC(
                        bass.AP(x1_scr.tensor, s * E * S + (oc * 128) * S + q0,
                                [[S, 128], [1, 512]]),
                        POOL[2 + slot2][:, :])
                    acc_pool_tok[slot2] = p.val("dG")
                prev_acc_round = p.val("dG")
                ob_free_tok = p.val("pe")
                ps2_tok = p.val("ve")

        if do_dump and qc == NQC - 1:
            if "d_kn" in dumps:
                p.wait("sync", "ve")
                DMA("sync", dumps["d_kn"], Kn[:, :])
            if "d_vt" in dumps:
                DMA("sync", dumps["d_vt"], Vt[:, :])
            if "d_kpe" in dumps:
                DMA("sync", dumps["d_kpe"][:, 0:S], kpe1[:, :])
                DMA("sync", dumps["d_kpe"][:, S:2 * S], kpe2[:, :])

    def ffn_tt(s, tt, do_dump):
        t0 = tt * 512

        p.wait("sync", "dG")
        p.wait("sync", "dS")
        p.wait("sync", "ve")
        p.wait("sync", "pe")
        for c in range(NCH):
            DMA("sync", x1t[:, c * 512:(c + 1) * 512],
                bass.AP(x1_scr.tensor, s * E * S + (c * 128) * S + t0,
                        [[S, 128], [1, 512]]))

        p.wait("vector", "dS")
        p.wait("scalar", "dS")
        first_tok = 0
        for c in range(NCH):
            csl = c % 2
            if c >= 2:
                p.wait("vector", "pe", first_tok)
                p.wait("scalar", "pe", first_tok)
            COPY("vector", SCB[:, csl * 512:(csl + 1) * 512],
                 x1t[:, c * 512:(c + 1) * 512], inc="ve")
            ACT(SCB[:, 1024 + csl * 512:1024 + (csl + 1) * 512],
                x1t[:, c * 512:(c + 1) * 512], AF.Square, inc="se")
            p.wait("tensor", "ve")
            p.wait("tensor", "se")
            MM(PS[0][0:32, :], ones32[:, :], SCB[:, csl * 512:(csl + 1) * 512],
               first=(c == 0), last=(c == NCH - 1))
            MM(PS[1][0:32, :], ones32[:, :],
               SCB[:, 1024 + csl * 512:1024 + (csl + 1) * 512],
               first=(c == 0), last=(c == NCH - 1), inc="pe")
            first_tok = p.val("pe")

        def rowmath_bcast(ps_sum, ps_sq, n, eps_col, mu_dst, r_dst):
            p.wait("vector", "pe")
            TS("vector", POOL[0][0:1, :], ps_sum[0:1, :], 1.0 / n, None, ALU.mult)
            TT("vector", POOL[1][0:1, :], POOL[0][0:1, :], POOL[0][0:1, :], ALU.mult)
            STT("vector", POOL[1][0:1, :], ps_sq[0:1, :], 1.0 / n, POOL[1][0:1, :],
                ALU.mult, ALU.subtract, inc="ve")
            p.wait("scalar", "ve")
            ACT(POOL[1][0:1, :], POOL[1][0:1, :], AF.Ln, bias=eps_col[0:1, :])
            ACT(POOL[1][0:1, :], POOL[1][0:1, :], AF.Exp, scale=-0.5, inc="se")
            p.wait("gpsimd", "ve")
            p.wait("gpsimd", "se")
            p.emit("gpsimd", lambda e, d=mu_dst: e.partition_broadcast(d, POOL[0][0:1, :]))
            p.emit("gpsimd", lambda e, d=r_dst: e.partition_broadcast(d, POOL[1][0:1, :]),
                   inc="ge")
            p.wait("vector", "ge")

        rowmath_bcast(PS[0], PS[1], E, smal[:, 1:2], POOL[2][:, :], POOL[3][:, :])

        first_tok = 0
        for c in range(NCH):
            csl = c % 2
            if c >= 2:
                p.wait("vector", "pe", first_tok)
                p.wait("scalar", "pe", first_tok)
            TT("vector", POOL[4][:, :], x1t[:, c * 512:(c + 1) * 512], POOL[2][:, :],
               ALU.subtract)
            TT("vector", POOL[4][:, :], POOL[4][:, :], POOL[3][:, :], ALU.mult)
            p.emit("vector", lambda e, c_=c: e.tensor_scalar(
                h2t[:, c_ * 512:(c_ + 1) * 512], POOL[4][:, :],
                modc[:, mcol(4, c_, s):mcol(4, c_, s) + 1],
                modc[:, mcol(3, c_, s):mcol(3, c_, s) + 1],
                ALU.mult, ALU.add))
            COPY("vector", SCB[:, csl * 512:(csl + 1) * 512],
                 h2t[:, c * 512:(c + 1) * 512], inc="ve")
            if c == 0:
                COPY("vector", Pb[:, 0:512], SCB[:, csl * 512:(csl + 1) * 512])
                COPY("vector", Pb[:, 512:1024], SCB[:, csl * 512:(csl + 1) * 512])
            else:
                TT("vector", Pb[:, 0:512], Pb[:, 0:512],
                   SCB[:, csl * 512:(csl + 1) * 512], ALU.max)
                TT("vector", Pb[:, 512:1024], Pb[:, 512:1024],
                   SCB[:, csl * 512:(csl + 1) * 512], ALU.min)
            p.wait("scalar", "ve")
            ACT(SCB[:, 1024 + csl * 512:1024 + (csl + 1) * 512],
                h2t[:, c * 512:(c + 1) * 512], AF.Square, inc="se")
            p.wait("tensor", "ve")
            p.wait("tensor", "se")
            MM(PS[2][0:32, :], ones32[:, :], SCB[:, csl * 512:(csl + 1) * 512],
               first=(c == 0), last=(c == NCH - 1))
            MM(PS[3][0:32, :], ones32[:, :],
               SCB[:, 1024 + csl * 512:1024 + (csl + 1) * 512],
               first=(c == 0), last=(c == NCH - 1), inc="pe")
            first_tok = p.val("pe")
        rowmath_bcast(PS[2], PS[3], E, smal[:, 3:4], POOL[2][:, :], POOL[3][:, :])

        TT("vector", POOL[4][:, :], Pb[:, 0:512], POOL[2][:, :], ALU.subtract)
        TT("vector", POOL[0][:, :], POOL[2][:, :], Pb[:, 512:1024], ALU.subtract)
        TT("vector", POOL[4][:, :], POOL[4][:, :], POOL[0][:, :], ALU.max, inc="ve")
        p.wait("gpsimd", "ve")
        p.emit("gpsimd", lambda e: e.partition_all_reduce(
            POOL[1][:, :], POOL[4][:, :], 128, ReduceOp.max), inc="ge")
        p.wait("scalar", "ge")
        ACT(POOL[4][:, :], POOL[1][:, :], AF.Ln)
        ACT(POOL[0][:, :], POOL[4][:, :], AF.Exp, scale=-1.0, bias=smal[:, 4:5],
            inc="se")
        p.wait("vector", "ge")
        TT("vector", POOL[3][:, :], POOL[3][:, :], POOL[1][:, :], ALU.mult)
        TS("vector", POOL[3][:, :], POOL[3][:, :], 1.0 / 128.0, None, ALU.mult)
        p.wait("vector", "se")

        xh_tok = [0] * NCH
        for c in range(NCH):
            TT("vector", POOL[4][:, :], h2t[:, c * 512:(c + 1) * 512], POOL[2][:, :],
               ALU.subtract)
            TT("vector", POOL[4][:, :], POOL[4][:, :], POOL[0][:, :], ALU.mult)
            TS("vector", POOL[4][:, :], POOL[4][:, :], CLIP_HI, -CLIP_HI, ALU.min,
               ALU.max)
            TT("vector", xh[:, c * 512:(c + 1) * 512], POOL[4][:, :], POOL[3][:, :],
               ALU.mult, inc="ve")
            xh_tok[c] = p.val("ve")
        xh_ready = p.val("ve")
        if do_dump and "d_xh" in dumps:
            p.wait("sync", "ve")
            DMA("sync", dumps["d_xh"], xh[:, :])

        t4_tok = p.t4_tok
        for u in range(64):
            og, g4 = divmod(u, 4)
            buf = u % 3
            q = ("sync", "dS") if u % 2 == 0 else ("scalar", "dC")
            p.wait(q[0], "pe", t4_tok[buf])
            DMA(q[0], T4[buf][:, :],
                bass.AP(tfc_scr.tensor, (og * 4 + g4) * 262144,
                        [[2048, 128], [1, 2048]]),
                sem=q[1])
            dtok = p.val(q[1])
            p.wait("tensor", q[1], dtok)
            p.wait("tensor", "ve", xh_tok[g4 * 4 + 3])
            if g4 == 0:
                p.wait("tensor", "se")
            for j in range(4):
                d = g4 * 4 + j
                for i in range(4):
                    MM(PS[i][:, :], T4[buf][:, i * 512 + j * 128:i * 512 + (j + 1) * 128],
                       xh[:, d * 512:(d + 1) * 512],
                       first=(d == 0), last=(d == NCH - 1),
                       inc=("pe" if i == 3 else None))
            t4_tok[buf] = p.val("pe")
            if g4 == 3:
                p.wait("scalar", "pe")
                for i in range(4):
                    oc = og * 4 + i
                    ACT(gbuf[:, oc * 512:(oc + 1) * 512], PS[i][:, :], AF.Gelu,
                        scale=gam[:, 2:3], inc=("se" if i == 3 else None))
                gelu_tok = p.val("se")
                p.wait("scalar", "se", gelu_tok)
                p.wait("vector", "se", gelu_tok)
                for i in range(4):
                    oc = og * 4 + i
                    csl = i % 2
                    if og > 0 or i >= 2:
                        p.wait("scalar", "pe", p.gstat_tok)
                    ACT(SCB[:, 1024 + csl * 512:1024 + (csl + 1) * 512],
                        gbuf[:, oc * 512:(oc + 1) * 512], AF.Square, inc="se")
                    if oc == 0:
                        COPY("vector", Pb[:, 0:512], gbuf[:, 0:512])
                        COPY("vector", Pb[:, 512:1024], gbuf[:, 0:512])
                    else:
                        TT("vector", Pb[:, 0:512], Pb[:, 0:512],
                           gbuf[:, oc * 512:(oc + 1) * 512], ALU.max)
                        TT("vector", Pb[:, 512:1024], Pb[:, 512:1024],
                           gbuf[:, oc * 512:(oc + 1) * 512], ALU.min)
                    p.wait("tensor", "se")
                    MM(PS[4][0:32, :], ones32[:, :], gbuf[:, oc * 512:(oc + 1) * 512],
                       first=(oc == 0), last=(oc == NOC - 1))
                    MM(PS[5][0:32, :], ones32[:, :],
                       SCB[:, 1024 + csl * 512:1024 + (csl + 1) * 512],
                       first=(oc == 0), last=(oc == NOC - 1), inc="pe")
                    p.gstat_tok = p.val("pe")
        if do_dump and "d_g" in dumps:
            p.wait("sync", "se")
            DMA("sync", dumps["d_g"], gbuf[:, 0:NOC * 512])

        rowmath_bcast(PS[4], PS[5], DFF, smal[:, 3:4], POOL[2][:, :], POOL[3][:, :])
        TT("vector", POOL[4][:, :], Pb[:, 0:512], POOL[2][:, :], ALU.subtract)
        TT("vector", POOL[0][:, :], POOL[2][:, :], Pb[:, 512:1024], ALU.subtract)
        TT("vector", POOL[4][:, :], POOL[4][:, :], POOL[0][:, :], ALU.max, inc="ve")
        p.wait("gpsimd", "ve")
        p.emit("gpsimd", lambda e: e.partition_all_reduce(
            POOL[1][:, :], POOL[4][:, :], 128, ReduceOp.max), inc="ge")
        p.wait("vector", "ge")
        COPY("vector", SCB[:, 1024:1536], POOL[2][:, :])
        TT("vector", POOL[3][:, :], POOL[3][:, :], POOL[1][:, :], ALU.mult)
        TS("vector", POOL[3][:, :], POOL[3][:, :], 1.0 / 128.0, None, ALU.mult)
        COPY("vector", SCB[:, 0:512], POOL[3][:, :])
        p.wait("scalar", "ge")
        ACT(POOL[4][:, :], POOL[1][:, :], AF.Ln)
        ACT(POOL[0][:, :], POOL[4][:, :], AF.Exp, scale=-1.0, bias=smal[:, 4:5],
            inc="se")
        p.wait("vector", "se")
        COPY("vector", SCB[:, 512:1024], POOL[0][:, :], inc="ve")

        xh2_tok = [0] * NOC
        for oc in range(NOC):
            TT("vector", gbuf[:, oc * 512:(oc + 1) * 512],
               gbuf[:, oc * 512:(oc + 1) * 512], SCB[:, 1024:1536], ALU.subtract)
            TT("vector", gbuf[:, oc * 512:(oc + 1) * 512],
               gbuf[:, oc * 512:(oc + 1) * 512], SCB[:, 512:1024], ALU.mult)
            TS("vector", gbuf[:, oc * 512:(oc + 1) * 512],
               gbuf[:, oc * 512:(oc + 1) * 512], CLIP_HI, -CLIP_HI, ALU.min, ALU.max)
            TT("vector", gbuf[:, oc * 512:(oc + 1) * 512],
               gbuf[:, oc * 512:(oc + 1) * 512], SCB[:, 0:512], ALU.mult,
               inc="ve")
            xh2_tok[oc] = p.val("ve")

        t4_tok = p.t4_tok
        for u in range(64):
            ocg, k4 = divmod(u, 16)
            buf = u % 3
            q = ("sync", "dS") if u % 2 == 0 else ("scalar", "dC")
            p.wait(q[0], "pe", t4_tok[buf])
            DMA(q[0], T4[buf][:, :],
                bass.AP(tpj_scr.tensor, (ocg * 16 + k4) * 262144,
                        [[2048, 128], [1, 2048]]),
                sem=q[1])
            dtok = p.val(q[1])
            p.wait("tensor", q[1], dtok)
            p.wait("tensor", "ve", xh2_tok[k4 * 4 + 3])
            if k4 == 0:
                p.wait("tensor", "ve")
            for j in range(4):
                kk = k4 * 4 + j
                for i in range(4):
                    MM(PS[i][:, :], T4[buf][:, i * 512 + j * 128:i * 512 + (j + 1) * 128],
                       gbuf[:, kk * 512:(kk + 1) * 512],
                       first=(kk == 0), last=(kk == 4 * NCH - 1),
                       inc=("pe" if i == 3 else None))
            t4_tok[buf] = p.val("pe")
            if k4 == 15:
                p.wait("vector", "pe")
                for i in range(4):
                    oc = ocg * 4 + i
                    p.emit("vector", lambda e, i_=i, oc_=oc: e.scalar_tensor_tensor(
                        x1t[:, oc_ * 512:(oc_ + 1) * 512], PS[i_][:, :],
                        modc[:, 192 + oc_ * 2 + s:192 + oc_ * 2 + s + 1],
                        x1t[:, oc_ * 512:(oc_ + 1) * 512],
                        ALU.mult, ALU.add), inc=("ve" if i == 3 else None))
        out_dma_tok = 0
        p.wait("tensor", "ve")
        for rt in range(4):
            bsl = rt % 2
            for ocg4 in range(4):
                for i in range(4):
                    oc = ocg4 * 4 + i
                    TR(PS[4 + bsl][:, i * 128:(i + 1) * 128],
                       x1t[:, oc * 512 + rt * 128:oc * 512 + (rt + 1) * 128],
                       inc=("pe" if i == 3 else None))
                p.wait("vector", "pe")
                if ocg4 == 0:
                    p.wait("vector", "dS", out_dma_tok)
                COPY("vector", otm[:, ocg4 * 512:(ocg4 + 1) * 512], PS[4 + bsl][:, :],
                     inc="ve")
                p.wait("tensor", "ve")
            p.wait("sync", "ve")
            DMA("sync", out_hbm[s, t0 + rt * 128:t0 + (rt + 1) * 128, :], otm[:, :])
            out_dma_tok = p.val("dS")

    PS = []

    if "M" in phases:
        phase_M()
    ctxM.close()
    PS.extend(ctx.enter_context(nc.psum_tensor(f"ps{i}", [128, 512], F32))
              for i in range(8))
    if "A" in phases:
        for s in range(NSEQ):
            p.barrier()
            for qc in range(NQC):
                attn_qc(s, qc, s == 0)
            p.barrier()
            if s == 0 and "d_x1scr" in dumps:
                p.wait("sync", "dG")
                DMA("sync", dumps["d_x1scr"], x1_scr[0])
            if "B" in phases:
                for tt in range(NQC):
                    ffn_tt(s, tt, s == 0 and tt == 0)
                p.barrier()

    p.barrier()
    for e in ENGS:
        p.wait(e, "dS")
        p.wait(e, "dG")
        p.wait(e, "dC")

    with nc.Block() as block:
        p.replay(block)
    ctx.close()
    nc.compile()
    return nc, dumps


def _consts():
    pos = np.arange(S, dtype=np.float64)
    inv = 1.0 / (10000.0 ** (np.arange(0, ROPE_D, 2, dtype=np.float64) / ROPE_D))
    ang = np.outer(inv, pos)
    cosT = np.tile(np.cos(ang), (4, 1)).astype(np.float32)
    sinT = np.tile(np.sin(ang), (4, 1)).astype(np.float32)
    return np.ascontiguousarray(cosT), np.ascontiguousarray(sinT)


def make_in_maps(x, c, wq, wkva, wkvb, wo, kvs, wfc, wpj, wmod):
    cosT, sinT = _consts()
    base = dict(
        wq=np.ascontiguousarray(np.asarray(wq, np.float32)),
        wkva=np.ascontiguousarray(np.asarray(wkva, np.float32)),
        wkvb=np.ascontiguousarray(np.asarray(wkvb, np.float32)),
        wo=np.ascontiguousarray(np.asarray(wo, np.float32)),
        kvs=np.ascontiguousarray(np.asarray(kvs, np.float32)).reshape(1, LORA),
        wfc=np.ascontiguousarray(np.asarray(wfc, np.float32)),
        wpj=np.ascontiguousarray(np.asarray(wpj, np.float32)),
        wmod=np.ascontiguousarray(np.asarray(wmod, np.float32)),
        cosT=cosT, sinT=sinT,
    )
    x = np.ascontiguousarray(np.asarray(x, np.float32))
    c = np.ascontiguousarray(np.asarray(c, np.float32))
    in_maps = []
    for core in range(8):
        m = dict(base)
        m["x"] = np.ascontiguousarray(x[2 * core:2 * core + 2])
        m["c"] = np.ascontiguousarray(c[2 * core:2 * core + 2])
        in_maps.append(m)
    return in_maps


def kernel(L0_attn_x, L0_attn_wq, L0_attn_wkv_a, L0_attn_wkv_b, L0_attn_wo,
           L0_attn_kv_norm_scale, L0_ffn_w_fc, L0_ffn_w_proj, L0_c, L0_wmod):
    in_maps = make_in_maps(L0_attn_x, L0_c, L0_attn_wq, L0_attn_wkv_a, L0_attn_wkv_b,
                           L0_attn_wo, L0_attn_kv_norm_scale, L0_ffn_w_fc,
                           L0_ffn_w_proj, L0_wmod)
    nc, _ = build_nc()
    res = run_bass_kernel_spmd(nc, in_maps, core_ids=list(range(8)))
    out = np.concatenate([np.asarray(r["out"]) for r in res.results], axis=0)
    return out.astype(np.float32)
